# version 06
# baseline (speedup 1.0000x reference)
import math
import numpy as np
import concourse.bass as bass
import concourse.mybir as mybir

F32 = mybir.dt.float32
BF16 = mybir.dt.bfloat16
I32 = mybir.dt.int32
OP = mybir.AluOpType
ACT = mybir.ActivationFunctionType
AX = mybir.AxisListType

T_KEYS = 4096
R_OWN = 2048
D = 2048
H = 16
HD = 128
I_FF = 5632
NT_KEY = T_KEYS // 128
NT_OWN = R_OWN // 128
NC_D = D // 128
NC_I = I_FF // 128
NJT = I_FF // 512
OWN0 = R_OWN


class Buf:
    __slots__ = ("name", "wr", "rd")
    def __init__(self, name):
        self.name = name
        self.wr = None
        self.rd = []


class KB:
    def __init__(self, nc):
        self.nc = nc
        self.eng = {
            "t": nc.tensor, "v": nc.vector, "s": nc.scalar,
            "g": nc.gpsimd, "y": nc.sync,
        }
        self.sem = {}
        self.cnt = {}
        self.wm = {}
        self._ctx = []
        for e in ("t", "v", "s", "g", "y"):
            cm = nc.semaphore(f"c_{e}")
            h = cm.__enter__()
            self._ctx.append(cm)
            self.sem[e] = h
            self.cnt[e] = 0
        self.NDMA = 8
        for d in ("dl", "ds"):
            for k in range(self.NDMA):
                cm = nc.semaphore(f"c_{d}{k}")
                h = cm.__enter__()
                self._ctx.append(cm)
                self.sem[f"{d}{k}"] = h
                self.cnt[f"{d}{k}"] = 0
            self.sem[d + "_rr"] = 0
        for e in ("t", "v", "s", "g", "y"):
            self.wm[e] = {}
        self.bufs = {}

    def close_sems(self):
        for s in reversed(self._ctx):
            s.__exit__(None, None, None)

    def buf(self, name):
        if name not in self.bufs:
            self.bufs[name] = Buf(name)
        return self.bufs[name]

    def _wait(self, consumer, token):
        if token is None:
            return
        sem_name, val = token
        if sem_name.startswith("dl") or sem_name.startswith("ds"):
            val = self.cnt[sem_name]
        w = self.wm[consumer]
        if w.get(sem_name, 0) >= val:
            return
        self.eng[consumer].wait_ge(self.sem[sem_name], val)
        w[sem_name] = val

    def op(self, e, fn, reads=(), writes=(), dma=None):
        reads = [self.buf(b) if isinstance(b, str) else b for b in reads]
        writes = [self.buf(b) if isinstance(b, str) else b for b in writes]
        for b in reads:
            self._wait(e, b.wr)
        for b in writes:
            self._wait(e, b.wr)
            for t in b.rd:
                self._wait(e, t)
        if dma is not None:
            k = self.sem[dma + "_rr"]
            self.sem[dma + "_rr"] = (k + 1) % self.NDMA
            dname = f"{dma}{k}"
            self._wait(e, (dname, self.cnt[dname]))
            r = fn(self.eng[e])
            self.cnt[dname] += 16
            r.then_inc(self.sem[dname], 16)
            tok = (dname, self.cnt[dname])
        else:
            r = fn(self.eng[e])
            self.cnt[e] += 1
            r.then_inc(self.sem[e], 1)
            tok = (e, self.cnt[e])
        for b in reads:
            b.rd.append(tok)
        for b in writes:
            b.wr = tok
            b.rd = []
        return tok

    def barrier(self):
        names = ["t", "v", "s", "g", "y"] + [f"{d}{k}" for d in ("dl", "ds") for k in range(self.NDMA)]
        targets = [(s, self.cnt[s]) for s in names if self.cnt[s] > 0]
        for e in ("t", "v", "s", "g", "y"):
            for sname, val in targets:
                if sname == e:
                    continue
                self._wait(e, (sname, val))


def snap_tiesup(kb, src, dst_bf16, tmp, n, sem_hint=None):
    raise NotImplementedError


def emit_snap(kb, srcap, srcbuf, dstap, dstbuf, t, n):
    B = kb.buf
    kb.op("v", lambda v: v.tensor_scalar(out=t["pi"][:, :n], in0=srcap.bitcast(I32),
          scalar1=0x7F800000, scalar2=None, op0=OP.bitwise_and),
          reads=[srcbuf], writes=["snap_pi"])
    kb.op("v", lambda v: v.tensor_scalar(out=t["u"][:, :n], in0=t["pi"][:, :n].bitcast(F32),
          scalar1=0.125, scalar2=2.0 ** -9, op0=OP.mult, op1=OP.max),
          reads=["snap_pi"], writes=["snap_u"])
    kb.op("v", lambda v: v.tensor_tensor(out=t["pi"][:, :n], in0=t["ci"][:, :n],
          in1=t["u"][:, :n].bitcast(I32), op=OP.subtract),
          reads=["snap_ci", "snap_u"], writes=["snap_pi"])
    kb.op("v", lambda v: v.tensor_tensor(out=t["y"][:, :n], in0=srcap,
          in1=t["pi"][:, :n].bitcast(F32), op=OP.mult),
          reads=[srcbuf, "snap_pi"], writes=["snap_y"])
    kb.op("v", lambda v: v.tensor_copy(out=t["f0"][:, :n], in_=t["y"][:, :n]),
          reads=["snap_y"], writes=["snap_f0"])
    kb.op("v", lambda v: v.tensor_copy(out=t["f1"][:, :n], in_=t["f0"][:, :n]),
          reads=["snap_f0"], writes=["snap_f1"])
    kb.op("v", lambda v: v.scalar_tensor_tensor(out=t["c1"][:, :n], in0=t["f1"][:, :n],
          scalar=-0.5, in1=t["y"][:, :n], op0=OP.add, op1=OP.is_le),
          reads=["snap_f1", "snap_y"], writes=["snap_c1"])
    kb.op("v", lambda v: v.scalar_tensor_tensor(out=t["c2"][:, :n], in0=t["f1"][:, :n],
          scalar=0.5, in1=t["y"][:, :n], op0=OP.add, op1=OP.is_le),
          reads=["snap_f1", "snap_y"], writes=["snap_c2"])
    kb.op("v", lambda v: v.tensor_tensor(out=t["c1"][:, :n], in0=t["c1"][:, :n],
          in1=t["c2"][:, :n], op=OP.add),
          reads=["snap_c1", "snap_c2"], writes=["snap_c1"])
    kb.op("v", lambda v: v.scalar_tensor_tensor(out=t["y"][:, :n], in0=t["c1"][:, :n],
          scalar=-1.0, in1=t["f1"][:, :n], op0=OP.add, op1=OP.add),
          reads=["snap_c1", "snap_f1"], writes=["snap_y"])
    kb.op("v", lambda v: v.tensor_tensor(out=dstap, in0=t["y"][:, :n],
          in1=t["u"][:, :n], op=OP.mult),
          reads=["snap_y", "snap_u"], writes=[dstbuf])


def emit_longdiv(kb, mx_ap, out_ap, t, n):
    B = "ld_"
    kb.op("v", lambda v: v.tensor_scalar(out=t["mi"][:, :n], in0=mx_ap.bitcast(I32),
          scalar1=0x7FFFFF, scalar2=0x800000, op0=OP.bitwise_and, op1=OP.bitwise_or),
          reads=["ld_mx"], writes=[B + "mi"])
    kb.op("v", lambda v: v.tensor_copy(out=t["mf"][:, :n], in_=t["mi"][:, :n]),
          reads=[B + "mi"], writes=[B + "mf"])
    kb.op("v", lambda v: v.tensor_scalar(out=t["mf"][:, :n], in0=t["mf"][:, :n],
          scalar1=2.0 ** -24, scalar2=None, op0=OP.mult),
          reads=[B + "mf"], writes=[B + "mf"])
    kb.op("v", lambda v: v.tensor_scalar(out=t["eb"][:, :n], in0=mx_ap.bitcast(I32),
          scalar1=23, scalar2=0xFF, op0=OP.logical_shift_right, op1=OP.bitwise_and),
          reads=["ld_mx"], writes=[B + "eb"])
    kb.op("v", lambda v: v.tensor_scalar(out=t["n2"][:, :n], in0=t["mf"][:, :n],
          scalar1=0.875, scalar2=None, op0=OP.is_gt),
          reads=[B + "mf"], writes=[B + "n2"])
    kb.op("v", lambda v: v.tensor_scalar(out=t["r"][:, :n], in0=t["n2"][:, :n],
          scalar1=0.4375, scalar2=0.4375, op0=OP.mult, op1=OP.add),
          reads=[B + "n2"], writes=[B + "r"])
    for qn in ("q0", "q1", "q2"):
        kb.op("v", lambda v, qn=qn: v.memset(t[qn][:, :n], 0.0),
              reads=[], writes=[B + qn])
    for i in range(1, 55):
        qn = "q0" if i <= 18 else ("q1" if i <= 36 else "q2")
        kb.op("v", lambda v: v.tensor_scalar(out=t["r"][:, :n], in0=t["r"][:, :n],
              scalar1=2.0, scalar2=None, op0=OP.mult),
              reads=[B + "r"], writes=[B + "r"])
        kb.op("v", lambda v: v.tensor_tensor(out=t["d"][:, :n], in0=t["r"][:, :n],
              in1=t["mf"][:, :n], op=OP.subtract),
              reads=[B + "r", B + "mf"], writes=[B + "d"])
        kb.op("v", lambda v: v.tensor_scalar(out=t["c"][:, :n], in0=t["d"][:, :n],
              scalar1=0.0, scalar2=None, op0=OP.is_ge),
              reads=[B + "d"], writes=[B + "c"])
        kb.op("v", lambda v: v.copy_predicated(t["r"][:, :n], t["c"][:, :n], t["d"][:, :n]),
              reads=[B + "c", B + "d", B + "r"], writes=[B + "r"])
        kb.op("v", lambda v, qn=qn: v.scalar_tensor_tensor(out=t[qn][:, :n],
              in0=t[qn][:, :n], scalar=2.0, in1=t["c"][:, :n], op0=OP.mult, op1=OP.add),
              reads=[B + qn, B + "c"], writes=[B + qn])
        if i == 53:
            kb.op("v", lambda v: v.tensor_copy(out=t["l"][:, :n], in_=t["c"][:, :n]),
                  reads=[B + "c"], writes=[B + "l"])
        if i == 54:
            kb.op("v", lambda v: v.tensor_copy(out=t["g"][:, :n], in_=t["c"][:, :n]),
                  reads=[B + "c"], writes=[B + "g"])
    kb.op("v", lambda v: v.tensor_scalar(out=t["st"][:, :n], in0=t["r"][:, :n],
          scalar1=0.0, scalar2=None, op0=OP.is_gt),
          reads=[B + "r"], writes=[B + "st"])
    kb.op("v", lambda v: v.tensor_tensor(out=t["st"][:, :n], in0=t["st"][:, :n],
          in1=t["l"][:, :n], op=OP.max),
          reads=[B + "st", B + "l"], writes=[B + "st"])
    kb.op("v", lambda v: v.tensor_tensor(out=t["st"][:, :n], in0=t["st"][:, :n],
          in1=t["g"][:, :n], op=OP.mult),
          reads=[B + "st", B + "g"], writes=[B + "st"])
    kb.op("v", lambda v: v.tensor_tensor(out=t["q2"][:, :n], in0=t["q2"][:, :n],
          in1=t["g"][:, :n], op=OP.subtract),
          reads=[B + "q2", B + "g"], writes=[B + "q2"])
    kb.op("v", lambda v: v.scalar_tensor_tensor(out=t["q2"][:, :n], in0=t["st"][:, :n],
          scalar=2.0, in1=t["q2"][:, :n], op0=OP.mult, op1=OP.add),
          reads=[B + "st", B + "q2"], writes=[B + "q2"])
    kb.op("v", lambda v: v.tensor_scalar(out=t["acc"][:, :n], in0=t["q0"][:, :n],
          scalar1=2.0 ** -18, scalar2=None, op0=OP.mult),
          reads=[B + "q0"], writes=[B + "acc"])
    kb.op("v", lambda v: v.scalar_tensor_tensor(out=t["acc"][:, :n], in0=t["q1"][:, :n],
          scalar=2.0 ** -36, in1=t["acc"][:, :n], op0=OP.mult, op1=OP.add),
          reads=[B + "q1", B + "acc"], writes=[B + "acc"])
    kb.op("v", lambda v: v.scalar_tensor_tensor(out=t["acc"][:, :n], in0=t["q2"][:, :n],
          scalar=2.0 ** -54, in1=t["acc"][:, :n], op0=OP.mult, op1=OP.add),
          reads=[B + "q2", B + "acc"], writes=[B + "acc"])
    kb.op("v", lambda v: v.tensor_copy(out=t["n2i"][:, :n], in_=t["n2"][:, :n]),
          reads=[B + "n2"], writes=[B + "n2i"])
    kb.op("v", lambda v: v.tensor_tensor(out=t["fb"][:, :n], in0=t["c260"][:, :n],
          in1=t["eb"][:, :n], op=OP.subtract),
          reads=[B + "c260", B + "eb"], writes=[B + "fb"])
    kb.op("v", lambda v: v.tensor_tensor(out=t["fb"][:, :n], in0=t["fb"][:, :n],
          in1=t["n2i"][:, :n], op=OP.subtract),
          reads=[B + "fb", B + "n2i"], writes=[B + "fb"])
    kb.op("v", lambda v: v.tensor_scalar(out=t["fb"][:, :n], in0=t["fb"][:, :n],
          scalar1=23, scalar2=None, op0=OP.logical_shift_left),
          reads=[B + "fb"], writes=[B + "fb"])
    kb.op("v", lambda v: v.tensor_tensor(out=out_ap, in0=t["acc"][:, :n],
          in1=t["fb"][:, :n].bitcast(F32), op=OP.mult),
          reads=[B + "acc", B + "fb"], writes=["ld_out"])



"""Phase emitters + full program builder for the prb megakernel."""


SNW = 384
NQT = R_OWN // 512
NKT = T_KEYS // 512


def alloc_all(nc, es):
    sb = {}
    def S(name, shape, dt=F32):
        sb[name] = es.enter_context(nc.sbuf_tensor(name, shape, dt))
    def P(name, shape, dt=F32):
        sb[name] = es.enter_context(nc.psum_tensor(name, shape, dt))
    S("ones", [128, 128]); S("ident", [128, 128]); S("ropeRT", [128, 128])
    S("trimask", [128, 2048], BF16); S("phb", [128, 32]); S("cvec", [128, 8])
    S("scv", [128, 1])
    S("big0", [128, 8192])
    S("xad", [128, 2048])
    S("wqsb", [128, 512], BF16)
    sb["xa"] = sb["big0"]
    sb["strip"] = sb["big0"]
    sb["wnat"] = sb["big0"]
    sb["tbuf"] = sb["big0"]
    sb["xnf"] = sb["big0"]
    sb["ksb"] = sb["big0"]
    sb["vsb"] = sb["big0"]
    S("stripb", [128, 5632], BF16)
    S("qtf", [128, 5632], BF16)
    S("sn_u", [128, SNW]); S("sn_y", [128, SNW]); S("sn_f1", [128, SNW])
    S("sn_c1", [128, SNW]); S("sn_c2", [128, SNW])
    S("sn_pi", [128, SNW], I32); S("sn_f0", [128, SNW], I32)
    S("sn_ci", [128, SNW], I32)
    for k in ("mf", "r", "d", "q0", "q1", "q2", "g", "l", "st", "n2", "acc"):
        S("ld_" + k, [128, 104])
    for k in ("eb", "mi", "n2i", "fb", "c260", "c"):
        S("ld_" + k, [128, 104], I32)
    S("sQmax", [128, 104]); S("sQ", [128, 104]); S("sRec", [128, 104])
    S("sT", [104, 128]); S("wT", [104, 128])
    S("mean", [128, 1]); S("var", [128, 1]); S("rstd", [128, 1])
    S("rmax", [128, 1]); S("sxq", [128, 1]); S("sxdT", [128, 16])
    S("stq", [128, 1]); S("stdq", [128, 1]); S("stdq2", [128, 1])
    S("wkt", [128, 2048]); S("wku", [128, 2048]); S("wv0", [128, 512]); S("wv1", [128, 512])
    S("kpre", [128, 512]); S("kc", [128, 512]); S("rs", [128, 512])
    S("kout", [128, 512])
    S("cost", [128, 512]); S("sint", [128, 512])
    S("qsb", [128, 512]); S("esb", [128, 512]); S("esc", [128, 512])
    sb["wq512"] = sb["qsb"]; sb["sqseg"] = sb["esb"]
    S("osb", [128, 512]); S("rsum", [1, 512])
    S("qxf", [128, 2048], BF16)
    S("phi", [128, 512], BF16); S("plo", [128, 512], BF16)
    S("phj", [128, 512], BF16); S("plj", [128, 512], BF16)
    S("onesb", [128, 128], BF16)
    S("vtb", [128, 128], BF16); S("vtl", [128, 128], BF16)
    S("qw0", [128, 512], BF16); S("qw1", [128, 512], BF16)
    S("qw2", [128, 512], BF16); S("qw3", [128, 512], BF16)
    S("qwgL", [128, 8192], BF16); S("qwuL", [128, 8192], BF16)
    S("qwgM", [128, 8192], BF16)
    S("gsb", [128, 512]); S("usb", [128, 512])
    S("ysb", [128, 512])
    S("wsgS", [128, 512]); S("wsuS", [128, 512]); S("wsdS", [128, 512])
    for i in range(8):
        P(f"p{i}", [128, 512])
    return sb


def declare_dram(nc, debug_out=()):
    d = {}
    def I(name, shape, dt=F32):
        d[name] = nc.dram_tensor(name, shape, dt, kind="ExternalInput").ap()
    def X(name, shape, dt=F32):
        if name in debug_out:
            d[name] = nc.dram_tensor(name, shape, dt, kind="ExternalOutput").ap()
        else:
            d[name] = nc.dram_tensor(name, shape, dt).ap()
    I("xk", [T_KEYS, D])
    I("wqT", [D, D]); I("wkT", [D, D]); I("wvT", [D, D]); I("woT", [D, D])
    I("wgN", [I_FF, D]); I("wgT", [D, I_FF])
    I("wuN", [I_FF, D]); I("wuT", [D, I_FF])
    I("wdN", [D, I_FF]); I("wdT", [I_FF, D])
    I("cosf", [128, T_KEYS]); I("sinf", [128, T_KEYS])
    I("conA", [128, 128]); I("conI", [128, 128]); I("conR", [128, 128])
    d["conT"] = nc.dram_tensor("conT", [128, 2048], BF16, kind="ExternalInput").ap()
    I("conP", [128, 32]); I("conC", [128, 8])
    d["y"] = nc.dram_tensor("y", [R_OWN, D], F32, kind="ExternalOutput").ap()
    X("XNF", [NT_KEY * NC_D * 128, 128])
    X("QXF", [NT_OWN * NC_D * 128, 128], BF16)
    X("KF", [H * NKT * 128, 512])
    X("VT", [NT_KEY * H * 128, 128])
    X("QF", [H * NQT * 128, 512])
    X("OF", [H * NQT * 128, 512])
    X("QWG", [NC_D * NJT * 128, 512], BF16)
    X("QWU", [NC_D * NJT * 128, 512], BF16)
    X("QWD", [NC_I * 4 * 128, 512], BF16)
    X("FFNY", [NT_OWN * 4 * 128, 512])
    X("WSQG", [128, I_FF])
    X("WSQU", [128, I_FF])
    X("WSQD", [128, D])
    X("WSG", [128, I_FF])
    X("WSU", [128, I_FF])
    X("WSD", [128, D])
    return d


def emit_consts(kb, sb, dram):
    for nm, dst in (("conA", "ones"), ("conI", "ident"), ("conR", "ropeRT"),
                    ("conT", "trimask"), ("conP", "phb"), ("conC", "cvec")):
        kb.op("y", lambda y, nm=nm, dst=dst: y.dma_start(out=sb[dst][:], in_=dram[nm][:]),
              writes=[dst], dma="dl")
    kb.op("v", lambda v: v.tensor_tensor(out=sb["scv"][:], in0=sb["cvec"][:, 1:2],
          in1=sb["cvec"][:, 2:3], op=OP.add), reads=["cvec"], writes=["scv"])
    kb.op("v", lambda v: v.memset(sb["ld_c260"][:], 263), writes=["ld_c260"])
    kb.op("v", lambda v: v.memset(sb["onesb"][:], 1.0), writes=["onesb"])
    kb.op("v", lambda v: v.memset(sb["sn_ci"][:], 254 << 23), writes=["snap_ci"])


def snap_chunks(kb, sb, srcap, srcbuf, dstap, dstbuf, width):
    t = {k: sb["sn_" + k] for k in ("pi", "u", "y", "f0", "f1", "c1", "c2")}
    t["ci"] = sb["sn_ci"]
    off = 0
    while off < width:
        n = min(SNW, width - off)
        emit_snap(kb, srcap[:, off:off + n], srcbuf, dstap[:, off:off + n],
                  dstbuf, t, n)
        off += n


def emit_phaseA(kb, sb, dram):
    XA = sb["xad"][:]
    ST = sb["big0"][:, 6144:8192]
    for tt in range(NT_KEY):
        kb.op("y", lambda y, tt=tt: y.dma_start(out=XA,
              in_=dram["xk"][tt * 128:(tt + 1) * 128, :]), writes=["xa"], dma="dl")
        kb.op("v", lambda v: v.tensor_reduce(out=sb["mean"][:], in_=XA,
              axis=AX.X, op=OP.add), reads=["xa"], writes=["mean"])
        kb.op("v", lambda v: v.tensor_scalar(out=sb["mean"][:], in0=sb["mean"][:],
              scalar1=1.0 / 2048, scalar2=None, op0=OP.mult),
              reads=["mean"], writes=["mean"])
        kb.op("v", lambda v: v.tensor_scalar(out=XA, in0=XA,
              scalar1=sb["mean"][:], scalar2=None, op0=OP.subtract),
              reads=["xa", "mean"], writes=["xa"])
        kb.op("v", lambda v: v.tensor_tensor(out=ST, in0=XA, in1=XA, op=OP.mult),
              reads=["xa"], writes=["strip"])
        kb.op("v", lambda v: v.tensor_reduce(out=sb["var"][:], in_=ST,
              axis=AX.X, op=OP.add), reads=["strip"], writes=["var"])
        kb.op("v", lambda v: v.tensor_scalar(out=sb["var"][:], in0=sb["var"][:],
              scalar1=1.0 / 2048, scalar2=None, op0=OP.mult),
              reads=["var"], writes=["var"])
        kb.op("s", lambda s: s.activation(out=sb["rstd"][:], in_=sb["var"][:],
              func=ACT.Sqrt, bias=sb["cvec"][:, 4:5], scale=sb["cvec"][:, 5:6]),
              reads=["var", "cvec"], writes=["rstd"])
        kb.op("v", lambda v: v.reciprocal(out=sb["rstd"][:], in_=sb["rstd"][:]),
              reads=["rstd"], writes=["rstd"])
        kb.op("v", lambda v: v.tensor_scalar(out=XA, in0=XA,
              scalar1=sb["rstd"][:], scalar2=None, op0=OP.mult),
              reads=["xa", "rstd"], writes=["xa"])
        for c in range(NC_D):
            pn = "p0" if c % 2 == 0 else "p1"
            kb.op("t", lambda t, c=c, pn=pn: t.transpose(sb[pn][:, :128],
                  XA[:, c * 128:(c + 1) * 128], sb["ident"][:]),
                  reads=["xa", "ident"], writes=[pn])
            kb.op("s", lambda s, c=c, pn=pn: s.copy(out=ST[:, c * 128:(c + 1) * 128],
                  in_=sb[pn][:, :128]), reads=[pn], writes=["strip"])
        kb.op("g", lambda g, tt=tt: g.dma_start(
            out=bass.AP(dram["XNF"].tensor, tt * NC_D * 128 * 128,
                        [[128, 128], [16384, NC_D], [1, 128]]),
            in_=sb["big0"][:, 6144:8192].rearrange("p (c j) -> p c j", j=128)),
            reads=["strip"], dma="ds")
        if tt < NT_KEY - NT_OWN:
            yield
        if tt >= NT_KEY - NT_OWN:
            ot = tt - (NT_KEY - NT_OWN)
            kb.op("v", lambda v: v.tensor_reduce(out=sb["rmax"][:], in_=XA,
                  axis=AX.X, op=OP.max, apply_absolute_value=True),
                  reads=["xa"], writes=["rmax"])
            kb.op("v", lambda v: v.reciprocal(out=sb["sxq"][:], in_=sb["rmax"][:]),
                  reads=["rmax"], writes=["sxq"])
            kb.op("v", lambda v: v.tensor_scalar(out=sb["sxq"][:], in0=sb["sxq"][:],
                  scalar1=448.0, scalar2=None, op0=OP.mult),
                  reads=["sxq"], writes=["sxq"])
            kb.op("v", lambda v, ot=ot: v.reciprocal(
                  out=sb["sxdT"][:, ot:ot + 1], in_=sb["sxq"][:]),
                  reads=["sxq"], writes=["sxdT"])
            kb.op("v", lambda v: v.tensor_scalar(out=XA, in0=XA,
                  scalar1=sb["sxq"][:], scalar2=None, op0=OP.mult),
                  reads=["xa", "sxq"], writes=["xa"])
            snap_chunks(kb, sb, XA, "xa", ST, "strip", 2048)
            for c in range(NC_D):
                pn = "p0" if c % 2 == 0 else "p1"
                kb.op("t", lambda t, c=c, pn=pn: t.transpose(sb[pn][:, :128],
                      ST[:, c * 128:(c + 1) * 128], sb["ident"][:]),
                      reads=["strip", "ident"], writes=[pn])
                kb.op("s", lambda s, c=c, pn=pn: s.copy(
                      out=sb["stripb"][:, c * 128:(c + 1) * 128], in_=sb[pn][:, :128]),
                      reads=[pn], writes=["stripb"])
            kb.op("g", lambda g, ot=ot: g.dma_start(
                out=bass.AP(dram["QXF"].tensor, ot * NC_D * 128 * 128,
                            [[128, 128], [16384, NC_D], [1, 128]]),
                in_=sb["stripb"][:, 0:2048].rearrange("p (c j) -> p c j", j=128)),
                reads=["stripb"], dma="ds")


def emit_phaseW_nat(kb, sb, dram):
    WN = sb["big0"][:, 0:5632]
    for base, nat, ntiles, wid in ((0, "wgN", 44, 2048), (44, "wuN", 44, 2048),
                                   (88, "wdN", 16, 5632)):
        for c in range(ntiles):
            kb.op("y", lambda y, nat=nat, c=c, wid=wid: y.dma_start(
                out=WN[:, :wid], in_=dram[nat][c * 128:(c + 1) * 128, :]),
                writes=["wnat"], dma="dl")
            kb.op("v", lambda v, c=c, base=base, wid=wid: v.tensor_reduce(
                out=sb["sQmax"][:, base + c:base + c + 1], in_=WN[:, :wid],
                axis=AX.X, op=OP.max, apply_absolute_value=True),
                reads=["wnat"], writes=["sQmax"])
            yield


def emit_phaseW_mid(kb, sb, dram):
    ld = {k: sb["ld_" + k] for k in ("mf", "r", "c", "d", "q0", "q1", "q2", "g",
                                     "l", "st", "n2", "acc", "eb", "mi", "n2i",
                                     "fb", "c260")}
    kb.bufs["ld_mx"] = kb.buf("sQmax")
    emit_longdiv(kb, sb["sQmax"][:], sb["sQ"][:], ld, 104)
    kb.bufs["sQ"] = kb.buf("ld_out")
    kb.op("v", lambda v: v.reciprocal(out=sb["sRec"][:], in_=sb["sQ"][:]),
          reads=["sQ"], writes=["sRec"])
    kb.op("t", lambda t: t.transpose(sb["p2"][0:104, :128], sb["sQ"][:], sb["ident"][:]),
          reads=["sQ", "ident"], writes=["p2"])
    kb.op("s", lambda s: s.copy(out=sb["sT"][:], in_=sb["p2"][0:104, :128]),
          reads=["p2"], writes=["sT"])
    kb.op("t", lambda t: t.transpose(sb["p3"][0:104, :128], sb["sRec"][:], sb["ident"][:]),
          reads=["sRec", "ident"], writes=["p3"])
    kb.op("s", lambda s: s.copy(out=sb["wT"][:], in_=sb["p3"][0:104, :128]),
          reads=["p3"], writes=["wT"])
    SROW = sb["big0"][0:1, 0:5632]
    for srcT, scr3 in (("wT", ("WSG", "WSU", "WSD")), ("sT", ("WSQG", "WSQU", "WSQD"))):
        for scr, base, nseg in ((scr3[0], 0, 44), (scr3[1], 44, 44), (scr3[2], 88, 16)):
            for c in range(nseg):
                kb.op("y", lambda y, c=c, base=base, srcT=srcT: y.dma_start(
                    out=SROW[:, c * 128:(c + 1) * 128],
                    in_=sb[srcT][base + c:base + c + 1, :]),
                    reads=[srcT], writes=["srow"], dma="dl")
            for s4 in range(nseg // 4):
                pw = "p2" if s4 % 2 == 0 else "p3"
                kb.op("t", lambda t, s4=s4, pw=pw: t.matmul(sb[pw][:, :512],
                      sb["ones"][0:1, :], SROW[:, s4 * 512:(s4 + 1) * 512],
                      start=True, stop=True),
                      reads=["ones", "srow"], writes=[pw])
                kb.op("s", lambda s, pw=pw: s.copy(out=sb["ysb"][:], in_=sb[pw][:, :512]),
                      reads=[pw], writes=["ysb"])
                kb.op("g", lambda g, scr=scr, s4=s4: g.dma_start(
                    out=dram[scr][:, s4 * 512:(s4 + 1) * 512], in_=sb["ysb"][:]),
                    reads=["ysb"], dma="ds")


def emit_phaseW_quant(kb, sb, dram):
    for mat, src_, qscr, nfc, nblk, out in (
            ("g", "wgT", "WSQG", NC_D, NJT, "QWG"),
            ("u", "wuT", "WSQU", NC_D, NJT, "QWU"),
            ("d", "wdT", "WSQD", NC_I, 4, "QWD")):
        for jt in range(nblk):
            kb.op("y", lambda y, jt=jt, qscr=qscr: y.dma_start(
                out=sb["sqseg"][:, :512], in_=dram[qscr][:, jt * 512:(jt + 1) * 512]),
                writes=["sqseg"], dma="dl")
            for fc in range(nfc):
                kb.op("y", lambda y, src_=src_, fc=fc, jt=jt: y.dma_start(
                    out=sb["wq512"][:, :512], in_=dram[src_][fc * 128:(fc + 1) * 128,
                                                      jt * 512:(jt + 1) * 512]),
                    writes=["wq512"], dma="dl")
                kb.op("v", lambda v: v.tensor_tensor(out=sb["wq512"][:, :512],
                      in0=sb["wq512"][:, :512], in1=sb["sqseg"][:, :512], op=OP.mult),
                      reads=["wq512", "sqseg"], writes=["wq512"])
                snap_chunks(kb, sb, sb["wq512"][:, :512], "wq512",
                            sb["wqsb"][:], "wqsb", 512)
                kb.op("g", lambda g, out=out, fc=fc, jt=jt, nblk=nblk: g.dma_start(
                    out=dram[out][(fc * nblk + jt) * 128:(fc * nblk + jt + 1) * 128, :],
                    in_=sb["wqsb"][:]), reads=["wqsb"], dma="ds")
                yield


def emit_phaseB(kb, sb, dram):
    wkb = [f"wk{c}" for c in range(NC_D)]
    xnb = [f"xn{c}" for c in range(NC_D)]
    for tkt in range(NKT):
        tt0 = tkt * 4
        for c in range(NC_D):
            kb.op("y", lambda y, c=c, tt0=tt0: y.dma_start(
                out=sb["big0"][:, c * 512:(c + 1) * 512].rearrange("p (a j) -> p a j", j=128),
                in_=bass.AP(dram["XNF"].tensor, (tt0 * NC_D + c) * 16384,
                            [[128, 128], [NC_D * 16384, 4], [1, 128]])),
                writes=[xnb[c]], dma="dl")
        kb.op("y", lambda y, tkt=tkt: y.dma_start(out=sb["cost"][:],
              in_=dram["cosf"][:, tkt * 512:(tkt + 1) * 512]), writes=["cost"], dma="dl")
        kb.op("y", lambda y, tkt=tkt: y.dma_start(out=sb["sint"][:],
              in_=dram["sinf"][:, tkt * 512:(tkt + 1) * 512]), writes=["sint"], dma="dl")
        projs = [("wkT", "KF", NKT, tkt)]
        if tkt >= NKT - NQT:
            projs.append(("wqT", "QF", NQT, tkt - (NKT - NQT)))
        for wsrc, dst, nblk, bidx in projs:
            for h in range(H):
                wt = "wkt" if h % 2 == 0 else "wku"
                wbs = [wt] * NC_D
                kb.op("y", lambda y, wsrc=wsrc, h=h, wt=wt: y.dma_start(
                    out=sb[wt][:].rearrange("p (c j) -> p c j", j=128),
                    in_=bass.AP(dram[wsrc].tensor, h * 128,
                                [[2048, 128], [128 * 2048, NC_D], [1, 128]])),
                    writes=[wt], dma="dl")
                for c in range(NC_D):
                    kb.op("t", lambda t, c=c, wt=wt: t.matmul(sb["p0"][:, :512],
                          sb[wt][:, c * 128:(c + 1) * 128],
                          sb["big0"][:, c * 512:(c + 1) * 512],
                          start=(c == 0), stop=(c == NC_D - 1)),
                          reads=[wbs[c], xnb[c]], writes=["p0"])
                kb.op("s", lambda s: s.copy(out=sb["kpre"][:], in_=sb["p0"][:, :512]),
                      reads=["p0"], writes=["kpre"])
                kb.op("t", lambda t: t.matmul(sb["p1"][:, :512], sb["ropeRT"][:],
                      sb["kpre"][:], start=True, stop=True),
                      reads=["ropeRT", "kpre"], writes=["p1"])
                kb.op("v", lambda v: v.tensor_tensor(out=sb["kc"][:], in0=sb["kpre"][:],
                      in1=sb["cost"][:], op=OP.mult),
                      reads=["kpre", "cost"], writes=["kc"])
                kb.op("v", lambda v: v.tensor_tensor(out=sb["rs"][:],
                      in0=sb["p1"][:, :512], in1=sb["sint"][:], op=OP.mult),
                      reads=["p1", "sint"], writes=["rs"])
                kb.op("v", lambda v: v.tensor_tensor(out=sb["kout"][:], in0=sb["kc"][:],
                      in1=sb["rs"][:], op=OP.add),
                      reads=["kc", "rs"], writes=["kout"])
                kb.op("g", lambda g, dst=dst, h=h, nblk=nblk, bidx=bidx: g.dma_start(
                    out=dram[dst][(h * nblk + bidx) * 128:(h * nblk + bidx + 1) * 128, :],
                    in_=sb["kout"][:]), reads=["kout"], dma="ds")
                yield
        for ot4 in range(4):
            for tp in range(2):
                for c in range(NC_D):
                    wb = "wv0" if c % 2 == 0 else "wv1"
                    kb.op("y", lambda y, c=c, ot4=ot4, wb=wb: y.dma_start(
                        out=sb[wb][:], in_=dram["wvT"][c * 128:(c + 1) * 128,
                                                       ot4 * 512:(ot4 + 1) * 512]),
                        writes=[wb], dma="dl")
                    for half in range(2):
                        ts4 = tp * 2 + half
                        pn = f"p{2 + half}"
                        kb.op("t", lambda t, c=c, ts4=ts4, pn=pn, wb=wb: t.matmul(
                              sb[pn][:, :512],
                              sb["big0"][:, c * 512 + ts4 * 128:c * 512 + ts4 * 128 + 128],
                              sb[wb][:], start=(c == 0), stop=(c == NC_D - 1)),
                              reads=[wb, xnb[c]], writes=[pn])
                for half in range(2):
                    ts4 = tp * 2 + half
                    pn = f"p{2 + half}"
                    kb.op("s", lambda s, pn=pn: s.copy(out=sb["kout"][:], in_=sb[pn][:, :512]),
                          reads=[pn], writes=["kout"])
                    kb.op("g", lambda g, tkt=tkt, ts4=ts4, ot4=ot4: g.dma_start(
                        out=bass.AP(dram["VT"].tensor,
                                    ((tkt * 4 + ts4) * H + ot4 * 4) * 16384,
                                    [[128, 128], [16384, 4], [1, 128]]),
                        in_=sb["kout"][:].rearrange("p (a j) -> p a j", j=128)),
                        reads=["kout"], dma="ds")
                yield


def emit_phaseC(kb, sb, dram):
    for h in range(H):
        for blk in range(NKT):
            kb.op("y", lambda y, h=h, blk=blk: y.dma_start(
                out=sb["big0"][:, blk * 512:(blk + 1) * 512],
                in_=dram["KF"][(h * NKT + blk) * 128:(h * NKT + blk + 1) * 128, :]),
                writes=[f"ksb{blk}"], dma="dl")
            kb.op("y", lambda y, h=h, blk=blk: y.dma_start(
                out=sb["big0"][:, 4096 + blk * 512:4096 + (blk + 1) * 512].rearrange(
                    "p (a j) -> p a j", j=128),
                in_=bass.AP(dram["VT"].tensor, (blk * 4 * H + h) * 16384,
                            [[128, 128], [H * 16384, 4], [1, 128]]),
                ), writes=[f"vsb{blk}"], dma="dl")
        for qt in range(NQT):
            kb.op("y", lambda y, h=h, qt=qt: y.dma_start(out=sb["qsb"][:],
                  in_=dram["QF"][(h * NQT + qt) * 128:(h * NQT + qt + 1) * 128, :]),
                  writes=["qsb"], dma="dl")
            nk = (T_KEYS - R_OWN) // 128 + 4 * (qt + 1)
            for j in range(nk):
                ps = "p0" if j % 2 == 0 else "p1"
                eb = "esb" if j % 2 == 0 else "esc"
                kb.op("t", lambda t, j=j, ps=ps: t.matmul(sb[ps][:, :512],
                      sb["big0"][:, j * 128:(j + 1) * 128], sb["qsb"][:],
                      start=True, stop=True),
                      reads=[f"ksb{j // 4}", "qsb"], writes=[ps])
                diag = j >= nk - 4
                if diag:
                    bias_ap, bias_buf = sb["cvec"][:, 0:1], "cvec"
                else:
                    bias_ap, bias_buf = sb["phb"][:, j:j + 1], "phb"
                kb.op("s", lambda s, ps=ps, eb=eb, bias_ap=bias_ap: s.activation(
                      out=sb[eb][:], in_=sb[ps][:, :512], func=ACT.Exp,
                      bias=bias_ap, scale=sb["scv"][:]),
                      reads=[ps, bias_buf, "scv"], writes=[eb])
                if diag:
                    off = j - (nk - 4)
                    kb.op("v", lambda v, off=off, eb=eb: v.tensor_tensor(out=sb[eb][:],
                          in0=sb[eb][:],
                          in1=sb["trimask"][:, off * 512:(off + 1) * 512], op=OP.mult),
                          reads=[eb, "trimask"], writes=[eb])
                ph = "phi" if j % 2 == 0 else "phj"
                pl = "plo" if j % 2 == 0 else "plj"
                kb.op("v", lambda v, eb=eb, ph=ph: v.tensor_copy(out=sb[ph][:],
                      in_=sb[eb][:]), reads=[eb], writes=[ph])
                kb.op("v", lambda v, eb=eb, ph=ph, pl=pl: v.tensor_tensor(out=sb[pl][:],
                      in0=sb[eb][:], in1=sb[ph][:], op=OP.subtract),
                      reads=[eb, ph], writes=[pl])
                kb.op("s", lambda s, j=j: s.copy(out=sb["vtb"][:],
                      in_=sb["big0"][:, 4096 + j * 128:4096 + (j + 1) * 128]),
                      reads=[f"vsb{j // 4}"], writes=["vtb"])
                kb.op("v", lambda v, j=j: v.tensor_tensor(out=sb["vtl"][:],
                      in0=sb["big0"][:, 4096 + j * 128:4096 + (j + 1) * 128],
                      in1=sb["vtb"][:], op=OP.subtract),
                      reads=[f"vsb{j // 4}", "vtb"], writes=["vtl"])
                for lh, rh in (("vtb", ph), ("vtb", pl), ("vtl", ph)):
                    kb.op("t", lambda t, j=j, nk=nk, lh=lh, rh=rh: t.matmul(
                          sb["p2"][:, :512], sb[lh][:], sb[rh][:],
                          start=(j == 0 and lh == "vtb" and rh == ph),
                          stop=(j == nk - 1 and lh == "vtl")),
                          reads=[lh, rh], writes=["p2"])
                for rh in (ph, pl):
                    kb.op("t", lambda t, j=j, nk=nk, rh=rh: t.matmul(sb["p3"][0:1, :512],
                          sb["onesb"][:, 0:1], sb[rh][:],
                          start=(j == 0 and rh == ph), stop=(j == nk - 1 and rh == pl)),
                          reads=["onesb", rh], writes=["p3"])
            for half in range(2):
                kb.op("v", lambda v, half=half: v.reciprocal(
                      out=sb["rsum"][0:1, half * 256:(half + 1) * 256],
                      in_=sb["p3"][0:1, half * 256:(half + 1) * 256]),
                      reads=["p3"], writes=["rsum"])
            kb.op("t", lambda t: t.matmul(sb["p0"][:, :512], sb["ones"][0:1, :],
                  sb["rsum"][0:1, :], start=True, stop=True),
                  reads=["ones", "rsum"], writes=["p0"])
            kb.op("s", lambda s: s.copy(out=sb["esb"][:], in_=sb["p0"][:, :512]),
                  reads=["p0"], writes=["esb"])
            kb.op("v", lambda v: v.tensor_tensor(out=sb["osb"][:], in0=sb["p2"][:, :512],
                  in1=sb["esb"][:], op=OP.mult),
                  reads=["p2", "esb"], writes=["osb"])
            kb.op("g", lambda g, h=h, qt=qt: g.dma_start(
                out=dram["OF"][(h * NQT + qt) * 128:(h * NQT + qt + 1) * 128, :],
                in_=sb["osb"][:]), reads=["osb"], dma="ds")


def _f_gu(kb, sb, dram, ot, qxb):
    kb.op("y", lambda y, ot=ot: y.dma_start(
        out=sb["qxf"][:].rearrange("p (c j) -> p c j", j=128),
        in_=bass.AP(dram["QXF"].tensor, ot * NC_D * 16384,
                    [[128, 128], [16384, NC_D], [1, 128]])),
        writes=qxb, dma="dl")
    for jt in range(NJT):
        kb.op("y", lambda y, jt=jt: y.dma_start(out=sb["wsgS"][:],
              in_=dram["WSG"][:, jt * 512:(jt + 1) * 512]), writes=["wsgS"], dma="dl")
        kb.op("y", lambda y, jt=jt: y.dma_start(out=sb["wsuS"][:],
              in_=dram["WSU"][:, jt * 512:(jt + 1) * 512]), writes=["wsuS"], dma="dl")
        pg = "p4" if jt % 2 == 0 else "p6"
        pu = "p5" if jt % 2 == 0 else "p7"
        gL = "qwgL" if jt % 2 == 0 else "qwgM"
        uL = "qwuL"
        kb.op("y", lambda y, jt=jt, gL=gL: y.dma_start(
            out=sb[gL][:].rearrange("p (c j) -> p c j", j=512),
            in_=bass.AP(dram["QWG"].tensor, jt * 128 * 512,
                        [[512, 128], [NJT * 128 * 512, NC_D], [1, 512]])),
            writes=[gL], dma="dl")
        kb.op("s", lambda s, jt=jt, uL=uL: s.dma_start(
            out=sb[uL][:].rearrange("p (c j) -> p c j", j=512),
            in_=bass.AP(dram["QWU"].tensor, jt * 128 * 512,
                        [[512, 128], [NJT * 128 * 512, NC_D], [1, 512]])),
            writes=[uL], dma="dl")
        for c in range(NC_D):
            kb.op("t", lambda t, c=c, pg=pg, gL=gL: t.matmul(sb[pg][:, :512],
                  sb["qxf"][:, c * 128:(c + 1) * 128],
                  sb[gL][:, c * 512:(c + 1) * 512],
                  start=(c == 0), stop=(c == NC_D - 1)),
                  reads=[qxb[c], gL], writes=[pg])
            kb.op("t", lambda t, c=c, pu=pu, uL=uL: t.matmul(sb[pu][:, :512],
                  sb["qxf"][:, c * 128:(c + 1) * 128],
                  sb[uL][:, c * 512:(c + 1) * 512],
                  start=(c == 0), stop=(c == NC_D - 1)),
                  reads=[qxb[c], uL], writes=[pu])
        kb.op("v", lambda v, ot=ot, pg=pg: v.scalar_tensor_tensor(out=sb["gsb"][:],
              in0=sb[pg][:, :512], scalar=sb["sxdT"][:, ot:ot + 1],
              in1=sb["wsgS"][:], op0=OP.mult, op1=OP.mult),
              reads=[pg, "sxdT", "wsgS"], writes=["gsb"])
        kb.op("s", lambda s: s.activation(out=sb["kpre"][:], in_=sb["gsb"][:],
              func=ACT.Silu), reads=["gsb"], writes=["kc2"])
        kb.op("v", lambda v, ot=ot, pu=pu: v.scalar_tensor_tensor(out=sb["usb"][:],
              in0=sb[pu][:, :512], scalar=sb["sxdT"][:, ot:ot + 1],
              in1=sb["wsuS"][:], op0=OP.mult, op1=OP.mult),
              reads=[pu, "sxdT", "wsuS"], writes=["usb"])
        kb.op("v", lambda v, jt=jt: v.tensor_tensor(
              out=sb["big0"][:, jt * 512:(jt + 1) * 512], in0=sb["kpre"][:],
              in1=sb["usb"][:], op=OP.mult),
              reads=["kc2", "usb"], writes=["tbuf"])
        yield


def _f_snap(kb, sb, dram, ot, qtb_):
    TB = sb["big0"][:, 0:5632]
    ST = sb["big0"][:, 6144:8192]
    kb.op("v", lambda v: v.tensor_reduce(out=sb["stq"][:], in_=TB,
          axis=AX.X, op=OP.max, apply_absolute_value=True),
          reads=["tbuf"], writes=["stq"])
    kb.op("v", lambda v: v.reciprocal(out=sb["stq"][:], in_=sb["stq"][:]),
          reads=["stq"], writes=["stq"])
    kb.op("v", lambda v: v.tensor_scalar(out=sb["stq"][:], in0=sb["stq"][:],
          scalar1=448.0, scalar2=None, op0=OP.mult),
          reads=["stq"], writes=["stq"])
    kb.op("v", lambda v: v.reciprocal(out=sb["stdq"][:], in_=sb["stq"][:]),
          reads=["stq"], writes=["stdq"])
    kb.op("v", lambda v: v.tensor_scalar(out=TB, in0=TB,
          scalar1=sb["stq"][:], scalar2=None, op0=OP.mult),
          reads=["tbuf", "stq"], writes=["tbuf"])
    for sec in range(3):
        w0 = sec * 2048
        wd = min(2048, I_FF - w0)
        snap_chunks(kb, sb, TB[:, w0:w0 + wd], "tbuf", ST[:, :wd], "strip", wd)
        for cl in range(wd // 128):
            c = sec * 16 + cl
            pn = "p6" if cl % 2 == 0 else "p7"
            kb.op("t", lambda t, cl=cl, pn=pn: t.transpose(sb[pn][:, :128],
                  ST[:, cl * 128:(cl + 1) * 128], sb["ident"][:]),
                  reads=["strip", "ident"], writes=[pn])
            kb.op("s", lambda s, c=c, pn=pn: s.copy(
                  out=sb["qtf"][:, c * 128:(c + 1) * 128], in_=sb[pn][:, :128]),
                  reads=[pn], writes=[qtb_[c]])
        yield


def _f_down(kb, sb, dram, ot, qtb_):
    for nt in range(4):
        kb.op("y", lambda y, nt=nt: y.dma_start(out=sb["wsdS"][:],
              in_=dram["WSD"][:, nt * 512:(nt + 1) * 512]), writes=["wsdS"], dma="dl")
        pd = "p2" if nt % 2 == 0 else "p3"
        for c0 in range(0, NC_I, 11):
            qe = "y" if (c0 // 11) % 2 == 0 else "s"
            dL = "stripb"
            kb.op(qe, lambda e, c0=c0, nt=nt, dL=dL: e.dma_start(
                out=sb[dL][:, :5632].rearrange("p (c j) -> p c j", j=512),
                in_=bass.AP(dram["QWD"].tensor, (c0 * 4 + nt) * 128 * 512,
                            [[512, 128], [4 * 128 * 512, 11], [1, 512]])),
                writes=[dL], dma="dl")
            for c in range(c0, c0 + 11):
                kb.op("t", lambda t, c=c, c0=c0, pd=pd, dL=dL: t.matmul(sb[pd][:, :512],
                      sb["qtf"][:, c * 128:(c + 1) * 128],
                      sb[dL][:, (c - c0) * 512:(c - c0 + 1) * 512],
                      start=(c == 0), stop=(c == NC_I - 1)),
                      reads=[qtb_[c], dL], writes=[pd])
        kb.op("v", lambda v, nt=nt, pd=pd: v.scalar_tensor_tensor(out=sb["ysb"][:],
              in0=sb[pd][:, :512], scalar=sb["stdq2"][:],
              in1=sb["wsdS"][:], op0=OP.mult, op1=OP.mult),
              reads=[pd, "stdq2", "wsdS"], writes=["ysb"])
        kb.op("g", lambda g, ot=ot, nt=nt: g.dma_start(
            out=dram["FFNY"][(ot * 4 + nt) * 128:(ot * 4 + nt + 1) * 128, :],
            in_=sb["ysb"][:]), reads=["ysb"], dma="ds")
        yield


def emit_phaseF(kb, sb, dram):
    qxb = [f"qx{c}" for c in range(NC_D)]
    qtb_ = [f"qt{c}" for c in range(NC_I)]
    for _ in _f_gu(kb, sb, dram, 0, qxb):
        pass
    for k in range(NT_OWN):
        for _ in _f_snap(kb, sb, dram, k, qtb_):
            pass
        kb.op("v", lambda v: v.tensor_copy(out=sb["stdq2"][:], in_=sb["stdq"][:]),
              reads=["stdq"], writes=["stdq2"])
        gens = [_f_down(kb, sb, dram, k, qtb_)]
        if k + 1 < NT_OWN:
            gens.append(_f_gu(kb, sb, dram, k + 1, qxb))
        done = [False] * len(gens)
        while not all(done):
            for i, gshare in enumerate(gens):
                steps = 1 if i == 0 else 3
                for _ in range(steps):
                    try:
                        next(gshare)
                    except StopIteration:
                        done[i] = True
                        break


def emit_phaseD(kb, sb, dram):
    ofb = [f"of{h}" for h in range(H)]
    for qt in range(NQT):
        for h in range(H):
            kb.op("y", lambda y, h=h, qt=qt: y.dma_start(
                out=sb["big0"][:, h * 512:(h + 1) * 512],
                in_=dram["OF"][(h * NQT + qt) * 128:(h * NQT + qt + 1) * 128, :]),
                writes=[ofb[h]], dma="dl")
        for nt in range(4):
            for h in range(H):
                wb = "wv0" if h % 2 == 0 else "wv1"
                kb.op("y", lambda y, h=h, nt=nt, wb=wb: y.dma_start(
                    out=sb[wb][:], in_=dram["woT"][h * 128:(h + 1) * 128,
                                                   nt * 512:(nt + 1) * 512]),
                    writes=[wb], dma="dl")
                for ts4 in range(4):
                    pn = f"p{2 + ts4}"
                    kb.op("t", lambda t, h=h, ts4=ts4, pn=pn, wb=wb: t.matmul(
                          sb[pn][:, :512],
                          sb["big0"][:, h * 512 + ts4 * 128:h * 512 + ts4 * 128 + 128],
                          sb[wb][:], start=(h == 0), stop=(h == H - 1)),
                          reads=[ofb[h], wb], writes=[pn])
            for ts4 in range(4):
                pn = f"p{2 + ts4}"
                xb = "gsb" if ts4 % 2 == 0 else "usb"
                fb = "kc" if ts4 % 2 == 0 else "rs"
                trow = OWN0 + qt * 512 + ts4 * 128
                kb.op("y", lambda y, trow=trow, nt=nt, xb=xb: y.dma_start(
                    out=sb[xb][:], in_=dram["xk"][trow:trow + 128,
                                                  nt * 512:(nt + 1) * 512]),
                    writes=[xb], dma="dl")
                kb.op("y", lambda y, qt=qt, ts4=ts4, nt=nt, fb=fb: y.dma_start(
                    out=sb[fb][:],
                    in_=dram["FFNY"][((qt * 4 + ts4) * 4 + nt) * 128:
                                     ((qt * 4 + ts4) * 4 + nt + 1) * 128, :]),
                    writes=[fb], dma="dl")
                kb.op("v", lambda v, pn=pn, xb=xb: v.tensor_tensor(
                      out=sb["osb"][:], in0=sb[pn][:, :512], in1=sb[xb][:], op=OP.add),
                      reads=[pn, xb], writes=["osb"])
                kb.op("v", lambda v, fb=fb: v.tensor_tensor(out=sb["osb"][:],
                      in0=sb["osb"][:], in1=sb[fb][:], op=OP.add),
                      reads=["osb", fb], writes=["osb"])
                kb.op("g", lambda g, qt=qt, ts4=ts4, nt=nt: g.dma_start(
                    out=dram["y"][qt * 512 + ts4 * 128:qt * 512 + ts4 * 128 + 128,
                                  nt * 512:(nt + 1) * 512],
                    in_=sb["osb"][:]), reads=["osb"], dma="ds")


def _run(gen):
    for _ in gen:
        pass


def _interleave(gens, ratios):
    gens = [iter(g) for g in gens]
    done = [False] * len(gens)
    while not all(done):
        for i, g in enumerate(gens):
            if done[i]:
                continue
            for _ in range(ratios[i]):
                try:
                    next(g)
                except StopIteration:
                    done[i] = True
                    break


def build_program(debug_out=(), phases="AWBCFD"):
    import contextlib
    nc = bass.Bass("TRN2", target_bir_lowering=False, debug=False)
    dram = declare_dram(nc, debug_out)
    es = contextlib.ExitStack()
    sb = alloc_all(nc, es)
    with nc.Block():
        kb = KB(nc)
        emit_consts(kb, sb, dram)
        if "A" in phases and "W" in phases:
            _interleave([emit_phaseA(kb, sb, dram), emit_phaseW_nat(kb, sb, dram)],
                        [1, 3])
            kb.barrier()
            emit_phaseW_mid(kb, sb, dram)
            kb.barrier()
        elif "A" in phases:
            _run(emit_phaseA(kb, sb, dram))
            kb.barrier()
        elif "W" in phases:
            _run(emit_phaseW_nat(kb, sb, dram))
            kb.barrier()
            emit_phaseW_mid(kb, sb, dram)
            kb.barrier()
        if "B" in phases and "W" in phases:
            _interleave([emit_phaseB(kb, sb, dram), emit_phaseW_quant(kb, sb, dram)],
                        [1, 2])
            kb.barrier()
        elif "B" in phases:
            _run(emit_phaseB(kb, sb, dram))
            kb.barrier()
        elif "W" in phases:
            _run(emit_phaseW_quant(kb, sb, dram))
            kb.barrier()
        if "F" in phases:
            emit_phaseF(kb, sb, dram)
            kb.barrier()
        if "C" in phases:
            emit_phaseC(kb, sb, dram)
            kb.barrier()
        if "D" in phases:
            emit_phaseD(kb, sb, dram)
        for k in range(kb.NDMA):
            nm = f"ds{k}"
            if kb.cnt[nm]:
                kb.eng["g"].wait_ge(kb.sem[nm], kb.cnt[nm])
        n_instr = sum(kb.cnt[e] for e in ("t", "v", "s", "g", "y"))
        kb.close_sems()
    es.close()
    return nc, n_instr


"""Host-side input preparation (pure layout + shape-only constants)."""


def rope_tables():
    import jax
    import jax.numpy as jnp
    cpu = jax.devices('cpu')[0]
    with jax.default_device(cpu):
        half, T = 64, 4096
        inv_freq = 1.0 / 10000.0 ** (jnp.arange(half, dtype=jnp.float32) / half)
        t = jnp.arange(T, dtype=jnp.float32)
        freqs = jnp.einsum('t,f->tf', t, inv_freq)
        cos = np.asarray(jnp.cos(freqs))
        sin = np.asarray(jnp.sin(freqs))
    cosfm = np.repeat(cos, 2, axis=1).T.copy()
    sinfm = np.repeat(sin, 2, axis=1).T.copy()
    return cosfm.astype(np.float32), sinfm.astype(np.float32)


EBIAS = -20.0


def make_consts():
    ones = np.ones((128, 128), np.float32)
    ident = np.eye(128, dtype=np.float32)
    R = np.zeros((128, 128), np.float32)
    for i in range(64):
        R[2 * i, 2 * i + 1] = -1.0
        R[2 * i + 1, 2 * i] = 1.0
    ropeRT = R.T.copy()
    import ml_dtypes
    tri = np.zeros((128, 2048), np.float32)
    for o in range(4):
        tk = np.arange(128)[:, None]
        tq = np.arange(512)[None, :]
        tri[:, o * 512:(o + 1) * 512] = (tq >= tk + 128 * o).astype(np.float32)
    cvec = np.zeros((128, 8), np.float32)
    c = 1.0 / math.sqrt(128.0)
    a = np.float32(c)
    b = np.float32(c - np.float64(a))
    cvec[:, 0] = EBIAS
    cvec[:, 1] = a
    cvec[:, 2] = b
    cvec[:, 4] = 1e-5
    cvec[:, 5] = 1.0
    return ones, ident, ropeRT, tri.astype(ml_dtypes.bfloat16), cvec


def prep_inputs(ins):
    x = np.ascontiguousarray(ins["L0_attn_x"], dtype=np.float32)
    down_idx = np.asarray(ins["L0_ffn_down_reorder_index"])
    wq = np.ascontiguousarray(ins["L0_attn_Wq"].T)
    wk = np.ascontiguousarray(ins["L0_attn_Wk"].T)
    wv = np.ascontiguousarray(ins["L0_attn_Wv"].T)
    wo = np.ascontiguousarray(ins["L0_attn_Wo"].T)
    wgN = np.ascontiguousarray(ins["L0_ffn_w_gate"][down_idx])
    wgT = np.ascontiguousarray(wgN.T)
    wuN = np.ascontiguousarray(ins["L0_ffn_w_up"][down_idx])
    wuT = np.ascontiguousarray(wuN.T)
    wdN = np.ascontiguousarray(ins["L0_ffn_w_down"])
    wdT = np.ascontiguousarray(wdN.T)
    cosfm, sinfm = rope_tables()
    ones, ident, ropeRT, tri, cvec = make_consts()
    cosA = np.concatenate([cosfm[:, :2048], cosfm[:, :2048]], axis=1).copy()
    sinA = np.concatenate([sinfm[:, :2048], sinfm[:, :2048]], axis=1).copy()
    phA = np.full((128, 32), EBIAS, np.float32); phA[:, :16] = -1e30
    phB = np.full((128, 32), EBIAS, np.float32)
    maps = []
    for core in range(8):
        b, half = core // 2, core % 2
        if half == 0:
            xk = np.zeros((4096, 2048), np.float32)
            xk[2048:] = x[b, :2048]
            cosc, sinc, phc = cosA, sinA, phA
        else:
            xk = np.ascontiguousarray(x[b])
            cosc, sinc, phc = cosfm, sinfm, phB
        maps.append(dict(
            xk=xk, wqT=wq, wkT=wk, wvT=wv, woT=wo,
            wgN=wgN, wgT=wgT, wuN=wuN, wuT=wuT, wdN=wdN, wdT=wdT,
            cosf=cosc, sinf=sinc,
            conA=ones, conI=ident, conR=ropeRT, conT=tri, conP=phc, conC=cvec,
        ))
    return maps


def gather_out(results):
    out = np.empty((4, 4096, 2048), np.float32)
    for core in range(8):
        b, half = core // 2, core % 2
        out[b, half * 2048:(half + 1) * 2048] = results[core]["y"]
    return out


_CACHED = {}

def _get_program():
    if "nc" not in _CACHED:
        _CACHED["nc"] = build_program(debug_out=(), phases="AWBCFD")[0]
    return _CACHED["nc"]


def kernel(**inputs):
    from concourse.bass_utils import run_bass_kernel_spmd
    maps = prep_inputs(inputs)
    nc = _get_program()
    res = run_bass_kernel_spmd(nc, maps, core_ids=list(range(8)))
    return gather_out(res.results)


# version 07
# speedup vs baseline: 1.1301x; 1.1301x over previous
import math
import numpy as np
import concourse.bass as bass
import concourse.mybir as mybir

F32 = mybir.dt.float32
BF16 = mybir.dt.bfloat16
I32 = mybir.dt.int32
OP = mybir.AluOpType
ACT = mybir.ActivationFunctionType
AX = mybir.AxisListType

T_KEYS = 4096
R_OWN = 2048
D = 2048
H = 16
HD = 128
I_FF = 5632
NT_KEY = T_KEYS // 128
NT_OWN = R_OWN // 128
NC_D = D // 128
NC_I = I_FF // 128
NJT = I_FF // 512
OWN0 = R_OWN


class Buf:
    __slots__ = ("name", "wr", "rd")
    def __init__(self, name):
        self.name = name
        self.wr = None
        self.rd = []


class KB:
    def __init__(self, nc):
        self.nc = nc
        self.eng = {
            "t": nc.tensor, "v": nc.vector, "s": nc.scalar,
            "g": nc.gpsimd, "y": nc.sync,
        }
        self.sem = {}
        self.cnt = {}
        self.wm = {}
        self._ctx = []
        for e in ("t", "v", "s", "g", "y"):
            cm = nc.semaphore(f"c_{e}")
            h = cm.__enter__()
            self._ctx.append(cm)
            self.sem[e] = h
            self.cnt[e] = 0
        self.NDMA = 8
        for d in ("dl", "ds"):
            for k in range(self.NDMA):
                cm = nc.semaphore(f"c_{d}{k}")
                h = cm.__enter__()
                self._ctx.append(cm)
                self.sem[f"{d}{k}"] = h
                self.cnt[f"{d}{k}"] = 0
            self.sem[d + "_rr"] = 0
        for e in ("t", "v", "s", "g", "y"):
            self.wm[e] = {}
        self.bufs = {}

    def close_sems(self):
        for s in reversed(self._ctx):
            s.__exit__(None, None, None)

    def buf(self, name):
        if name not in self.bufs:
            self.bufs[name] = Buf(name)
        return self.bufs[name]

    def _wait(self, consumer, token):
        if token is None:
            return
        sem_name, val = token
        if sem_name.startswith("dl") or sem_name.startswith("ds"):
            val = self.cnt[sem_name]
        w = self.wm[consumer]
        if w.get(sem_name, 0) >= val:
            return
        self.eng[consumer].wait_ge(self.sem[sem_name], val)
        w[sem_name] = val

    def op(self, e, fn, reads=(), writes=(), dma=None):
        reads = [self.buf(b) if isinstance(b, str) else b for b in reads]
        writes = [self.buf(b) if isinstance(b, str) else b for b in writes]
        for b in reads:
            self._wait(e, b.wr)
        for b in writes:
            self._wait(e, b.wr)
            for t in b.rd:
                self._wait(e, t)
        if dma is not None:
            k = self.sem[dma + "_rr"]
            self.sem[dma + "_rr"] = (k + 1) % self.NDMA
            dname = f"{dma}{k}"
            self._wait(e, (dname, self.cnt[dname]))
            r = fn(self.eng[e])
            self.cnt[dname] += 16
            r.then_inc(self.sem[dname], 16)
            tok = (dname, self.cnt[dname])
        else:
            r = fn(self.eng[e])
            self.cnt[e] += 1
            r.then_inc(self.sem[e], 1)
            tok = (e, self.cnt[e])
        for b in reads:
            b.rd.append(tok)
        for b in writes:
            b.wr = tok
            b.rd = []
        return tok

    def barrier(self):
        names = ["t", "v", "s", "g", "y"] + [f"{d}{k}" for d in ("dl", "ds") for k in range(self.NDMA)]
        targets = [(s, self.cnt[s]) for s in names if self.cnt[s] > 0]
        for e in ("t", "v", "s", "g", "y"):
            for sname, val in targets:
                if sname == e:
                    continue
                self._wait(e, (sname, val))


def snap_tiesup(kb, src, dst_bf16, tmp, n, sem_hint=None):
    raise NotImplementedError


def emit_snap(kb, srcap, srcbuf, dstap, dstbuf, t, n):
    B = kb.buf
    kb.op("v", lambda v: v.tensor_scalar(out=t["pi"][:, :n], in0=srcap.bitcast(I32),
          scalar1=0x7F800000, scalar2=None, op0=OP.bitwise_and),
          reads=[srcbuf], writes=["snap_pi"])
    kb.op("v", lambda v: v.tensor_scalar(out=t["u"][:, :n], in0=t["pi"][:, :n].bitcast(F32),
          scalar1=0.125, scalar2=2.0 ** -9, op0=OP.mult, op1=OP.max),
          reads=["snap_pi"], writes=["snap_u"])
    kb.op("v", lambda v: v.tensor_tensor(out=t["pi"][:, :n], in0=t["ci"][:, :n],
          in1=t["u"][:, :n].bitcast(I32), op=OP.subtract),
          reads=["snap_ci", "snap_u"], writes=["snap_pi"])
    kb.op("v", lambda v: v.tensor_tensor(out=t["y"][:, :n], in0=srcap,
          in1=t["pi"][:, :n].bitcast(F32), op=OP.mult),
          reads=[srcbuf, "snap_pi"], writes=["snap_y"])
    kb.op("v", lambda v: v.tensor_copy(out=t["f0"][:, :n], in_=t["y"][:, :n]),
          reads=["snap_y"], writes=["snap_f0"])
    kb.op("v", lambda v: v.tensor_copy(out=t["f1"][:, :n], in_=t["f0"][:, :n]),
          reads=["snap_f0"], writes=["snap_f1"])
    kb.op("v", lambda v: v.scalar_tensor_tensor(out=t["c1"][:, :n], in0=t["f1"][:, :n],
          scalar=-0.5, in1=t["y"][:, :n], op0=OP.add, op1=OP.is_le),
          reads=["snap_f1", "snap_y"], writes=["snap_c1"])
    kb.op("v", lambda v: v.scalar_tensor_tensor(out=t["c2"][:, :n], in0=t["f1"][:, :n],
          scalar=0.5, in1=t["y"][:, :n], op0=OP.add, op1=OP.is_le),
          reads=["snap_f1", "snap_y"], writes=["snap_c2"])
    kb.op("v", lambda v: v.tensor_tensor(out=t["c1"][:, :n], in0=t["c1"][:, :n],
          in1=t["c2"][:, :n], op=OP.add),
          reads=["snap_c1", "snap_c2"], writes=["snap_c1"])
    kb.op("v", lambda v: v.scalar_tensor_tensor(out=t["y"][:, :n], in0=t["c1"][:, :n],
          scalar=-1.0, in1=t["f1"][:, :n], op0=OP.add, op1=OP.add),
          reads=["snap_c1", "snap_f1"], writes=["snap_y"])
    kb.op("v", lambda v: v.tensor_tensor(out=dstap, in0=t["y"][:, :n],
          in1=t["u"][:, :n], op=OP.mult),
          reads=["snap_y", "snap_u"], writes=[dstbuf])


def emit_longdiv(kb, mx_ap, out_ap, t, n):
    B = "ld_"
    kb.op("v", lambda v: v.tensor_scalar(out=t["mi"][:, :n], in0=mx_ap.bitcast(I32),
          scalar1=0x7FFFFF, scalar2=0x800000, op0=OP.bitwise_and, op1=OP.bitwise_or),
          reads=["ld_mx"], writes=[B + "mi"])
    kb.op("v", lambda v: v.tensor_copy(out=t["mf"][:, :n], in_=t["mi"][:, :n]),
          reads=[B + "mi"], writes=[B + "mf"])
    kb.op("v", lambda v: v.tensor_scalar(out=t["mf"][:, :n], in0=t["mf"][:, :n],
          scalar1=2.0 ** -24, scalar2=None, op0=OP.mult),
          reads=[B + "mf"], writes=[B + "mf"])
    kb.op("v", lambda v: v.tensor_scalar(out=t["eb"][:, :n], in0=mx_ap.bitcast(I32),
          scalar1=23, scalar2=0xFF, op0=OP.logical_shift_right, op1=OP.bitwise_and),
          reads=["ld_mx"], writes=[B + "eb"])
    kb.op("v", lambda v: v.tensor_scalar(out=t["n2"][:, :n], in0=t["mf"][:, :n],
          scalar1=0.875, scalar2=None, op0=OP.is_gt),
          reads=[B + "mf"], writes=[B + "n2"])
    kb.op("v", lambda v: v.tensor_scalar(out=t["r"][:, :n], in0=t["n2"][:, :n],
          scalar1=0.4375, scalar2=0.4375, op0=OP.mult, op1=OP.add),
          reads=[B + "n2"], writes=[B + "r"])
    for qn in ("q0", "q1", "q2"):
        kb.op("v", lambda v, qn=qn: v.memset(t[qn][:, :n], 0.0),
              reads=[], writes=[B + qn])
    for i in range(1, 55):
        qn = "q0" if i <= 18 else ("q1" if i <= 36 else "q2")
        kb.op("v", lambda v: v.tensor_scalar(out=t["r"][:, :n], in0=t["r"][:, :n],
              scalar1=2.0, scalar2=None, op0=OP.mult),
              reads=[B + "r"], writes=[B + "r"])
        kb.op("v", lambda v: v.tensor_tensor(out=t["d"][:, :n], in0=t["r"][:, :n],
              in1=t["mf"][:, :n], op=OP.subtract),
              reads=[B + "r", B + "mf"], writes=[B + "d"])
        kb.op("v", lambda v: v.tensor_scalar(out=t["c"][:, :n], in0=t["d"][:, :n],
              scalar1=0.0, scalar2=None, op0=OP.is_ge),
              reads=[B + "d"], writes=[B + "c"])
        kb.op("v", lambda v: v.copy_predicated(t["r"][:, :n], t["c"][:, :n], t["d"][:, :n]),
              reads=[B + "c", B + "d", B + "r"], writes=[B + "r"])
        kb.op("v", lambda v, qn=qn: v.scalar_tensor_tensor(out=t[qn][:, :n],
              in0=t[qn][:, :n], scalar=2.0, in1=t["c"][:, :n], op0=OP.mult, op1=OP.add),
              reads=[B + qn, B + "c"], writes=[B + qn])
        if i == 53:
            kb.op("v", lambda v: v.tensor_copy(out=t["l"][:, :n], in_=t["c"][:, :n]),
                  reads=[B + "c"], writes=[B + "l"])
        if i == 54:
            kb.op("v", lambda v: v.tensor_copy(out=t["g"][:, :n], in_=t["c"][:, :n]),
                  reads=[B + "c"], writes=[B + "g"])
    kb.op("v", lambda v: v.tensor_scalar(out=t["st"][:, :n], in0=t["r"][:, :n],
          scalar1=0.0, scalar2=None, op0=OP.is_gt),
          reads=[B + "r"], writes=[B + "st"])
    kb.op("v", lambda v: v.tensor_tensor(out=t["st"][:, :n], in0=t["st"][:, :n],
          in1=t["l"][:, :n], op=OP.max),
          reads=[B + "st", B + "l"], writes=[B + "st"])
    kb.op("v", lambda v: v.tensor_tensor(out=t["st"][:, :n], in0=t["st"][:, :n],
          in1=t["g"][:, :n], op=OP.mult),
          reads=[B + "st", B + "g"], writes=[B + "st"])
    kb.op("v", lambda v: v.tensor_tensor(out=t["q2"][:, :n], in0=t["q2"][:, :n],
          in1=t["g"][:, :n], op=OP.subtract),
          reads=[B + "q2", B + "g"], writes=[B + "q2"])
    kb.op("v", lambda v: v.scalar_tensor_tensor(out=t["q2"][:, :n], in0=t["st"][:, :n],
          scalar=2.0, in1=t["q2"][:, :n], op0=OP.mult, op1=OP.add),
          reads=[B + "st", B + "q2"], writes=[B + "q2"])
    kb.op("v", lambda v: v.tensor_scalar(out=t["acc"][:, :n], in0=t["q0"][:, :n],
          scalar1=2.0 ** -18, scalar2=None, op0=OP.mult),
          reads=[B + "q0"], writes=[B + "acc"])
    kb.op("v", lambda v: v.scalar_tensor_tensor(out=t["acc"][:, :n], in0=t["q1"][:, :n],
          scalar=2.0 ** -36, in1=t["acc"][:, :n], op0=OP.mult, op1=OP.add),
          reads=[B + "q1", B + "acc"], writes=[B + "acc"])
    kb.op("v", lambda v: v.scalar_tensor_tensor(out=t["acc"][:, :n], in0=t["q2"][:, :n],
          scalar=2.0 ** -54, in1=t["acc"][:, :n], op0=OP.mult, op1=OP.add),
          reads=[B + "q2", B + "acc"], writes=[B + "acc"])
    kb.op("v", lambda v: v.tensor_copy(out=t["n2i"][:, :n], in_=t["n2"][:, :n]),
          reads=[B + "n2"], writes=[B + "n2i"])
    kb.op("v", lambda v: v.tensor_tensor(out=t["fb"][:, :n], in0=t["c260"][:, :n],
          in1=t["eb"][:, :n], op=OP.subtract),
          reads=[B + "c260", B + "eb"], writes=[B + "fb"])
    kb.op("v", lambda v: v.tensor_tensor(out=t["fb"][:, :n], in0=t["fb"][:, :n],
          in1=t["n2i"][:, :n], op=OP.subtract),
          reads=[B + "fb", B + "n2i"], writes=[B + "fb"])
    kb.op("v", lambda v: v.tensor_scalar(out=t["fb"][:, :n], in0=t["fb"][:, :n],
          scalar1=23, scalar2=None, op0=OP.logical_shift_left),
          reads=[B + "fb"], writes=[B + "fb"])
    kb.op("v", lambda v: v.tensor_tensor(out=out_ap, in0=t["acc"][:, :n],
          in1=t["fb"][:, :n].bitcast(F32), op=OP.mult),
          reads=[B + "acc", B + "fb"], writes=["ld_out"])



"""Phase emitters + full program builder for the prb megakernel."""


SNW = 384
NQT = R_OWN // 512
NKT = T_KEYS // 512


def alloc_all(nc, es):
    sb = {}
    def S(name, shape, dt=F32):
        sb[name] = es.enter_context(nc.sbuf_tensor(name, shape, dt))
    def P(name, shape, dt=F32):
        sb[name] = es.enter_context(nc.psum_tensor(name, shape, dt))
    S("ones", [128, 128]); S("ident", [128, 128]); S("ropeRT", [128, 128])
    S("trimask", [128, 2048], BF16); S("phb", [128, 32]); S("cvec", [128, 8])
    S("scv", [128, 1])
    S("big0", [128, 8192])
    S("xad", [128, 2048])
    S("wqsb", [128, 512], BF16)
    sb["xa"] = sb["big0"]
    sb["strip"] = sb["big0"]
    sb["wnat"] = sb["big0"]
    sb["tbuf"] = sb["big0"]
    sb["xnf"] = sb["big0"]
    sb["ksb"] = sb["big0"]
    sb["vsb"] = sb["big0"]
    S("stripb", [128, 5632], BF16)
    S("qtf", [128, 5632], BF16)
    S("sn_u", [128, SNW]); S("sn_y", [128, SNW]); S("sn_f1", [128, SNW])
    S("sn_c1", [128, SNW]); S("sn_c2", [128, SNW])
    S("sn_pi", [128, SNW], I32); S("sn_f0", [128, SNW], I32)
    S("sn_ci", [128, SNW], I32)
    for k in ("mf", "r", "d", "q0", "q1", "q2", "g", "l", "st", "n2", "acc"):
        S("ld_" + k, [128, 104])
    for k in ("eb", "mi", "n2i", "fb", "c260", "c"):
        S("ld_" + k, [128, 104], I32)
    S("sQmax", [128, 104]); S("sQ", [128, 104]); S("sRec", [128, 104])
    S("sT", [104, 128]); S("wT", [104, 128])
    S("mean", [128, 1]); S("var", [128, 1]); S("rstd", [128, 1])
    S("rmax", [128, 1]); S("sxq", [128, 1]); S("sxdT", [128, 16])
    S("stq", [128, 1]); S("stdq", [128, 1]); S("stdq2", [128, 1])
    S("wkt", [128, 2048]); S("wku", [128, 2048]); S("wv0", [128, 512]); S("wv1", [128, 512])
    S("kpre", [128, 512]); S("kc", [128, 512]); S("rs", [128, 512])
    S("kout", [128, 512])
    S("cost", [128, 512]); S("sint", [128, 512])
    S("qsb", [128, 512]); S("esb", [128, 512]); S("esc", [128, 512])
    sb["wq512"] = sb["qsb"]; sb["sqseg"] = sb["esb"]
    S("osb", [128, 512]); S("rsum", [1, 512])
    S("qxf", [128, 2048], BF16)
    S("qw0", [128, 512], BF16); S("qw1", [128, 512], BF16)
    S("qw2", [128, 512], BF16); S("qw3", [128, 512], BF16)
    S("qwgL", [128, 8192], BF16); S("qwuL", [128, 8192], BF16)
    S("qwgM", [128, 8192], BF16)
    S("gsb", [128, 512]); S("usb", [128, 512])
    S("ysb", [128, 512])
    S("wsgS", [128, 512]); S("wsuS", [128, 512]); S("wsdS", [128, 512])
    for i in range(8):
        P(f"p{i}", [128, 512])
    return sb


def declare_dram(nc, debug_out=()):
    d = {}
    def I(name, shape, dt=F32):
        d[name] = nc.dram_tensor(name, shape, dt, kind="ExternalInput").ap()
    def X(name, shape, dt=F32):
        if name in debug_out:
            d[name] = nc.dram_tensor(name, shape, dt, kind="ExternalOutput").ap()
        else:
            d[name] = nc.dram_tensor(name, shape, dt).ap()
    I("xk", [T_KEYS, D])
    I("wqT", [D, D]); I("wkT", [D, D]); I("wvT", [D, D]); I("woT", [D, D])
    I("wgN", [I_FF, D]); I("wgT", [D, I_FF])
    I("wuN", [I_FF, D]); I("wuT", [D, I_FF])
    I("wdN", [D, I_FF]); I("wdT", [I_FF, D])
    I("cosf", [128, T_KEYS]); I("sinf", [128, T_KEYS])
    I("conA", [128, 128]); I("conI", [128, 128]); I("conR", [128, 128])
    d["conT"] = nc.dram_tensor("conT", [128, 2048], BF16, kind="ExternalInput").ap()
    I("conP", [128, 32]); I("conC", [128, 8])
    d["y"] = nc.dram_tensor("y", [R_OWN, D], F32, kind="ExternalOutput").ap()
    X("XNF", [NT_KEY * NC_D * 128, 128])
    X("QXF", [NT_OWN * NC_D * 128, 128], BF16)
    X("KF", [H * NKT * 128, 512])
    X("VT", [NT_KEY * H * 128, 128])
    X("QF", [H * NQT * 128, 512])
    X("OF", [H * NQT * 128, 512])
    X("QWG", [NC_D * NJT * 128, 512], BF16)
    X("QWU", [NC_D * NJT * 128, 512], BF16)
    X("QWD", [NC_I * 4 * 128, 512], BF16)
    X("FFNY", [NT_OWN * 4 * 128, 512])
    X("WSQG", [128, I_FF])
    X("WSQU", [128, I_FF])
    X("WSQD", [128, D])
    X("WSG", [128, I_FF])
    X("WSU", [128, I_FF])
    X("WSD", [128, D])
    return d


def emit_consts(kb, sb, dram):
    for nm, dst in (("conA", "ones"), ("conI", "ident"), ("conR", "ropeRT"),
                    ("conT", "trimask"), ("conP", "phb"), ("conC", "cvec")):
        kb.op("y", lambda y, nm=nm, dst=dst: y.dma_start(out=sb[dst][:], in_=dram[nm][:]),
              writes=[dst], dma="dl")
    kb.op("v", lambda v: v.tensor_tensor(out=sb["scv"][:], in0=sb["cvec"][:, 1:2],
          in1=sb["cvec"][:, 2:3], op=OP.add), reads=["cvec"], writes=["scv"])
    kb.op("v", lambda v: v.memset(sb["ld_c260"][:], 263), writes=["ld_c260"])
    kb.op("v", lambda v: v.memset(sb["sn_ci"][:], 254 << 23), writes=["snap_ci"])


def snap_chunks(kb, sb, srcap, srcbuf, dstap, dstbuf, width):
    t = {k: sb["sn_" + k] for k in ("pi", "u", "y", "f0", "f1", "c1", "c2")}
    t["ci"] = sb["sn_ci"]
    off = 0
    while off < width:
        n = min(SNW, width - off)
        emit_snap(kb, srcap[:, off:off + n], srcbuf, dstap[:, off:off + n],
                  dstbuf, t, n)
        off += n


def emit_phaseA(kb, sb, dram):
    XA = sb["xad"][:]
    ST = sb["big0"][:, 6144:8192]
    for tt in range(NT_KEY):
        kb.op("y", lambda y, tt=tt: y.dma_start(out=XA,
              in_=dram["xk"][tt * 128:(tt + 1) * 128, :]), writes=["xa"], dma="dl")
        kb.op("v", lambda v: v.tensor_reduce(out=sb["mean"][:], in_=XA,
              axis=AX.X, op=OP.add), reads=["xa"], writes=["mean"])
        kb.op("v", lambda v: v.tensor_scalar(out=sb["mean"][:], in0=sb["mean"][:],
              scalar1=1.0 / 2048, scalar2=None, op0=OP.mult),
              reads=["mean"], writes=["mean"])
        kb.op("v", lambda v: v.tensor_scalar(out=XA, in0=XA,
              scalar1=sb["mean"][:], scalar2=None, op0=OP.subtract),
              reads=["xa", "mean"], writes=["xa"])
        kb.op("v", lambda v: v.tensor_tensor(out=ST, in0=XA, in1=XA, op=OP.mult),
              reads=["xa"], writes=["strip"])
        kb.op("v", lambda v: v.tensor_reduce(out=sb["var"][:], in_=ST,
              axis=AX.X, op=OP.add), reads=["strip"], writes=["var"])
        kb.op("v", lambda v: v.tensor_scalar(out=sb["var"][:], in0=sb["var"][:],
              scalar1=1.0 / 2048, scalar2=None, op0=OP.mult),
              reads=["var"], writes=["var"])
        kb.op("s", lambda s: s.activation(out=sb["rstd"][:], in_=sb["var"][:],
              func=ACT.Sqrt, bias=sb["cvec"][:, 4:5], scale=sb["cvec"][:, 5:6]),
              reads=["var", "cvec"], writes=["rstd"])
        kb.op("v", lambda v: v.reciprocal(out=sb["rstd"][:], in_=sb["rstd"][:]),
              reads=["rstd"], writes=["rstd"])
        kb.op("v", lambda v: v.tensor_scalar(out=XA, in0=XA,
              scalar1=sb["rstd"][:], scalar2=None, op0=OP.mult),
              reads=["xa", "rstd"], writes=["xa"])
        for c in range(NC_D):
            pn = "p0" if c % 2 == 0 else "p1"
            kb.op("t", lambda t, c=c, pn=pn: t.transpose(sb[pn][:, :128],
                  XA[:, c * 128:(c + 1) * 128], sb["ident"][:]),
                  reads=["xa", "ident"], writes=[pn])
            kb.op("s", lambda s, c=c, pn=pn: s.copy(out=ST[:, c * 128:(c + 1) * 128],
                  in_=sb[pn][:, :128]), reads=[pn], writes=["strip"])
        kb.op("g", lambda g, tt=tt: g.dma_start(
            out=bass.AP(dram["XNF"].tensor, tt * NC_D * 128 * 128,
                        [[128, 128], [16384, NC_D], [1, 128]]),
            in_=sb["big0"][:, 6144:8192].rearrange("p (c j) -> p c j", j=128)),
            reads=["strip"], dma="ds")
        if tt < NT_KEY - NT_OWN:
            yield
        if tt >= NT_KEY - NT_OWN:
            ot = tt - (NT_KEY - NT_OWN)
            kb.op("v", lambda v: v.tensor_reduce(out=sb["rmax"][:], in_=XA,
                  axis=AX.X, op=OP.max, apply_absolute_value=True),
                  reads=["xa"], writes=["rmax"])
            kb.op("v", lambda v: v.reciprocal(out=sb["sxq"][:], in_=sb["rmax"][:]),
                  reads=["rmax"], writes=["sxq"])
            kb.op("v", lambda v: v.tensor_scalar(out=sb["sxq"][:], in0=sb["sxq"][:],
                  scalar1=448.0, scalar2=None, op0=OP.mult),
                  reads=["sxq"], writes=["sxq"])
            kb.op("v", lambda v, ot=ot: v.reciprocal(
                  out=sb["sxdT"][:, ot:ot + 1], in_=sb["sxq"][:]),
                  reads=["sxq"], writes=["sxdT"])
            kb.op("v", lambda v: v.tensor_scalar(out=XA, in0=XA,
                  scalar1=sb["sxq"][:], scalar2=None, op0=OP.mult),
                  reads=["xa", "sxq"], writes=["xa"])
            snap_chunks(kb, sb, XA, "xa", ST, "strip", 2048)
            for c in range(NC_D):
                pn = "p0" if c % 2 == 0 else "p1"
                kb.op("t", lambda t, c=c, pn=pn: t.transpose(sb[pn][:, :128],
                      ST[:, c * 128:(c + 1) * 128], sb["ident"][:]),
                      reads=["strip", "ident"], writes=[pn])
                kb.op("s", lambda s, c=c, pn=pn: s.copy(
                      out=sb["stripb"][:, c * 128:(c + 1) * 128], in_=sb[pn][:, :128]),
                      reads=[pn], writes=["stripb"])
            kb.op("g", lambda g, ot=ot: g.dma_start(
                out=bass.AP(dram["QXF"].tensor, ot * NC_D * 128 * 128,
                            [[128, 128], [16384, NC_D], [1, 128]]),
                in_=sb["stripb"][:, 0:2048].rearrange("p (c j) -> p c j", j=128)),
                reads=["stripb"], dma="ds")


def emit_phaseW_nat(kb, sb, dram):
    WN = sb["big0"][:, 0:5632]
    for base, nat, ntiles, wid in ((0, "wgN", 44, 2048), (44, "wuN", 44, 2048),
                                   (88, "wdN", 16, 5632)):
        for c in range(ntiles):
            kb.op("y", lambda y, nat=nat, c=c, wid=wid: y.dma_start(
                out=WN[:, :wid], in_=dram[nat][c * 128:(c + 1) * 128, :]),
                writes=["wnat"], dma="dl")
            kb.op("v", lambda v, c=c, base=base, wid=wid: v.tensor_reduce(
                out=sb["sQmax"][:, base + c:base + c + 1], in_=WN[:, :wid],
                axis=AX.X, op=OP.max, apply_absolute_value=True),
                reads=["wnat"], writes=["sQmax"])
            yield


def emit_phaseW_mid(kb, sb, dram):
    ld = {k: sb["ld_" + k] for k in ("mf", "r", "c", "d", "q0", "q1", "q2", "g",
                                     "l", "st", "n2", "acc", "eb", "mi", "n2i",
                                     "fb", "c260")}
    kb.bufs["ld_mx"] = kb.buf("sQmax")
    emit_longdiv(kb, sb["sQmax"][:], sb["sQ"][:], ld, 104)
    kb.bufs["sQ"] = kb.buf("ld_out")
    kb.op("v", lambda v: v.reciprocal(out=sb["sRec"][:], in_=sb["sQ"][:]),
          reads=["sQ"], writes=["sRec"])
    kb.op("t", lambda t: t.transpose(sb["p2"][0:104, :128], sb["sQ"][:], sb["ident"][:]),
          reads=["sQ", "ident"], writes=["p2"])
    kb.op("s", lambda s: s.copy(out=sb["sT"][:], in_=sb["p2"][0:104, :128]),
          reads=["p2"], writes=["sT"])
    kb.op("t", lambda t: t.transpose(sb["p3"][0:104, :128], sb["sRec"][:], sb["ident"][:]),
          reads=["sRec", "ident"], writes=["p3"])
    kb.op("s", lambda s: s.copy(out=sb["wT"][:], in_=sb["p3"][0:104, :128]),
          reads=["p3"], writes=["wT"])
    SROW = sb["big0"][0:1, 0:5632]
    for srcT, scr3 in (("wT", ("WSG", "WSU", "WSD")), ("sT", ("WSQG", "WSQU", "WSQD"))):
        for scr, base, nseg in ((scr3[0], 0, 44), (scr3[1], 44, 44), (scr3[2], 88, 16)):
            for c in range(nseg):
                kb.op("y", lambda y, c=c, base=base, srcT=srcT: y.dma_start(
                    out=SROW[:, c * 128:(c + 1) * 128],
                    in_=sb[srcT][base + c:base + c + 1, :]),
                    reads=[srcT], writes=["srow"], dma="dl")
            for s4 in range(nseg // 4):
                pw = "p2" if s4 % 2 == 0 else "p3"
                kb.op("t", lambda t, s4=s4, pw=pw: t.matmul(sb[pw][:, :512],
                      sb["ones"][0:1, :], SROW[:, s4 * 512:(s4 + 1) * 512],
                      start=True, stop=True),
                      reads=["ones", "srow"], writes=[pw])
                kb.op("s", lambda s, pw=pw: s.copy(out=sb["ysb"][:], in_=sb[pw][:, :512]),
                      reads=[pw], writes=["ysb"])
                kb.op("g", lambda g, scr=scr, s4=s4: g.dma_start(
                    out=dram[scr][:, s4 * 512:(s4 + 1) * 512], in_=sb["ysb"][:]),
                    reads=["ysb"], dma="ds")


def emit_phaseW_quant(kb, sb, dram):
    for mat, src_, qscr, nfc, nblk, out in (
            ("g", "wgT", "WSQG", NC_D, NJT, "QWG"),
            ("u", "wuT", "WSQU", NC_D, NJT, "QWU"),
            ("d", "wdT", "WSQD", NC_I, 4, "QWD")):
        for jt in range(nblk):
            kb.op("y", lambda y, jt=jt, qscr=qscr: y.dma_start(
                out=sb["sqseg"][:, :512], in_=dram[qscr][:, jt * 512:(jt + 1) * 512]),
                writes=["sqseg"], dma="dl")
            for fc in range(nfc):
                kb.op("y", lambda y, src_=src_, fc=fc, jt=jt: y.dma_start(
                    out=sb["wq512"][:, :512], in_=dram[src_][fc * 128:(fc + 1) * 128,
                                                      jt * 512:(jt + 1) * 512]),
                    writes=["wq512"], dma="dl")
                kb.op("v", lambda v: v.tensor_tensor(out=sb["wq512"][:, :512],
                      in0=sb["wq512"][:, :512], in1=sb["sqseg"][:, :512], op=OP.mult),
                      reads=["wq512", "sqseg"], writes=["wq512"])
                snap_chunks(kb, sb, sb["wq512"][:, :512], "wq512",
                            sb["wqsb"][:], "wqsb", 512)
                kb.op("g", lambda g, out=out, fc=fc, jt=jt, nblk=nblk: g.dma_start(
                    out=dram[out][(fc * nblk + jt) * 128:(fc * nblk + jt + 1) * 128, :],
                    in_=sb["wqsb"][:]), reads=["wqsb"], dma="ds")
                yield


def emit_phaseB(kb, sb, dram):
    wkb = [f"wk{c}" for c in range(NC_D)]
    xnb = [f"xn{c}" for c in range(NC_D)]
    for tkt in range(NKT):
        tt0 = tkt * 4
        for c in range(NC_D):
            kb.op("y", lambda y, c=c, tt0=tt0: y.dma_start(
                out=sb["big0"][:, c * 512:(c + 1) * 512].rearrange("p (a j) -> p a j", j=128),
                in_=bass.AP(dram["XNF"].tensor, (tt0 * NC_D + c) * 16384,
                            [[128, 128], [NC_D * 16384, 4], [1, 128]])),
                writes=[xnb[c]], dma="dl")
        kb.op("y", lambda y, tkt=tkt: y.dma_start(out=sb["cost"][:],
              in_=dram["cosf"][:, tkt * 512:(tkt + 1) * 512]), writes=["cost"], dma="dl")
        kb.op("y", lambda y, tkt=tkt: y.dma_start(out=sb["sint"][:],
              in_=dram["sinf"][:, tkt * 512:(tkt + 1) * 512]), writes=["sint"], dma="dl")
        projs = [("wkT", "KF", NKT, tkt)]
        if tkt >= NKT - NQT:
            projs.append(("wqT", "QF", NQT, tkt - (NKT - NQT)))
        for wsrc, dst, nblk, bidx in projs:
            for h in range(H):
                wt = "wkt" if h % 2 == 0 else "wku"
                wbs = [wt] * NC_D
                kb.op("y", lambda y, wsrc=wsrc, h=h, wt=wt: y.dma_start(
                    out=sb[wt][:].rearrange("p (c j) -> p c j", j=128),
                    in_=bass.AP(dram[wsrc].tensor, h * 128,
                                [[2048, 128], [128 * 2048, NC_D], [1, 128]])),
                    writes=[wt], dma="dl")
                for c in range(NC_D):
                    kb.op("t", lambda t, c=c, wt=wt: t.matmul(sb["p0"][:, :512],
                          sb[wt][:, c * 128:(c + 1) * 128],
                          sb["big0"][:, c * 512:(c + 1) * 512],
                          start=(c == 0), stop=(c == NC_D - 1)),
                          reads=[wbs[c], xnb[c]], writes=["p0"])
                kb.op("s", lambda s: s.copy(out=sb["kpre"][:], in_=sb["p0"][:, :512]),
                      reads=["p0"], writes=["kpre"])
                kb.op("t", lambda t: t.matmul(sb["p1"][:, :512], sb["ropeRT"][:],
                      sb["kpre"][:], start=True, stop=True),
                      reads=["ropeRT", "kpre"], writes=["p1"])
                kb.op("v", lambda v: v.tensor_tensor(out=sb["kc"][:], in0=sb["kpre"][:],
                      in1=sb["cost"][:], op=OP.mult),
                      reads=["kpre", "cost"], writes=["kc"])
                kb.op("v", lambda v: v.tensor_tensor(out=sb["rs"][:],
                      in0=sb["p1"][:, :512], in1=sb["sint"][:], op=OP.mult),
                      reads=["p1", "sint"], writes=["rs"])
                kb.op("v", lambda v: v.tensor_tensor(out=sb["kout"][:], in0=sb["kc"][:],
                      in1=sb["rs"][:], op=OP.add),
                      reads=["kc", "rs"], writes=["kout"])
                kb.op("g", lambda g, dst=dst, h=h, nblk=nblk, bidx=bidx: g.dma_start(
                    out=dram[dst][(h * nblk + bidx) * 128:(h * nblk + bidx + 1) * 128, :],
                    in_=sb["kout"][:]), reads=["kout"], dma="ds")
                yield
        for ot4 in range(4):
            for tp in range(2):
                for c in range(NC_D):
                    wb = "wv0" if c % 2 == 0 else "wv1"
                    kb.op("y", lambda y, c=c, ot4=ot4, wb=wb: y.dma_start(
                        out=sb[wb][:], in_=dram["wvT"][c * 128:(c + 1) * 128,
                                                       ot4 * 512:(ot4 + 1) * 512]),
                        writes=[wb], dma="dl")
                    for half in range(2):
                        ts4 = tp * 2 + half
                        pn = f"p{2 + half}"
                        kb.op("t", lambda t, c=c, ts4=ts4, pn=pn, wb=wb: t.matmul(
                              sb[pn][:, :512],
                              sb["big0"][:, c * 512 + ts4 * 128:c * 512 + ts4 * 128 + 128],
                              sb[wb][:], start=(c == 0), stop=(c == NC_D - 1)),
                              reads=[wb, xnb[c]], writes=[pn])
                for half in range(2):
                    ts4 = tp * 2 + half
                    pn = f"p{2 + half}"
                    kb.op("s", lambda s, pn=pn: s.copy(out=sb["kout"][:], in_=sb[pn][:, :512]),
                          reads=[pn], writes=["kout"])
                    kb.op("g", lambda g, tkt=tkt, ts4=ts4, ot4=ot4: g.dma_start(
                        out=bass.AP(dram["VT"].tensor,
                                    ((tkt * 4 + ts4) * H + ot4 * 4) * 16384,
                                    [[128, 128], [16384, 4], [1, 128]]),
                        in_=sb["kout"][:].rearrange("p (a j) -> p a j", j=128)),
                        reads=["kout"], dma="ds")
                yield


def emit_phaseC(kb, sb, dram):
    for h in range(H):
        for blk in range(NKT):
            kb.op("y", lambda y, h=h, blk=blk: y.dma_start(
                out=sb["big0"][:, blk * 512:(blk + 1) * 512],
                in_=dram["KF"][(h * NKT + blk) * 128:(h * NKT + blk + 1) * 128, :]),
                writes=[f"ksb{blk}"], dma="dl")
            kb.op("y", lambda y, h=h, blk=blk: y.dma_start(
                out=sb["big0"][:, 4096 + blk * 512:4096 + (blk + 1) * 512].rearrange(
                    "p (a j) -> p a j", j=128),
                in_=bass.AP(dram["VT"].tensor, (blk * 4 * H + h) * 16384,
                            [[128, 128], [H * 16384, 4], [1, 128]]),
                ), writes=[f"vsb{blk}"], dma="dl")
        for qt in range(NQT):
            kb.op("y", lambda y, h=h, qt=qt: y.dma_start(out=sb["qsb"][:],
                  in_=dram["QF"][(h * NQT + qt) * 128:(h * NQT + qt + 1) * 128, :]),
                  writes=["qsb"], dma="dl")
            nk = (T_KEYS - R_OWN) // 128 + 4 * (qt + 1)
            for j in range(nk):
                ps = "p0" if j % 2 == 0 else "p1"
                eb = "esb" if j % 2 == 0 else "esc"
                kb.op("t", lambda t, j=j, ps=ps: t.matmul(sb[ps][:, :512],
                      sb["big0"][:, j * 128:(j + 1) * 128], sb["qsb"][:],
                      start=True, stop=True),
                      reads=[f"ksb{j // 4}", "qsb"], writes=[ps])
                diag = j >= nk - 4
                if diag:
                    bias_ap, bias_buf = sb["cvec"][:, 0:1], "cvec"
                else:
                    bias_ap, bias_buf = sb["phb"][:, j:j + 1], "phb"
                kb.op("s", lambda s, ps=ps, eb=eb, bias_ap=bias_ap: s.activation(
                      out=sb[eb][:], in_=sb[ps][:, :512], func=ACT.Exp,
                      bias=bias_ap, scale=sb["scv"][:]),
                      reads=[ps, bias_buf, "scv"], writes=[eb])
                if diag:
                    off = j - (nk - 4)
                    kb.op("v", lambda v, off=off, eb=eb: v.tensor_tensor(out=sb[eb][:],
                          in0=sb[eb][:],
                          in1=sb["trimask"][:, off * 512:(off + 1) * 512], op=OP.mult),
                          reads=[eb, "trimask"], writes=[eb])
                kb.op("t", lambda t, j=j, nk=nk, eb=eb: t.matmul(sb["p2"][:, :512],
                      sb["big0"][:, 4096 + j * 128:4096 + (j + 1) * 128], sb[eb][:],
                      start=(j == 0), stop=(j == nk - 1)),
                      reads=[f"vsb{j // 4}", eb], writes=["p2"])
                kb.op("t", lambda t, j=j, nk=nk, eb=eb: t.matmul(sb["p3"][0:1, :512],
                      sb["ones"][:, 0:1], sb[eb][:],
                      start=(j == 0), stop=(j == nk - 1)),
                      reads=["ones", eb], writes=["p3"])
            for half in range(2):
                kb.op("v", lambda v, half=half: v.reciprocal(
                      out=sb["rsum"][0:1, half * 256:(half + 1) * 256],
                      in_=sb["p3"][0:1, half * 256:(half + 1) * 256]),
                      reads=["p3"], writes=["rsum"])
            kb.op("t", lambda t: t.matmul(sb["p0"][:, :512], sb["ones"][0:1, :],
                  sb["rsum"][0:1, :], start=True, stop=True),
                  reads=["ones", "rsum"], writes=["p0"])
            kb.op("s", lambda s: s.copy(out=sb["esb"][:], in_=sb["p0"][:, :512]),
                  reads=["p0"], writes=["esb"])
            kb.op("v", lambda v: v.tensor_tensor(out=sb["osb"][:], in0=sb["p2"][:, :512],
                  in1=sb["esb"][:], op=OP.mult),
                  reads=["p2", "esb"], writes=["osb"])
            kb.op("g", lambda g, h=h, qt=qt: g.dma_start(
                out=dram["OF"][(h * NQT + qt) * 128:(h * NQT + qt + 1) * 128, :],
                in_=sb["osb"][:]), reads=["osb"], dma="ds")


def _f_gu(kb, sb, dram, ot, qxb):
    kb.op("y", lambda y, ot=ot: y.dma_start(
        out=sb["qxf"][:].rearrange("p (c j) -> p c j", j=128),
        in_=bass.AP(dram["QXF"].tensor, ot * NC_D * 16384,
                    [[128, 128], [16384, NC_D], [1, 128]])),
        writes=qxb, dma="dl")
    for jt in range(NJT):
        kb.op("y", lambda y, jt=jt: y.dma_start(out=sb["wsgS"][:],
              in_=dram["WSG"][:, jt * 512:(jt + 1) * 512]), writes=["wsgS"], dma="dl")
        kb.op("y", lambda y, jt=jt: y.dma_start(out=sb["wsuS"][:],
              in_=dram["WSU"][:, jt * 512:(jt + 1) * 512]), writes=["wsuS"], dma="dl")
        pg = "p4" if jt % 2 == 0 else "p6"
        pu = "p5" if jt % 2 == 0 else "p7"
        gL = "qwgL" if jt % 2 == 0 else "qwgM"
        uL = "qwuL"
        kb.op("y", lambda y, jt=jt, gL=gL: y.dma_start(
            out=sb[gL][:].rearrange("p (c j) -> p c j", j=512),
            in_=bass.AP(dram["QWG"].tensor, jt * 128 * 512,
                        [[512, 128], [NJT * 128 * 512, NC_D], [1, 512]])),
            writes=[gL], dma="dl")
        kb.op("s", lambda s, jt=jt, uL=uL: s.dma_start(
            out=sb[uL][:].rearrange("p (c j) -> p c j", j=512),
            in_=bass.AP(dram["QWU"].tensor, jt * 128 * 512,
                        [[512, 128], [NJT * 128 * 512, NC_D], [1, 512]])),
            writes=[uL], dma="dl")
        for c in range(NC_D):
            kb.op("t", lambda t, c=c, pg=pg, gL=gL: t.matmul(sb[pg][:, :512],
                  sb["qxf"][:, c * 128:(c + 1) * 128],
                  sb[gL][:, c * 512:(c + 1) * 512],
                  start=(c == 0), stop=(c == NC_D - 1)),
                  reads=[qxb[c], gL], writes=[pg])
            kb.op("t", lambda t, c=c, pu=pu, uL=uL: t.matmul(sb[pu][:, :512],
                  sb["qxf"][:, c * 128:(c + 1) * 128],
                  sb[uL][:, c * 512:(c + 1) * 512],
                  start=(c == 0), stop=(c == NC_D - 1)),
                  reads=[qxb[c], uL], writes=[pu])
        kb.op("v", lambda v, ot=ot, pg=pg: v.scalar_tensor_tensor(out=sb["gsb"][:],
              in0=sb[pg][:, :512], scalar=sb["sxdT"][:, ot:ot + 1],
              in1=sb["wsgS"][:], op0=OP.mult, op1=OP.mult),
              reads=[pg, "sxdT", "wsgS"], writes=["gsb"])
        kb.op("s", lambda s: s.activation(out=sb["kpre"][:], in_=sb["gsb"][:],
              func=ACT.Silu), reads=["gsb"], writes=["kc2"])
        kb.op("v", lambda v, ot=ot, pu=pu: v.scalar_tensor_tensor(out=sb["usb"][:],
              in0=sb[pu][:, :512], scalar=sb["sxdT"][:, ot:ot + 1],
              in1=sb["wsuS"][:], op0=OP.mult, op1=OP.mult),
              reads=[pu, "sxdT", "wsuS"], writes=["usb"])
        kb.op("v", lambda v, jt=jt: v.tensor_tensor(
              out=sb["big0"][:, jt * 512:(jt + 1) * 512], in0=sb["kpre"][:],
              in1=sb["usb"][:], op=OP.mult),
              reads=["kc2", "usb"], writes=["tbuf"])
        yield


def _f_snap(kb, sb, dram, ot, qtb_):
    TB = sb["big0"][:, 0:5632]
    ST = sb["big0"][:, 6144:8192]
    kb.op("v", lambda v: v.tensor_reduce(out=sb["stq"][:], in_=TB,
          axis=AX.X, op=OP.max, apply_absolute_value=True),
          reads=["tbuf"], writes=["stq"])
    kb.op("v", lambda v: v.reciprocal(out=sb["stq"][:], in_=sb["stq"][:]),
          reads=["stq"], writes=["stq"])
    kb.op("v", lambda v: v.tensor_scalar(out=sb["stq"][:], in0=sb["stq"][:],
          scalar1=448.0, scalar2=None, op0=OP.mult),
          reads=["stq"], writes=["stq"])
    kb.op("v", lambda v: v.reciprocal(out=sb["stdq"][:], in_=sb["stq"][:]),
          reads=["stq"], writes=["stdq"])
    kb.op("v", lambda v: v.tensor_scalar(out=TB, in0=TB,
          scalar1=sb["stq"][:], scalar2=None, op0=OP.mult),
          reads=["tbuf", "stq"], writes=["tbuf"])
    for sec in range(3):
        w0 = sec * 2048
        wd = min(2048, I_FF - w0)
        snap_chunks(kb, sb, TB[:, w0:w0 + wd], "tbuf", ST[:, :wd], "strip", wd)
        for cl in range(wd // 128):
            c = sec * 16 + cl
            pn = "p6" if cl % 2 == 0 else "p7"
            kb.op("t", lambda t, cl=cl, pn=pn: t.transpose(sb[pn][:, :128],
                  ST[:, cl * 128:(cl + 1) * 128], sb["ident"][:]),
                  reads=["strip", "ident"], writes=[pn])
            kb.op("s", lambda s, c=c, pn=pn: s.copy(
                  out=sb["qtf"][:, c * 128:(c + 1) * 128], in_=sb[pn][:, :128]),
                  reads=[pn], writes=[qtb_[c]])
        yield


def _f_down(kb, sb, dram, ot, qtb_):
    for nt in range(4):
        kb.op("y", lambda y, nt=nt: y.dma_start(out=sb["wsdS"][:],
              in_=dram["WSD"][:, nt * 512:(nt + 1) * 512]), writes=["wsdS"], dma="dl")
        pd = "p2" if nt % 2 == 0 else "p3"
        for c0 in range(0, NC_I, 11):
            qe = "y" if (c0 // 11) % 2 == 0 else "s"
            dL = "stripb"
            kb.op(qe, lambda e, c0=c0, nt=nt, dL=dL: e.dma_start(
                out=sb[dL][:, :5632].rearrange("p (c j) -> p c j", j=512),
                in_=bass.AP(dram["QWD"].tensor, (c0 * 4 + nt) * 128 * 512,
                            [[512, 128], [4 * 128 * 512, 11], [1, 512]])),
                writes=[dL], dma="dl")
            for c in range(c0, c0 + 11):
                kb.op("t", lambda t, c=c, c0=c0, pd=pd, dL=dL: t.matmul(sb[pd][:, :512],
                      sb["qtf"][:, c * 128:(c + 1) * 128],
                      sb[dL][:, (c - c0) * 512:(c - c0 + 1) * 512],
                      start=(c == 0), stop=(c == NC_I - 1)),
                      reads=[qtb_[c], dL], writes=[pd])
        kb.op("v", lambda v, nt=nt, pd=pd: v.scalar_tensor_tensor(out=sb["ysb"][:],
              in0=sb[pd][:, :512], scalar=sb["stdq2"][:],
              in1=sb["wsdS"][:], op0=OP.mult, op1=OP.mult),
              reads=[pd, "stdq2", "wsdS"], writes=["ysb"])
        kb.op("g", lambda g, ot=ot, nt=nt: g.dma_start(
            out=dram["FFNY"][(ot * 4 + nt) * 128:(ot * 4 + nt + 1) * 128, :],
            in_=sb["ysb"][:]), reads=["ysb"], dma="ds")
        yield


def emit_phaseF(kb, sb, dram):
    qxb = [f"qx{c}" for c in range(NC_D)]
    qtb_ = [f"qt{c}" for c in range(NC_I)]
    for _ in _f_gu(kb, sb, dram, 0, qxb):
        pass
    for k in range(NT_OWN):
        for _ in _f_snap(kb, sb, dram, k, qtb_):
            pass
        kb.op("v", lambda v: v.tensor_copy(out=sb["stdq2"][:], in_=sb["stdq"][:]),
              reads=["stdq"], writes=["stdq2"])
        gens = [_f_down(kb, sb, dram, k, qtb_)]
        if k + 1 < NT_OWN:
            gens.append(_f_gu(kb, sb, dram, k + 1, qxb))
        done = [False] * len(gens)
        while not all(done):
            for i, gshare in enumerate(gens):
                steps = 1 if i == 0 else 3
                for _ in range(steps):
                    try:
                        next(gshare)
                    except StopIteration:
                        done[i] = True
                        break


def emit_phaseD(kb, sb, dram):
    ofb = [f"of{h}" for h in range(H)]
    for qt in range(NQT):
        for h in range(H):
            kb.op("y", lambda y, h=h, qt=qt: y.dma_start(
                out=sb["big0"][:, h * 512:(h + 1) * 512],
                in_=dram["OF"][(h * NQT + qt) * 128:(h * NQT + qt + 1) * 128, :]),
                writes=[ofb[h]], dma="dl")
        for nt in range(4):
            for h in range(H):
                wb = "wv0" if h % 2 == 0 else "wv1"
                kb.op("y", lambda y, h=h, nt=nt, wb=wb: y.dma_start(
                    out=sb[wb][:], in_=dram["woT"][h * 128:(h + 1) * 128,
                                                   nt * 512:(nt + 1) * 512]),
                    writes=[wb], dma="dl")
                for ts4 in range(4):
                    pn = f"p{2 + ts4}"
                    kb.op("t", lambda t, h=h, ts4=ts4, pn=pn, wb=wb: t.matmul(
                          sb[pn][:, :512],
                          sb["big0"][:, h * 512 + ts4 * 128:h * 512 + ts4 * 128 + 128],
                          sb[wb][:], start=(h == 0), stop=(h == H - 1)),
                          reads=[ofb[h], wb], writes=[pn])
            for ts4 in range(4):
                pn = f"p{2 + ts4}"
                xb = "gsb" if ts4 % 2 == 0 else "usb"
                fb = "kc" if ts4 % 2 == 0 else "rs"
                trow = OWN0 + qt * 512 + ts4 * 128
                kb.op("y", lambda y, trow=trow, nt=nt, xb=xb: y.dma_start(
                    out=sb[xb][:], in_=dram["xk"][trow:trow + 128,
                                                  nt * 512:(nt + 1) * 512]),
                    writes=[xb], dma="dl")
                kb.op("y", lambda y, qt=qt, ts4=ts4, nt=nt, fb=fb: y.dma_start(
                    out=sb[fb][:],
                    in_=dram["FFNY"][((qt * 4 + ts4) * 4 + nt) * 128:
                                     ((qt * 4 + ts4) * 4 + nt + 1) * 128, :]),
                    writes=[fb], dma="dl")
                kb.op("v", lambda v, pn=pn, xb=xb: v.tensor_tensor(
                      out=sb["osb"][:], in0=sb[pn][:, :512], in1=sb[xb][:], op=OP.add),
                      reads=[pn, xb], writes=["osb"])
                kb.op("v", lambda v, fb=fb: v.tensor_tensor(out=sb["osb"][:],
                      in0=sb["osb"][:], in1=sb[fb][:], op=OP.add),
                      reads=["osb", fb], writes=["osb"])
                kb.op("g", lambda g, qt=qt, ts4=ts4, nt=nt: g.dma_start(
                    out=dram["y"][qt * 512 + ts4 * 128:qt * 512 + ts4 * 128 + 128,
                                  nt * 512:(nt + 1) * 512],
                    in_=sb["osb"][:]), reads=["osb"], dma="ds")


def _run(gen):
    for _ in gen:
        pass


def _interleave(gens, ratios):
    gens = [iter(g) for g in gens]
    done = [False] * len(gens)
    while not all(done):
        for i, g in enumerate(gens):
            if done[i]:
                continue
            for _ in range(ratios[i]):
                try:
                    next(g)
                except StopIteration:
                    done[i] = True
                    break


def build_program(debug_out=(), phases="AWBCFD"):
    import contextlib
    nc = bass.Bass("TRN2", target_bir_lowering=False, debug=False)
    dram = declare_dram(nc, debug_out)
    es = contextlib.ExitStack()
    sb = alloc_all(nc, es)
    with nc.Block():
        kb = KB(nc)
        emit_consts(kb, sb, dram)
        if "A" in phases and "W" in phases:
            _interleave([emit_phaseA(kb, sb, dram), emit_phaseW_nat(kb, sb, dram)],
                        [1, 3])
            kb.barrier()
            emit_phaseW_mid(kb, sb, dram)
            kb.barrier()
        elif "A" in phases:
            _run(emit_phaseA(kb, sb, dram))
            kb.barrier()
        elif "W" in phases:
            _run(emit_phaseW_nat(kb, sb, dram))
            kb.barrier()
            emit_phaseW_mid(kb, sb, dram)
            kb.barrier()
        if "B" in phases and "W" in phases:
            _interleave([emit_phaseB(kb, sb, dram), emit_phaseW_quant(kb, sb, dram)],
                        [1, 2])
            kb.barrier()
        elif "B" in phases:
            _run(emit_phaseB(kb, sb, dram))
            kb.barrier()
        elif "W" in phases:
            _run(emit_phaseW_quant(kb, sb, dram))
            kb.barrier()
        if "F" in phases:
            emit_phaseF(kb, sb, dram)
            kb.barrier()
        if "C" in phases:
            emit_phaseC(kb, sb, dram)
            kb.barrier()
        if "D" in phases:
            emit_phaseD(kb, sb, dram)
        for k in range(kb.NDMA):
            nm = f"ds{k}"
            if kb.cnt[nm]:
                kb.eng["g"].wait_ge(kb.sem[nm], kb.cnt[nm])
        n_instr = sum(kb.cnt[e] for e in ("t", "v", "s", "g", "y"))
        kb.close_sems()
    es.close()
    return nc, n_instr


"""Host-side input preparation (pure layout + shape-only constants)."""


def rope_tables():
    import jax
    import jax.numpy as jnp
    cpu = jax.devices('cpu')[0]
    with jax.default_device(cpu):
        half, T = 64, 4096
        inv_freq = 1.0 / 10000.0 ** (jnp.arange(half, dtype=jnp.float32) / half)
        t = jnp.arange(T, dtype=jnp.float32)
        freqs = jnp.einsum('t,f->tf', t, inv_freq)
        cos = np.asarray(jnp.cos(freqs))
        sin = np.asarray(jnp.sin(freqs))
    cosfm = np.repeat(cos, 2, axis=1).T.copy()
    sinfm = np.repeat(sin, 2, axis=1).T.copy()
    return cosfm.astype(np.float32), sinfm.astype(np.float32)


EBIAS = -20.0


def make_consts():
    ones = np.ones((128, 128), np.float32)
    ident = np.eye(128, dtype=np.float32)
    R = np.zeros((128, 128), np.float32)
    for i in range(64):
        R[2 * i, 2 * i + 1] = -1.0
        R[2 * i + 1, 2 * i] = 1.0
    ropeRT = R.T.copy()
    import ml_dtypes
    tri = np.zeros((128, 2048), np.float32)
    for o in range(4):
        tk = np.arange(128)[:, None]
        tq = np.arange(512)[None, :]
        tri[:, o * 512:(o + 1) * 512] = (tq >= tk + 128 * o).astype(np.float32)
    cvec = np.zeros((128, 8), np.float32)
    c = 1.0 / math.sqrt(128.0)
    a = np.float32(c)
    b = np.float32(c - np.float64(a))
    cvec[:, 0] = EBIAS
    cvec[:, 1] = a
    cvec[:, 2] = b
    cvec[:, 4] = 1e-5
    cvec[:, 5] = 1.0
    return ones, ident, ropeRT, tri.astype(ml_dtypes.bfloat16), cvec


def prep_inputs(ins):
    x = np.ascontiguousarray(ins["L0_attn_x"], dtype=np.float32)
    down_idx = np.asarray(ins["L0_ffn_down_reorder_index"])
    wq = np.ascontiguousarray(ins["L0_attn_Wq"].T)
    wk = np.ascontiguousarray(ins["L0_attn_Wk"].T)
    wv = np.ascontiguousarray(ins["L0_attn_Wv"].T)
    wo = np.ascontiguousarray(ins["L0_attn_Wo"].T)
    wgN = np.ascontiguousarray(ins["L0_ffn_w_gate"][down_idx])
    wgT = np.ascontiguousarray(wgN.T)
    wuN = np.ascontiguousarray(ins["L0_ffn_w_up"][down_idx])
    wuT = np.ascontiguousarray(wuN.T)
    wdN = np.ascontiguousarray(ins["L0_ffn_w_down"])
    wdT = np.ascontiguousarray(wdN.T)
    cosfm, sinfm = rope_tables()
    ones, ident, ropeRT, tri, cvec = make_consts()
    cosA = np.concatenate([cosfm[:, :2048], cosfm[:, :2048]], axis=1).copy()
    sinA = np.concatenate([sinfm[:, :2048], sinfm[:, :2048]], axis=1).copy()
    phA = np.full((128, 32), EBIAS, np.float32); phA[:, :16] = -1e30
    phB = np.full((128, 32), EBIAS, np.float32)
    maps = []
    for core in range(8):
        b, half = core // 2, core % 2
        if half == 0:
            xk = np.zeros((4096, 2048), np.float32)
            xk[2048:] = x[b, :2048]
            cosc, sinc, phc = cosA, sinA, phA
        else:
            xk = np.ascontiguousarray(x[b])
            cosc, sinc, phc = cosfm, sinfm, phB
        maps.append(dict(
            xk=xk, wqT=wq, wkT=wk, wvT=wv, woT=wo,
            wgN=wgN, wgT=wgT, wuN=wuN, wuT=wuT, wdN=wdN, wdT=wdT,
            cosf=cosc, sinf=sinc,
            conA=ones, conI=ident, conR=ropeRT, conT=tri, conP=phc, conC=cvec,
        ))
    return maps


def gather_out(results):
    out = np.empty((4, 4096, 2048), np.float32)
    for core in range(8):
        b, half = core // 2, core % 2
        out[b, half * 2048:(half + 1) * 2048] = results[core]["y"]
    return out


_CACHED = {}

def _get_program():
    if "nc" not in _CACHED:
        _CACHED["nc"] = build_program(debug_out=(), phases="AWBCFD")[0]
    return _CACHED["nc"]


def kernel(**inputs):
    from concourse.bass_utils import run_bass_kernel_spmd
    maps = prep_inputs(inputs)
    nc = _get_program()
    res = run_bass_kernel_spmd(nc, maps, core_ids=list(range(8)))
    return gather_out(res.results)


# version 09
# speedup vs baseline: 1.3152x; 1.1638x over previous
import math
import numpy as np
import concourse.bass as bass
import concourse.mybir as mybir

F32 = mybir.dt.float32
BF16 = mybir.dt.bfloat16
I32 = mybir.dt.int32
OP = mybir.AluOpType
ACT = mybir.ActivationFunctionType
AX = mybir.AxisListType

T_KEYS = 4096
R_OWN = 2048
D = 2048
H = 16
HD = 128
I_FF = 5632
NT_KEY = T_KEYS // 128
NT_OWN = R_OWN // 128
NC_D = D // 128
NC_I = I_FF // 128
NJT = I_FF // 512
OWN0 = R_OWN


class Buf:
    __slots__ = ("name", "wr", "rd")
    def __init__(self, name):
        self.name = name
        self.wr = None
        self.rd = []


class KB:
    def __init__(self, nc):
        self.nc = nc
        self.eng = {
            "t": nc.tensor, "v": nc.vector, "s": nc.scalar,
            "g": nc.gpsimd, "y": nc.sync,
        }
        self.sem = {}
        self.cnt = {}
        self.wm = {}
        self._ctx = []
        for e in ("t", "v", "s", "g", "y"):
            cm = nc.semaphore(f"c_{e}")
            h = cm.__enter__()
            self._ctx.append(cm)
            self.sem[e] = h
            self.cnt[e] = 0
        self.NDMA = 8
        for d in ("dl", "ds"):
            for k in range(self.NDMA):
                cm = nc.semaphore(f"c_{d}{k}")
                h = cm.__enter__()
                self._ctx.append(cm)
                self.sem[f"{d}{k}"] = h
                self.cnt[f"{d}{k}"] = 0
            self.sem[d + "_rr"] = 0
        for e in ("t", "v", "s", "g", "y"):
            self.wm[e] = {}
        self.bufs = {}

    def close_sems(self):
        for s in reversed(self._ctx):
            s.__exit__(None, None, None)

    def buf(self, name):
        if name not in self.bufs:
            self.bufs[name] = Buf(name)
        return self.bufs[name]

    def _wait(self, consumer, token):
        if token is None:
            return
        sem_name, val = token
        if sem_name.startswith("dl") or sem_name.startswith("ds"):
            val = self.cnt[sem_name]
        w = self.wm[consumer]
        if w.get(sem_name, 0) >= val:
            return
        self.eng[consumer].wait_ge(self.sem[sem_name], val)
        w[sem_name] = val

    def op(self, e, fn, reads=(), writes=(), dma=None):
        reads = [self.buf(b) if isinstance(b, str) else b for b in reads]
        writes = [self.buf(b) if isinstance(b, str) else b for b in writes]
        for b in reads:
            self._wait(e, b.wr)
        for b in writes:
            self._wait(e, b.wr)
            for t in b.rd:
                self._wait(e, t)
        if dma is not None:
            k = self.sem[dma + "_rr"]
            self.sem[dma + "_rr"] = (k + 1) % self.NDMA
            dname = f"{dma}{k}"
            self._wait(e, (dname, self.cnt[dname]))
            r = fn(self.eng[e])
            self.cnt[dname] += 16
            r.then_inc(self.sem[dname], 16)
            tok = (dname, self.cnt[dname])
        else:
            r = fn(self.eng[e])
            self.cnt[e] += 1
            r.then_inc(self.sem[e], 1)
            tok = (e, self.cnt[e])
        for b in reads:
            b.rd.append(tok)
        for b in writes:
            b.wr = tok
            b.rd = []
        return tok

    def barrier(self):
        names = ["t", "v", "s", "g", "y"] + [f"{d}{k}" for d in ("dl", "ds") for k in range(self.NDMA)]
        targets = [(s, self.cnt[s]) for s in names if self.cnt[s] > 0]
        for e in ("t", "v", "s", "g", "y"):
            for sname, val in targets:
                if sname == e:
                    continue
                self._wait(e, (sname, val))


def snap_tiesup(kb, src, dst_bf16, tmp, n, sem_hint=None):
    raise NotImplementedError


def emit_snap(kb, srcap, srcbuf, dstap, dstbuf, t, n):
    B = kb.buf
    kb.op("v", lambda v: v.tensor_scalar(out=t["pi"][:, :n], in0=srcap.bitcast(I32),
          scalar1=0x7F800000, scalar2=None, op0=OP.bitwise_and),
          reads=[srcbuf], writes=["snap_pi"])
    kb.op("v", lambda v: v.tensor_scalar(out=t["u"][:, :n], in0=t["pi"][:, :n].bitcast(F32),
          scalar1=0.125, scalar2=2.0 ** -9, op0=OP.mult, op1=OP.max),
          reads=["snap_pi"], writes=["snap_u"])
    kb.op("v", lambda v: v.tensor_tensor(out=t["pi"][:, :n], in0=t["ci"][:, :n],
          in1=t["u"][:, :n].bitcast(I32), op=OP.subtract),
          reads=["snap_ci", "snap_u"], writes=["snap_pi"])
    kb.op("v", lambda v: v.tensor_tensor(out=t["y"][:, :n], in0=srcap,
          in1=t["pi"][:, :n].bitcast(F32), op=OP.mult),
          reads=[srcbuf, "snap_pi"], writes=["snap_y"])
    kb.op("v", lambda v: v.tensor_copy(out=t["f0"][:, :n], in_=t["y"][:, :n]),
          reads=["snap_y"], writes=["snap_f0"])
    kb.op("v", lambda v: v.tensor_copy(out=t["f1"][:, :n], in_=t["f0"][:, :n]),
          reads=["snap_f0"], writes=["snap_f1"])
    kb.op("v", lambda v: v.scalar_tensor_tensor(out=t["c1"][:, :n], in0=t["f1"][:, :n],
          scalar=-0.5, in1=t["y"][:, :n], op0=OP.add, op1=OP.is_le),
          reads=["snap_f1", "snap_y"], writes=["snap_c1"])
    kb.op("v", lambda v: v.scalar_tensor_tensor(out=t["c2"][:, :n], in0=t["f1"][:, :n],
          scalar=0.5, in1=t["y"][:, :n], op0=OP.add, op1=OP.is_le),
          reads=["snap_f1", "snap_y"], writes=["snap_c2"])
    kb.op("v", lambda v: v.tensor_tensor(out=t["c1"][:, :n], in0=t["c1"][:, :n],
          in1=t["c2"][:, :n], op=OP.add),
          reads=["snap_c1", "snap_c2"], writes=["snap_c1"])
    kb.op("v", lambda v: v.scalar_tensor_tensor(out=t["y"][:, :n], in0=t["c1"][:, :n],
          scalar=-1.0, in1=t["f1"][:, :n], op0=OP.add, op1=OP.add),
          reads=["snap_c1", "snap_f1"], writes=["snap_y"])
    kb.op("v", lambda v: v.tensor_tensor(out=dstap, in0=t["y"][:, :n],
          in1=t["u"][:, :n], op=OP.mult),
          reads=["snap_y", "snap_u"], writes=[dstbuf])


def emit_longdiv(kb, mx_ap, out_ap, t, n):
    B = "ld_"
    kb.op("v", lambda v: v.tensor_scalar(out=t["mi"][:, :n], in0=mx_ap.bitcast(I32),
          scalar1=0x7FFFFF, scalar2=0x800000, op0=OP.bitwise_and, op1=OP.bitwise_or),
          reads=["ld_mx"], writes=[B + "mi"])
    kb.op("v", lambda v: v.tensor_copy(out=t["mf"][:, :n], in_=t["mi"][:, :n]),
          reads=[B + "mi"], writes=[B + "mf"])
    kb.op("v", lambda v: v.tensor_scalar(out=t["mf"][:, :n], in0=t["mf"][:, :n],
          scalar1=2.0 ** -24, scalar2=None, op0=OP.mult),
          reads=[B + "mf"], writes=[B + "mf"])
    kb.op("v", lambda v: v.tensor_scalar(out=t["eb"][:, :n], in0=mx_ap.bitcast(I32),
          scalar1=23, scalar2=0xFF, op0=OP.logical_shift_right, op1=OP.bitwise_and),
          reads=["ld_mx"], writes=[B + "eb"])
    kb.op("v", lambda v: v.tensor_scalar(out=t["n2"][:, :n], in0=t["mf"][:, :n],
          scalar1=0.875, scalar2=None, op0=OP.is_gt),
          reads=[B + "mf"], writes=[B + "n2"])
    kb.op("v", lambda v: v.tensor_scalar(out=t["r"][:, :n], in0=t["n2"][:, :n],
          scalar1=0.4375, scalar2=0.4375, op0=OP.mult, op1=OP.add),
          reads=[B + "n2"], writes=[B + "r"])
    for qn in ("q0", "q1", "q2"):
        kb.op("v", lambda v, qn=qn: v.memset(t[qn][:, :n], 0.0),
              reads=[], writes=[B + qn])
    for i in range(1, 55):
        qn = "q0" if i <= 18 else ("q1" if i <= 36 else "q2")
        kb.op("v", lambda v: v.tensor_scalar(out=t["r"][:, :n], in0=t["r"][:, :n],
              scalar1=2.0, scalar2=None, op0=OP.mult),
              reads=[B + "r"], writes=[B + "r"])
        kb.op("v", lambda v: v.tensor_tensor(out=t["d"][:, :n], in0=t["r"][:, :n],
              in1=t["mf"][:, :n], op=OP.subtract),
              reads=[B + "r", B + "mf"], writes=[B + "d"])
        kb.op("v", lambda v: v.tensor_scalar(out=t["c"][:, :n], in0=t["d"][:, :n],
              scalar1=0.0, scalar2=None, op0=OP.is_ge),
              reads=[B + "d"], writes=[B + "c"])
        kb.op("v", lambda v: v.copy_predicated(t["r"][:, :n], t["c"][:, :n], t["d"][:, :n]),
              reads=[B + "c", B + "d", B + "r"], writes=[B + "r"])
        kb.op("v", lambda v, qn=qn: v.scalar_tensor_tensor(out=t[qn][:, :n],
              in0=t[qn][:, :n], scalar=2.0, in1=t["c"][:, :n], op0=OP.mult, op1=OP.add),
              reads=[B + qn, B + "c"], writes=[B + qn])
        if i == 53:
            kb.op("v", lambda v: v.tensor_copy(out=t["l"][:, :n], in_=t["c"][:, :n]),
                  reads=[B + "c"], writes=[B + "l"])
        if i == 54:
            kb.op("v", lambda v: v.tensor_copy(out=t["g"][:, :n], in_=t["c"][:, :n]),
                  reads=[B + "c"], writes=[B + "g"])
    kb.op("v", lambda v: v.tensor_scalar(out=t["st"][:, :n], in0=t["r"][:, :n],
          scalar1=0.0, scalar2=None, op0=OP.is_gt),
          reads=[B + "r"], writes=[B + "st"])
    kb.op("v", lambda v: v.tensor_tensor(out=t["st"][:, :n], in0=t["st"][:, :n],
          in1=t["l"][:, :n], op=OP.max),
          reads=[B + "st", B + "l"], writes=[B + "st"])
    kb.op("v", lambda v: v.tensor_tensor(out=t["st"][:, :n], in0=t["st"][:, :n],
          in1=t["g"][:, :n], op=OP.mult),
          reads=[B + "st", B + "g"], writes=[B + "st"])
    kb.op("v", lambda v: v.tensor_tensor(out=t["q2"][:, :n], in0=t["q2"][:, :n],
          in1=t["g"][:, :n], op=OP.subtract),
          reads=[B + "q2", B + "g"], writes=[B + "q2"])
    kb.op("v", lambda v: v.scalar_tensor_tensor(out=t["q2"][:, :n], in0=t["st"][:, :n],
          scalar=2.0, in1=t["q2"][:, :n], op0=OP.mult, op1=OP.add),
          reads=[B + "st", B + "q2"], writes=[B + "q2"])
    kb.op("v", lambda v: v.tensor_scalar(out=t["acc"][:, :n], in0=t["q0"][:, :n],
          scalar1=2.0 ** -18, scalar2=None, op0=OP.mult),
          reads=[B + "q0"], writes=[B + "acc"])
    kb.op("v", lambda v: v.scalar_tensor_tensor(out=t["acc"][:, :n], in0=t["q1"][:, :n],
          scalar=2.0 ** -36, in1=t["acc"][:, :n], op0=OP.mult, op1=OP.add),
          reads=[B + "q1", B + "acc"], writes=[B + "acc"])
    kb.op("v", lambda v: v.scalar_tensor_tensor(out=t["acc"][:, :n], in0=t["q2"][:, :n],
          scalar=2.0 ** -54, in1=t["acc"][:, :n], op0=OP.mult, op1=OP.add),
          reads=[B + "q2", B + "acc"], writes=[B + "acc"])
    kb.op("v", lambda v: v.tensor_copy(out=t["n2i"][:, :n], in_=t["n2"][:, :n]),
          reads=[B + "n2"], writes=[B + "n2i"])
    kb.op("v", lambda v: v.tensor_tensor(out=t["fb"][:, :n], in0=t["c260"][:, :n],
          in1=t["eb"][:, :n], op=OP.subtract),
          reads=[B + "c260", B + "eb"], writes=[B + "fb"])
    kb.op("v", lambda v: v.tensor_tensor(out=t["fb"][:, :n], in0=t["fb"][:, :n],
          in1=t["n2i"][:, :n], op=OP.subtract),
          reads=[B + "fb", B + "n2i"], writes=[B + "fb"])
    kb.op("v", lambda v: v.tensor_scalar(out=t["fb"][:, :n], in0=t["fb"][:, :n],
          scalar1=23, scalar2=None, op0=OP.logical_shift_left),
          reads=[B + "fb"], writes=[B + "fb"])
    kb.op("v", lambda v: v.tensor_tensor(out=out_ap, in0=t["acc"][:, :n],
          in1=t["fb"][:, :n].bitcast(F32), op=OP.mult),
          reads=[B + "acc", B + "fb"], writes=["ld_out"])



"""Phase emitters + full program builder for the prb megakernel."""


SNW = 320
NQT = R_OWN // 512
NKT = T_KEYS // 512


def alloc_all(nc, es):
    sb = {}
    def S(name, shape, dt=F32):
        sb[name] = es.enter_context(nc.sbuf_tensor(name, shape, dt))
    def P(name, shape, dt=F32):
        sb[name] = es.enter_context(nc.psum_tensor(name, shape, dt))
    S("ones", [128, 128]); S("ident", [128, 128]); S("ropeRT", [128, 128])
    S("trimask", [128, 2048], BF16); S("phb", [128, 32]); S("cvec", [128, 8])
    S("scv", [128, 1])
    S("big0", [128, 8192])
    S("xad", [128, 2048])
    S("wqsb", [128, 512], BF16)
    sb["xa"] = sb["big0"]
    sb["strip"] = sb["big0"]
    sb["wnat"] = sb["big0"]
    sb["tbuf"] = sb["big0"]
    sb["xnf"] = sb["big0"]
    sb["ksb"] = sb["big0"]
    sb["vsb"] = sb["big0"]
    S("stripb", [128, 2048], BF16)
    S("tb0", [128, 5632], BF16); S("tb1", [128, 5632], BF16)
    S("qtf", [128, 5632], BF16)
    S("sn_u", [128, SNW]); S("sn_y", [128, SNW]); S("sn_f1", [128, SNW])
    S("sn_c1", [128, SNW]); S("sn_c2", [128, SNW])
    S("sn_pi", [128, SNW], I32); S("sn_f0", [128, SNW], I32)
    S("sn_ci", [128, SNW], I32)
    for k in ("mf", "r", "d", "q0", "q1", "q2", "g", "l", "st", "n2", "acc"):
        S("ld_" + k, [128, 104])
    for k in ("eb", "mi", "n2i", "fb", "c260", "c"):
        S("ld_" + k, [128, 104], I32)
    S("sQmax", [128, 104]); S("sQ", [128, 104]); S("sRec", [128, 104])
    S("sT", [104, 128]); S("wT", [104, 128])
    S("mean", [128, 1]); S("var", [128, 1]); S("rstd", [128, 1])
    S("rmax", [128, 1]); S("sxq", [128, 1]); S("sxdT", [128, 16])
    S("stq", [128, 1]); S("stdq", [128, 1]); S("stdq2", [128, 1])
    S("wkt", [128, 2048]); S("wku", [128, 2048]); S("wv0", [128, 512]); S("wv1", [128, 512])
    S("kpre", [128, 512]); S("kpr2", [128, 512]); S("kc", [128, 512]); S("rs", [128, 512])
    S("kout", [128, 512])
    S("cost", [128, 512]); S("sint", [128, 512])
    S("qsb", [128, 512]); S("esb", [128, 512]); S("esc", [128, 512])
    sb["wq512"] = sb["qsb"]; sb["sqseg"] = sb["esb"]
    S("osb", [128, 512]); S("rsum", [1, 512])
    S("qxf", [128, 2048], BF16); S("qxf2", [128, 2048], BF16)
    S("qwgL", [128, 8192], BF16); S("qwuL", [128, 8192], BF16)
    S("gsb", [128, 512]); S("usb", [128, 512])
    S("ysb", [128, 512])
    S("wsgS", [128, 512]); S("wsuS", [128, 512]); S("wsdS", [128, 512])
    for i in range(8):
        P(f"p{i}", [128, 512])
    return sb


def declare_dram(nc, debug_out=()):
    d = {}
    def I(name, shape, dt=F32):
        d[name] = nc.dram_tensor(name, shape, dt, kind="ExternalInput").ap()
    def X(name, shape, dt=F32):
        if name in debug_out:
            d[name] = nc.dram_tensor(name, shape, dt, kind="ExternalOutput").ap()
        else:
            d[name] = nc.dram_tensor(name, shape, dt).ap()
    I("xk", [T_KEYS, D])
    I("wqT", [D, D]); I("wkT", [D, D]); I("wvT", [D, D]); I("woT", [D, D])
    I("wgN", [I_FF, D]); I("wgT", [D, I_FF])
    I("wuN", [I_FF, D]); I("wuT", [D, I_FF])
    I("wdN", [D, I_FF]); I("wdT", [I_FF, D])
    I("cosf", [128, T_KEYS]); I("sinf", [128, T_KEYS])
    I("conA", [128, 128]); I("conI", [128, 128]); I("conR", [128, 128])
    d["conT"] = nc.dram_tensor("conT", [128, 2048], BF16, kind="ExternalInput").ap()
    I("conP", [128, 32]); I("conC", [128, 8])
    d["y"] = nc.dram_tensor("y", [R_OWN, D], F32, kind="ExternalOutput").ap()
    X("XNF", [NT_KEY * NC_D * 128, 128])
    X("QXF", [NT_OWN * NC_D * 128, 128], BF16)
    X("KF", [H * NKT * 128, 512])
    X("VT", [NT_KEY * H * 128, 128])
    X("QF", [H * NQT * 128, 512])
    X("OF", [H * NQT * 128, 512])
    X("QWG", [NC_D * NJT * 128, 512], BF16)
    X("QWU", [NC_D * NJT * 128, 512], BF16)
    X("QWD", [NC_I * 4 * 128, 512], BF16)
    X("FFNY", [NT_OWN * 4 * 128, 512])
    X("WSQG", [128, I_FF])
    X("WSQU", [128, I_FF])
    X("WSQD", [128, D])
    X("WSG", [128, I_FF])
    X("WSU", [128, I_FF])
    X("WSD", [128, D])
    return d


def emit_consts(kb, sb, dram):
    for nm, dst in (("conA", "ones"), ("conI", "ident"), ("conR", "ropeRT"),
                    ("conT", "trimask"), ("conP", "phb"), ("conC", "cvec")):
        kb.op("y", lambda y, nm=nm, dst=dst: y.dma_start(out=sb[dst][:], in_=dram[nm][:]),
              writes=[dst], dma="dl")
    kb.op("v", lambda v: v.tensor_tensor(out=sb["scv"][:], in0=sb["cvec"][:, 1:2],
          in1=sb["cvec"][:, 2:3], op=OP.add), reads=["cvec"], writes=["scv"])
    kb.op("v", lambda v: v.memset(sb["ld_c260"][:], 263), writes=["ld_c260"])
    kb.op("v", lambda v: v.memset(sb["sn_ci"][:], 254 << 23), writes=["snap_ci"])


def snap_chunks(kb, sb, srcap, srcbuf, dstap, dstbuf, width):
    t = {k: sb["sn_" + k] for k in ("pi", "u", "y", "f0", "f1", "c1", "c2")}
    t["ci"] = sb["sn_ci"]
    off = 0
    while off < width:
        n = min(SNW, width - off)
        emit_snap(kb, srcap[:, off:off + n], srcbuf, dstap[:, off:off + n],
                  dstbuf, t, n)
        off += n


def emit_phaseA(kb, sb, dram):
    XA = sb["xad"][:]
    ST = sb["big0"][:, 6144:8192]
    for tt in range(NT_KEY):
        kb.op("y", lambda y, tt=tt: y.dma_start(out=XA,
              in_=dram["xk"][tt * 128:(tt + 1) * 128, :]), writes=["xa"], dma="dl")
        kb.op("v", lambda v: v.tensor_reduce(out=sb["mean"][:], in_=XA,
              axis=AX.X, op=OP.add), reads=["xa"], writes=["mean"])
        kb.op("v", lambda v: v.tensor_scalar(out=sb["mean"][:], in0=sb["mean"][:],
              scalar1=1.0 / 2048, scalar2=None, op0=OP.mult),
              reads=["mean"], writes=["mean"])
        kb.op("v", lambda v: v.tensor_scalar(out=XA, in0=XA,
              scalar1=sb["mean"][:], scalar2=None, op0=OP.subtract),
              reads=["xa", "mean"], writes=["xa"])
        kb.op("v", lambda v: v.tensor_tensor(out=ST, in0=XA, in1=XA, op=OP.mult),
              reads=["xa"], writes=["strip"])
        kb.op("v", lambda v: v.tensor_reduce(out=sb["var"][:], in_=ST,
              axis=AX.X, op=OP.add), reads=["strip"], writes=["var"])
        kb.op("v", lambda v: v.tensor_scalar(out=sb["var"][:], in0=sb["var"][:],
              scalar1=1.0 / 2048, scalar2=None, op0=OP.mult),
              reads=["var"], writes=["var"])
        kb.op("s", lambda s: s.activation(out=sb["rstd"][:], in_=sb["var"][:],
              func=ACT.Sqrt, bias=sb["cvec"][:, 4:5], scale=sb["cvec"][:, 5:6]),
              reads=["var", "cvec"], writes=["rstd"])
        kb.op("v", lambda v: v.reciprocal(out=sb["rstd"][:], in_=sb["rstd"][:]),
              reads=["rstd"], writes=["rstd"])
        kb.op("v", lambda v: v.tensor_scalar(out=XA, in0=XA,
              scalar1=sb["rstd"][:], scalar2=None, op0=OP.mult),
              reads=["xa", "rstd"], writes=["xa"])
        for c in range(NC_D):
            pn = "p0" if c % 2 == 0 else "p1"
            kb.op("t", lambda t, c=c, pn=pn: t.transpose(sb[pn][:, :128],
                  XA[:, c * 128:(c + 1) * 128], sb["ident"][:]),
                  reads=["xa", "ident"], writes=[pn])
            kb.op("s", lambda s, c=c, pn=pn: s.copy(out=ST[:, c * 128:(c + 1) * 128],
                  in_=sb[pn][:, :128]), reads=[pn], writes=["strip"])
        kb.op("g", lambda g, tt=tt: g.dma_start(
            out=bass.AP(dram["XNF"].tensor, tt * NC_D * 128 * 128,
                        [[128, 128], [16384, NC_D], [1, 128]]),
            in_=sb["big0"][:, 6144:8192].rearrange("p (c j) -> p c j", j=128)),
            reads=["strip"], dma="ds")
        if tt < NT_KEY - NT_OWN:
            yield
        if tt >= NT_KEY - NT_OWN:
            ot = tt - (NT_KEY - NT_OWN)
            kb.op("v", lambda v: v.tensor_reduce(out=sb["rmax"][:], in_=XA,
                  axis=AX.X, op=OP.max, apply_absolute_value=True),
                  reads=["xa"], writes=["rmax"])
            kb.op("v", lambda v: v.reciprocal(out=sb["sxq"][:], in_=sb["rmax"][:]),
                  reads=["rmax"], writes=["sxq"])
            kb.op("v", lambda v: v.tensor_scalar(out=sb["sxq"][:], in0=sb["sxq"][:],
                  scalar1=448.0, scalar2=None, op0=OP.mult),
                  reads=["sxq"], writes=["sxq"])
            kb.op("v", lambda v, ot=ot: v.reciprocal(
                  out=sb["sxdT"][:, ot:ot + 1], in_=sb["sxq"][:]),
                  reads=["sxq"], writes=["sxdT"])
            kb.op("v", lambda v: v.tensor_scalar(out=XA, in0=XA,
                  scalar1=sb["sxq"][:], scalar2=None, op0=OP.mult),
                  reads=["xa", "sxq"], writes=["xa"])
            snap_chunks(kb, sb, XA, "xa", ST, "strip", 2048)
            for c in range(NC_D):
                pn = "p0" if c % 2 == 0 else "p1"
                kb.op("t", lambda t, c=c, pn=pn: t.transpose(sb[pn][:, :128],
                      ST[:, c * 128:(c + 1) * 128], sb["ident"][:]),
                      reads=["strip", "ident"], writes=[pn])
                kb.op("s", lambda s, c=c, pn=pn: s.copy(
                      out=sb["stripb"][:, c * 128:(c + 1) * 128], in_=sb[pn][:, :128]),
                      reads=[pn], writes=["stripb"])
            kb.op("g", lambda g, ot=ot: g.dma_start(
                out=bass.AP(dram["QXF"].tensor, ot * NC_D * 128 * 128,
                            [[128, 128], [16384, NC_D], [1, 128]]),
                in_=sb["stripb"][:, 0:2048].rearrange("p (c j) -> p c j", j=128)),
                reads=["stripb"], dma="ds")


def emit_phaseW_nat(kb, sb, dram):
    WN = sb["big0"][:, 0:5632]
    for base, nat, ntiles, wid in ((0, "wgN", 44, 2048), (44, "wuN", 44, 2048),
                                   (88, "wdN", 16, 5632)):
        for c in range(ntiles):
            kb.op("y", lambda y, nat=nat, c=c, wid=wid: y.dma_start(
                out=WN[:, :wid], in_=dram[nat][c * 128:(c + 1) * 128, :]),
                writes=["wnat"], dma="dl")
            kb.op("v", lambda v, c=c, base=base, wid=wid: v.tensor_reduce(
                out=sb["sQmax"][:, base + c:base + c + 1], in_=WN[:, :wid],
                axis=AX.X, op=OP.max, apply_absolute_value=True),
                reads=["wnat"], writes=["sQmax"])
            yield


def emit_phaseW_mid(kb, sb, dram):
    ld = {k: sb["ld_" + k] for k in ("mf", "r", "c", "d", "q0", "q1", "q2", "g",
                                     "l", "st", "n2", "acc", "eb", "mi", "n2i",
                                     "fb", "c260")}
    kb.bufs["ld_mx"] = kb.buf("sQmax")
    emit_longdiv(kb, sb["sQmax"][:], sb["sQ"][:], ld, 104)
    kb.bufs["sQ"] = kb.buf("ld_out")
    kb.op("v", lambda v: v.reciprocal(out=sb["sRec"][:], in_=sb["sQ"][:]),
          reads=["sQ"], writes=["sRec"])
    kb.op("t", lambda t: t.transpose(sb["p2"][0:104, :128], sb["sQ"][:], sb["ident"][:]),
          reads=["sQ", "ident"], writes=["p2"])
    kb.op("s", lambda s: s.copy(out=sb["sT"][:], in_=sb["p2"][0:104, :128]),
          reads=["p2"], writes=["sT"])
    kb.op("t", lambda t: t.transpose(sb["p3"][0:104, :128], sb["sRec"][:], sb["ident"][:]),
          reads=["sRec", "ident"], writes=["p3"])
    kb.op("s", lambda s: s.copy(out=sb["wT"][:], in_=sb["p3"][0:104, :128]),
          reads=["p3"], writes=["wT"])
    SROW = sb["big0"][0:1, 0:5632]
    for srcT, scr3 in (("wT", ("WSG", "WSU", "WSD")), ("sT", ("WSQG", "WSQU", "WSQD"))):
        for scr, base, nseg in ((scr3[0], 0, 44), (scr3[1], 44, 44), (scr3[2], 88, 16)):
            for c in range(nseg):
                kb.op("y", lambda y, c=c, base=base, srcT=srcT: y.dma_start(
                    out=SROW[:, c * 128:(c + 1) * 128],
                    in_=sb[srcT][base + c:base + c + 1, :]),
                    reads=[srcT], writes=["srow"], dma="dl")
            for s4 in range(nseg // 4):
                pw = "p2" if s4 % 2 == 0 else "p3"
                kb.op("t", lambda t, s4=s4, pw=pw: t.matmul(sb[pw][:, :512],
                      sb["ones"][0:1, :], SROW[:, s4 * 512:(s4 + 1) * 512],
                      start=True, stop=True),
                      reads=["ones", "srow"], writes=[pw])
                kb.op("s", lambda s, pw=pw: s.copy(out=sb["ysb"][:], in_=sb[pw][:, :512]),
                      reads=[pw], writes=["ysb"])
                kb.op("g", lambda g, scr=scr, s4=s4: g.dma_start(
                    out=dram[scr][:, s4 * 512:(s4 + 1) * 512], in_=sb["ysb"][:]),
                    reads=["ysb"], dma="ds")


def emit_phaseW_quant(kb, sb, dram):
    for mat, src_, qscr, nfc, nblk, out in (
            ("g", "wgT", "WSQG", NC_D, NJT, "QWG"),
            ("u", "wuT", "WSQU", NC_D, NJT, "QWU"),
            ("d", "wdT", "WSQD", NC_I, 4, "QWD")):
        for jt in range(nblk):
            kb.op("y", lambda y, jt=jt, qscr=qscr: y.dma_start(
                out=sb["sqseg"][:, :512], in_=dram[qscr][:, jt * 512:(jt + 1) * 512]),
                writes=["sqseg"], dma="dl")
            for fc in range(nfc):
                kb.op("y", lambda y, src_=src_, fc=fc, jt=jt: y.dma_start(
                    out=sb["wq512"][:, :512], in_=dram[src_][fc * 128:(fc + 1) * 128,
                                                      jt * 512:(jt + 1) * 512]),
                    writes=["wq512"], dma="dl")
                kb.op("v", lambda v: v.tensor_tensor(out=sb["wq512"][:, :512],
                      in0=sb["wq512"][:, :512], in1=sb["sqseg"][:, :512], op=OP.mult),
                      reads=["wq512", "sqseg"], writes=["wq512"])
                snap_chunks(kb, sb, sb["wq512"][:, :512], "wq512",
                            sb["wqsb"][:], "wqsb", 512)
                kb.op("g", lambda g, out=out, fc=fc, jt=jt, nblk=nblk: g.dma_start(
                    out=dram[out][(fc * nblk + jt) * 128:(fc * nblk + jt + 1) * 128, :],
                    in_=sb["wqsb"][:]), reads=["wqsb"], dma="ds")
                yield


def emit_phaseB(kb, sb, dram):
    wkb = [f"wk{c}" for c in range(NC_D)]
    xnb = [f"xn{c}" for c in range(NC_D)]
    for tkt in range(NKT):
        tt0 = tkt * 4
        for c in range(NC_D):
            kb.op("y", lambda y, c=c, tt0=tt0: y.dma_start(
                out=sb["big0"][:, c * 512:(c + 1) * 512].rearrange("p (a j) -> p a j", j=128),
                in_=bass.AP(dram["XNF"].tensor, (tt0 * NC_D + c) * 16384,
                            [[128, 128], [NC_D * 16384, 4], [1, 128]])),
                writes=[xnb[c]], dma="dl")
        kb.op("y", lambda y, tkt=tkt: y.dma_start(out=sb["cost"][:],
              in_=dram["cosf"][:, tkt * 512:(tkt + 1) * 512]), writes=["cost"], dma="dl")
        kb.op("y", lambda y, tkt=tkt: y.dma_start(out=sb["sint"][:],
              in_=dram["sinf"][:, tkt * 512:(tkt + 1) * 512]), writes=["sint"], dma="dl")
        projs = [("wkT", "KF", NKT, tkt)]
        if tkt >= NKT - NQT:
            projs.append(("wqT", "QF", NQT, tkt - (NKT - NQT)))
        for wsrc, dst, nblk, bidx in projs:
            def kq_mm(h):
                wt = "wkt" if h % 2 == 0 else "wku"
                kb.op("y", lambda y, wsrc=wsrc, h=h, wt=wt: y.dma_start(
                    out=sb[wt][:].rearrange("p (c j) -> p c j", j=128),
                    in_=bass.AP(dram[wsrc].tensor, h * 128,
                                [[2048, 128], [128 * 2048, NC_D], [1, 128]])),
                    writes=[wt], dma="dl")
                pk = "p0" if h % 2 == 0 else "p1"
                for c in range(NC_D):
                    kb.op("t", lambda t, c=c, wt=wt, pk=pk: t.matmul(sb[pk][:, :512],
                          sb[wt][:, c * 128:(c + 1) * 128],
                          sb["big0"][:, c * 512:(c + 1) * 512],
                          start=(c == 0), stop=(c == NC_D - 1)),
                          reads=[wt, xnb[c]], writes=[pk])
            def rope_tail(h):
                pk = "p0" if h % 2 == 0 else "p1"
                kp = "kpre" if h % 2 == 0 else "kpr2"
                kb.op("s", lambda s, pk=pk, kp=kp: s.copy(out=sb[kp][:], in_=sb[pk][:, :512]),
                      reads=[pk], writes=[kp])
                kb.op("t", lambda t, kp=kp: t.matmul(sb["p2"][:, :512], sb["ropeRT"][:],
                      sb[kp][:], start=True, stop=True),
                      reads=["ropeRT", kp], writes=["p2"])
                kb.op("v", lambda v, kp=kp: v.tensor_tensor(out=sb["kc"][:], in0=sb[kp][:],
                      in1=sb["cost"][:], op=OP.mult),
                      reads=[kp, "cost"], writes=["kc"])
                kb.op("v", lambda v: v.tensor_tensor(out=sb["rs"][:],
                      in0=sb["p2"][:, :512], in1=sb["sint"][:], op=OP.mult),
                      reads=["p2", "sint"], writes=["rs"])
                kb.op("v", lambda v: v.tensor_tensor(out=sb["kout"][:], in0=sb["kc"][:],
                      in1=sb["rs"][:], op=OP.add),
                      reads=["kc", "rs"], writes=["kout"])
                kb.op("g", lambda g, dst=dst, h=h, nblk=nblk, bidx=bidx: g.dma_start(
                    out=dram[dst][(h * nblk + bidx) * 128:(h * nblk + bidx + 1) * 128, :],
                    in_=sb["kout"][:]), reads=["kout"], dma="ds")
            for h in range(H):
                kq_mm(h)
                if h >= 1:
                    rope_tail(h - 1)
                yield
            rope_tail(H - 1)
        for ot4 in range(4):
            for tp in range(2):
                for c in range(NC_D):
                    wb = "wv0" if c % 2 == 0 else "wv1"
                    kb.op("y", lambda y, c=c, ot4=ot4, wb=wb: y.dma_start(
                        out=sb[wb][:], in_=dram["wvT"][c * 128:(c + 1) * 128,
                                                       ot4 * 512:(ot4 + 1) * 512]),
                        writes=[wb], dma="dl")
                    for half in range(2):
                        ts4 = tp * 2 + half
                        pn = f"p{2 + half}"
                        kb.op("t", lambda t, c=c, ts4=ts4, pn=pn, wb=wb: t.matmul(
                              sb[pn][:, :512],
                              sb["big0"][:, c * 512 + ts4 * 128:c * 512 + ts4 * 128 + 128],
                              sb[wb][:], start=(c == 0), stop=(c == NC_D - 1)),
                              reads=[wb, xnb[c]], writes=[pn])
                for half in range(2):
                    ts4 = tp * 2 + half
                    pn = f"p{2 + half}"
                    kb.op("s", lambda s, pn=pn: s.copy(out=sb["kout"][:], in_=sb[pn][:, :512]),
                          reads=[pn], writes=["kout"])
                    kb.op("g", lambda g, tkt=tkt, ts4=ts4, ot4=ot4: g.dma_start(
                        out=bass.AP(dram["VT"].tensor,
                                    ((tkt * 4 + ts4) * H + ot4 * 4) * 16384,
                                    [[128, 128], [16384, 4], [1, 128]]),
                        in_=sb["kout"][:].rearrange("p (a j) -> p a j", j=128)),
                        reads=["kout"], dma="ds")
                yield


def emit_phaseC(kb, sb, dram):
    for h in range(H):
        for blk in range(NKT):
            kb.op("y", lambda y, h=h, blk=blk: y.dma_start(
                out=sb["big0"][:, blk * 512:(blk + 1) * 512],
                in_=dram["KF"][(h * NKT + blk) * 128:(h * NKT + blk + 1) * 128, :]),
                writes=[f"ksb{blk}"], dma="dl")
            kb.op("y", lambda y, h=h, blk=blk: y.dma_start(
                out=sb["big0"][:, 4096 + blk * 512:4096 + (blk + 1) * 512].rearrange(
                    "p (a j) -> p a j", j=128),
                in_=bass.AP(dram["VT"].tensor, (blk * 4 * H + h) * 16384,
                            [[128, 128], [H * 16384, 4], [1, 128]]),
                ), writes=[f"vsb{blk}"], dma="dl")
        for qt in range(NQT):
            kb.op("y", lambda y, h=h, qt=qt: y.dma_start(out=sb["qsb"][:],
                  in_=dram["QF"][(h * NQT + qt) * 128:(h * NQT + qt + 1) * 128, :]),
                  writes=["qsb"], dma="dl")
            nk = (T_KEYS - R_OWN) // 128 + 4 * (qt + 1)
            def s_stage(j):
                ps = "p0" if j % 2 == 0 else "p1"
                kb.op("t", lambda t, j=j, ps=ps: t.matmul(sb[ps][:, :512],
                      sb["big0"][:, j * 128:(j + 1) * 128], sb["qsb"][:],
                      start=True, stop=True),
                      reads=[f"ksb{j // 4}", "qsb"], writes=[ps])
            def e_stage(j):
                ps = "p0" if j % 2 == 0 else "p1"
                eb = "esb" if j % 2 == 0 else "esc"
                diag = j >= nk - 4
                if diag:
                    bias_ap, bias_buf = sb["cvec"][:, 0:1], "cvec"
                else:
                    bias_ap, bias_buf = sb["phb"][:, j:j + 1], "phb"
                kb.op("s", lambda s, ps=ps, eb=eb, bias_ap=bias_ap: s.activation(
                      out=sb[eb][:], in_=sb[ps][:, :512], func=ACT.Exp,
                      bias=bias_ap, scale=sb["scv"][:]),
                      reads=[ps, bias_buf, "scv"], writes=[eb])
                if diag:
                    off = j - (nk - 4)
                    kb.op("v", lambda v, off=off, eb=eb: v.tensor_tensor(out=sb[eb][:],
                          in0=sb[eb][:],
                          in1=sb["trimask"][:, off * 512:(off + 1) * 512], op=OP.mult),
                          reads=[eb, "trimask"], writes=[eb])
            def pv_stage(j):
                eb = "esb" if j % 2 == 0 else "esc"
                kb.op("t", lambda t, j=j, eb=eb: t.matmul(sb["p2"][:, :512],
                      sb["big0"][:, 4096 + j * 128:4096 + (j + 1) * 128], sb[eb][:],
                      start=(j == 0), stop=(j == nk - 1)),
                      reads=[f"vsb{j // 4}", eb], writes=["p2"])
                kb.op("t", lambda t, j=j, eb=eb: t.matmul(sb["p3"][0:1, :512],
                      sb["ones"][:, 0:1], sb[eb][:],
                      start=(j == 0), stop=(j == nk - 1)),
                      reads=["ones", eb], writes=["p3"])
            for j in range(nk):
                s_stage(j)
                e_stage(j)
                if j >= 1:
                    pv_stage(j - 1)
            pv_stage(nk - 1)
            for half in range(2):
                kb.op("v", lambda v, half=half: v.reciprocal(
                      out=sb["rsum"][0:1, half * 256:(half + 1) * 256],
                      in_=sb["p3"][0:1, half * 256:(half + 1) * 256]),
                      reads=["p3"], writes=["rsum"])
            kb.op("t", lambda t: t.matmul(sb["p0"][:, :512], sb["ones"][0:1, :],
                  sb["rsum"][0:1, :], start=True, stop=True),
                  reads=["ones", "rsum"], writes=["p0"])
            kb.op("s", lambda s: s.copy(out=sb["esb"][:], in_=sb["p0"][:, :512]),
                  reads=["p0"], writes=["esb"])
            kb.op("v", lambda v: v.tensor_tensor(out=sb["osb"][:], in0=sb["p2"][:, :512],
                  in1=sb["esb"][:], op=OP.mult),
                  reads=["p2", "esb"], writes=["osb"])
            kb.op("g", lambda g, h=h, qt=qt: g.dma_start(
                out=dram["OF"][(h * NQT + qt) * 128:(h * NQT + qt + 1) * 128, :],
                in_=sb["osb"][:]), reads=["osb"], dma="ds")


def _f_gu(kb, sb, dram, pr, qxb, qxb2):
    ot0, ot1 = 2 * pr, 2 * pr + 1
    kb.op("y", lambda y, ot0=ot0: y.dma_start(
        out=sb["qxf"][:].rearrange("p (c j) -> p c j", j=128),
        in_=bass.AP(dram["QXF"].tensor, ot0 * NC_D * 16384,
                    [[128, 128], [16384, NC_D], [1, 128]])),
        writes=qxb, dma="dl")
    kb.op("y", lambda y, ot1=ot1: y.dma_start(
        out=sb["qxf2"][:].rearrange("p (c j) -> p c j", j=128),
        in_=bass.AP(dram["QXF"].tensor, ot1 * NC_D * 16384,
                    [[128, 128], [16384, NC_D], [1, 128]])),
        writes=qxb2, dma="dl")
    for jt in range(NJT):
        kb.op("y", lambda y, jt=jt: y.dma_start(out=sb["wsgS"][:],
              in_=dram["WSG"][:, jt * 512:(jt + 1) * 512]), writes=["wsgS"], dma="dl")
        kb.op("y", lambda y, jt=jt: y.dma_start(out=sb["wsuS"][:],
              in_=dram["WSU"][:, jt * 512:(jt + 1) * 512]), writes=["wsuS"], dma="dl")
        kb.op("y", lambda y, jt=jt: y.dma_start(
            out=sb["qwgL"][:].rearrange("p (c j) -> p c j", j=512),
            in_=bass.AP(dram["QWG"].tensor, jt * 128 * 512,
                        [[512, 128], [NJT * 128 * 512, NC_D], [1, 512]])),
            writes=["qwgL"], dma="dl")
        kb.op("s", lambda s, jt=jt: s.dma_start(
            out=sb["qwuL"][:].rearrange("p (c j) -> p c j", j=512),
            in_=bass.AP(dram["QWU"].tensor, jt * 128 * 512,
                        [[512, 128], [NJT * 128 * 512, NC_D], [1, 512]])),
            writes=["qwuL"], dma="dl")
        for ti, (qx, qb, ot) in enumerate((("qxf", qxb, ot0), ("qxf2", qxb2, ot1))):
            pg = "p4" if ti == 0 else "p6"
            pu = "p5" if ti == 0 else "p7"
            for c in range(NC_D):
                kb.op("t", lambda t, c=c, pg=pg, qx=qx: t.matmul(sb[pg][:, :512],
                      sb[qx][:, c * 128:(c + 1) * 128],
                      sb["qwgL"][:, c * 512:(c + 1) * 512],
                      start=(c == 0), stop=(c == NC_D - 1)),
                      reads=[qb[c], "qwgL"], writes=[pg])
                kb.op("t", lambda t, c=c, pu=pu, qx=qx: t.matmul(sb[pu][:, :512],
                      sb[qx][:, c * 128:(c + 1) * 128],
                      sb["qwuL"][:, c * 512:(c + 1) * 512],
                      start=(c == 0), stop=(c == NC_D - 1)),
                      reads=[qb[c], "qwuL"], writes=[pu])
            kb.op("v", lambda v, ot=ot, pg=pg: v.scalar_tensor_tensor(out=sb["gsb"][:],
                  in0=sb[pg][:, :512], scalar=sb["sxdT"][:, ot:ot + 1],
                  in1=sb["wsgS"][:], op0=OP.mult, op1=OP.mult),
                  reads=[pg, "sxdT", "wsgS"], writes=["gsb"])
            kb.op("s", lambda s: s.activation(out=sb["kpre"][:], in_=sb["gsb"][:],
                  func=ACT.Silu), reads=["gsb"], writes=["kc2"])
            kb.op("v", lambda v, ot=ot, pu=pu: v.scalar_tensor_tensor(out=sb["usb"][:],
                  in0=sb[pu][:, :512], scalar=sb["sxdT"][:, ot:ot + 1],
                  in1=sb["wsuS"][:], op0=OP.mult, op1=OP.mult),
                  reads=[pu, "sxdT", "wsuS"], writes=["usb"])
            tb = "tb0" if ti == 0 else "tb1"
            kb.op("v", lambda v, jt=jt, tb=tb: v.tensor_tensor(
                  out=sb[tb][:, jt * 512:(jt + 1) * 512], in0=sb["kpre"][:],
                  in1=sb["usb"][:], op=OP.mult),
                  reads=["kc2", "usb"], writes=[tb])
        yield


def _f_snap(kb, sb, dram, ti, qtb_):
    tb = "tb0" if ti == 0 else "tb1"
    TB = sb[tb]
    S1 = sb["big0"][:, 0:2048]
    S2 = sb["big0"][:, 2048:4096]
    kb.op("v", lambda v, tb=tb: v.tensor_reduce(out=sb["stq"][:], in_=TB[:],
          axis=AX.X, op=OP.max, apply_absolute_value=True),
          reads=[tb], writes=["stq"])
    kb.op("v", lambda v: v.reciprocal(out=sb["stq"][:], in_=sb["stq"][:]),
          reads=["stq"], writes=["stq"])
    kb.op("v", lambda v: v.tensor_scalar(out=sb["stq"][:], in0=sb["stq"][:],
          scalar1=448.0, scalar2=None, op0=OP.mult),
          reads=["stq"], writes=["stq"])
    kb.op("v", lambda v: v.reciprocal(out=sb["stdq"][:], in_=sb["stq"][:]),
          reads=["stq"], writes=["stdq"])
    for sec in range(3):
        w0 = sec * 2048
        wd = min(2048, I_FF - w0)
        kb.op("v", lambda v, w0=w0, wd=wd, tb=tb: v.tensor_scalar(out=S1[:, :wd],
              in0=TB[:, w0:w0 + wd], scalar1=sb["stq"][:], scalar2=None, op0=OP.mult),
              reads=[tb, "stq"], writes=["fsn1"])
        snap_chunks(kb, sb, S1[:, :wd], "fsn1", S2[:, :wd], "fsn2", wd)
        for cl in range(wd // 128):
            c = sec * 16 + cl
            pn = "p6" if cl % 2 == 0 else "p7"
            kb.op("t", lambda t, cl=cl, pn=pn: t.transpose(sb[pn][:, :128],
                  S2[:, cl * 128:(cl + 1) * 128], sb["ident"][:]),
                  reads=["fsn2", "ident"], writes=[pn])
            kb.op("s", lambda s, c=c, pn=pn: s.copy(
                  out=sb["qtf"][:, c * 128:(c + 1) * 128], in_=sb[pn][:, :128]),
                  reads=[pn], writes=[qtb_[c]])
        yield


def _f_down(kb, sb, dram, ot, qtb_):
    for nt in range(4):
        kb.op("y", lambda y, nt=nt: y.dma_start(out=sb["wsdS"][:],
              in_=dram["WSD"][:, nt * 512:(nt + 1) * 512]), writes=["wsdS"], dma="dl")
        pd = "p2" if nt % 2 == 0 else "p3"
        for c0 in range(0, NC_I, 11):
            qe = "y" if (c0 // 11) % 2 == 0 else "s"
            dln = "qwgL" if (c0 // 11) % 2 == 0 else "qwuL"
            kb.op(qe, lambda e, c0=c0, nt=nt, dln=dln: e.dma_start(
                out=sb[dln][:, 0:5632].rearrange("p (c j) -> p c j", j=512),
                in_=bass.AP(dram["QWD"].tensor, (c0 * 4 + nt) * 128 * 512,
                            [[512, 128], [4 * 128 * 512, 11], [1, 512]])),
                writes=[dln], dma="dl")
            for c in range(c0, c0 + 11):
                kb.op("t", lambda t, c=c, c0=c0, pd=pd, dln=dln: t.matmul(
                      sb[pd][:, :512], sb["qtf"][:, c * 128:(c + 1) * 128],
                      sb[dln][:, (c - c0) * 512:(c - c0 + 1) * 512],
                      start=(c == 0), stop=(c == NC_I - 1)),
                      reads=[qtb_[c], dln], writes=[pd])
        kb.op("v", lambda v, nt=nt, pd=pd: v.scalar_tensor_tensor(out=sb["ysb"][:],
              in0=sb[pd][:, :512], scalar=sb["stdq2"][:],
              in1=sb["wsdS"][:], op0=OP.mult, op1=OP.mult),
              reads=[pd, "stdq2", "wsdS"], writes=["ysb"])
        kb.op("g", lambda g, ot=ot, nt=nt: g.dma_start(
            out=dram["FFNY"][(ot * 4 + nt) * 128:(ot * 4 + nt + 1) * 128, :],
            in_=sb["ysb"][:]), reads=["ysb"], dma="ds")
        yield


def emit_phaseF(kb, sb, dram):
    qxb = [f"qx{c}" for c in range(NC_D)]
    qxb2 = [f"qy{c}" for c in range(NC_D)]
    qtb_ = [f"qt{c}" for c in range(NC_I)]
    for pr in range(NT_OWN // 2):
        for _ in _f_gu(kb, sb, dram, pr, qxb, qxb2):
            pass
        for ti in range(2):
            for _ in _f_snap(kb, sb, dram, ti, qtb_):
                pass
            kb.op("v", lambda v: v.tensor_copy(out=sb["stdq2"][:], in_=sb["stdq"][:]),
                  reads=["stdq"], writes=["stdq2"])
            for _ in _f_down(kb, sb, dram, 2 * pr + ti, qtb_):
                pass


def emit_phaseD(kb, sb, dram):
    ofb = [f"of{h}" for h in range(H)]
    for qt in range(NQT):
        for h in range(H):
            kb.op("y", lambda y, h=h, qt=qt: y.dma_start(
                out=sb["big0"][:, h * 512:(h + 1) * 512],
                in_=dram["OF"][(h * NQT + qt) * 128:(h * NQT + qt + 1) * 128, :]),
                writes=[ofb[h]], dma="dl")
        for nt in range(4):
            for h in range(H):
                wb = "wv0" if h % 2 == 0 else "wv1"
                kb.op("y", lambda y, h=h, nt=nt, wb=wb: y.dma_start(
                    out=sb[wb][:], in_=dram["woT"][h * 128:(h + 1) * 128,
                                                   nt * 512:(nt + 1) * 512]),
                    writes=[wb], dma="dl")
                for ts4 in range(4):
                    pn = f"p{2 + ts4}"
                    kb.op("t", lambda t, h=h, ts4=ts4, pn=pn, wb=wb: t.matmul(
                          sb[pn][:, :512],
                          sb["big0"][:, h * 512 + ts4 * 128:h * 512 + ts4 * 128 + 128],
                          sb[wb][:], start=(h == 0), stop=(h == H - 1)),
                          reads=[ofb[h], wb], writes=[pn])
            for ts4 in range(4):
                pn = f"p{2 + ts4}"
                xb = "gsb" if ts4 % 2 == 0 else "usb"
                fb = "kc" if ts4 % 2 == 0 else "rs"
                trow = OWN0 + qt * 512 + ts4 * 128
                kb.op("y", lambda y, trow=trow, nt=nt, xb=xb: y.dma_start(
                    out=sb[xb][:], in_=dram["xk"][trow:trow + 128,
                                                  nt * 512:(nt + 1) * 512]),
                    writes=[xb], dma="dl")
                kb.op("y", lambda y, qt=qt, ts4=ts4, nt=nt, fb=fb: y.dma_start(
                    out=sb[fb][:],
                    in_=dram["FFNY"][((qt * 4 + ts4) * 4 + nt) * 128:
                                     ((qt * 4 + ts4) * 4 + nt + 1) * 128, :]),
                    writes=[fb], dma="dl")
                kb.op("v", lambda v, pn=pn, xb=xb: v.tensor_tensor(
                      out=sb["osb"][:], in0=sb[pn][:, :512], in1=sb[xb][:], op=OP.add),
                      reads=[pn, xb], writes=["osb"])
                kb.op("v", lambda v, fb=fb: v.tensor_tensor(out=sb["osb"][:],
                      in0=sb["osb"][:], in1=sb[fb][:], op=OP.add),
                      reads=["osb", fb], writes=["osb"])
                kb.op("g", lambda g, qt=qt, ts4=ts4, nt=nt: g.dma_start(
                    out=dram["y"][qt * 512 + ts4 * 128:qt * 512 + ts4 * 128 + 128,
                                  nt * 512:(nt + 1) * 512],
                    in_=sb["osb"][:]), reads=["osb"], dma="ds")


def _run(gen):
    for _ in gen:
        pass


def _interleave(gens, ratios):
    gens = [iter(g) for g in gens]
    done = [False] * len(gens)
    while not all(done):
        for i, g in enumerate(gens):
            if done[i]:
                continue
            for _ in range(ratios[i]):
                try:
                    next(g)
                except StopIteration:
                    done[i] = True
                    break


def build_program(debug_out=(), phases="AWBCFD"):
    import contextlib
    nc = bass.Bass("TRN2", target_bir_lowering=False, debug=False)
    dram = declare_dram(nc, debug_out)
    es = contextlib.ExitStack()
    sb = alloc_all(nc, es)
    with nc.Block():
        kb = KB(nc)
        emit_consts(kb, sb, dram)
        if "A" in phases and "W" in phases:
            _interleave([emit_phaseA(kb, sb, dram), emit_phaseW_nat(kb, sb, dram)],
                        [1, 3])
            kb.barrier()
            emit_phaseW_mid(kb, sb, dram)
            kb.barrier()
        elif "A" in phases:
            _run(emit_phaseA(kb, sb, dram))
            kb.barrier()
        elif "W" in phases:
            _run(emit_phaseW_nat(kb, sb, dram))
            kb.barrier()
            emit_phaseW_mid(kb, sb, dram)
            kb.barrier()
        if "B" in phases and "W" in phases:
            _interleave([emit_phaseB(kb, sb, dram), emit_phaseW_quant(kb, sb, dram)],
                        [1, 2])
            kb.barrier()
        elif "B" in phases:
            _run(emit_phaseB(kb, sb, dram))
            kb.barrier()
        elif "W" in phases:
            _run(emit_phaseW_quant(kb, sb, dram))
            kb.barrier()
        if "F" in phases:
            emit_phaseF(kb, sb, dram)
            kb.barrier()
        if "C" in phases:
            emit_phaseC(kb, sb, dram)
            kb.barrier()
        if "D" in phases:
            emit_phaseD(kb, sb, dram)
        for k in range(kb.NDMA):
            nm = f"ds{k}"
            if kb.cnt[nm]:
                kb.eng["g"].wait_ge(kb.sem[nm], kb.cnt[nm])
        n_instr = sum(kb.cnt[e] for e in ("t", "v", "s", "g", "y"))
        kb.close_sems()
    es.close()
    return nc, n_instr


"""Host-side input preparation (pure layout + shape-only constants)."""


def rope_tables():
    import jax
    import jax.numpy as jnp
    cpu = jax.devices('cpu')[0]
    with jax.default_device(cpu):
        half, T = 64, 4096
        inv_freq = 1.0 / 10000.0 ** (jnp.arange(half, dtype=jnp.float32) / half)
        t = jnp.arange(T, dtype=jnp.float32)
        freqs = jnp.einsum('t,f->tf', t, inv_freq)
        cos = np.asarray(jnp.cos(freqs))
        sin = np.asarray(jnp.sin(freqs))
    cosfm = np.repeat(cos, 2, axis=1).T.copy()
    sinfm = np.repeat(sin, 2, axis=1).T.copy()
    return cosfm.astype(np.float32), sinfm.astype(np.float32)


EBIAS = -20.0


def make_consts():
    ones = np.ones((128, 128), np.float32)
    ident = np.eye(128, dtype=np.float32)
    R = np.zeros((128, 128), np.float32)
    for i in range(64):
        R[2 * i, 2 * i + 1] = -1.0
        R[2 * i + 1, 2 * i] = 1.0
    ropeRT = R.T.copy()
    import ml_dtypes
    tri = np.zeros((128, 2048), np.float32)
    for o in range(4):
        tk = np.arange(128)[:, None]
        tq = np.arange(512)[None, :]
        tri[:, o * 512:(o + 1) * 512] = (tq >= tk + 128 * o).astype(np.float32)
    cvec = np.zeros((128, 8), np.float32)
    c = 1.0 / math.sqrt(128.0)
    a = np.float32(c)
    b = np.float32(c - np.float64(a))
    cvec[:, 0] = EBIAS
    cvec[:, 1] = a
    cvec[:, 2] = b
    cvec[:, 4] = 1e-5
    cvec[:, 5] = 1.0
    return ones, ident, ropeRT, tri.astype(ml_dtypes.bfloat16), cvec


def prep_inputs(ins):
    x = np.ascontiguousarray(ins["L0_attn_x"], dtype=np.float32)
    down_idx = np.asarray(ins["L0_ffn_down_reorder_index"])
    wq = np.ascontiguousarray(ins["L0_attn_Wq"].T)
    wk = np.ascontiguousarray(ins["L0_attn_Wk"].T)
    wv = np.ascontiguousarray(ins["L0_attn_Wv"].T)
    wo = np.ascontiguousarray(ins["L0_attn_Wo"].T)
    wgN = np.ascontiguousarray(ins["L0_ffn_w_gate"][down_idx])
    wgT = np.ascontiguousarray(wgN.T)
    wuN = np.ascontiguousarray(ins["L0_ffn_w_up"][down_idx])
    wuT = np.ascontiguousarray(wuN.T)
    wdN = np.ascontiguousarray(ins["L0_ffn_w_down"])
    wdT = np.ascontiguousarray(wdN.T)
    cosfm, sinfm = rope_tables()
    ones, ident, ropeRT, tri, cvec = make_consts()
    cosA = np.concatenate([cosfm[:, :2048], cosfm[:, :2048]], axis=1).copy()
    sinA = np.concatenate([sinfm[:, :2048], sinfm[:, :2048]], axis=1).copy()
    phA = np.full((128, 32), EBIAS, np.float32); phA[:, :16] = -1e30
    phB = np.full((128, 32), EBIAS, np.float32)
    maps = []
    for core in range(8):
        b, half = core // 2, core % 2
        if half == 0:
            xk = np.zeros((4096, 2048), np.float32)
            xk[2048:] = x[b, :2048]
            cosc, sinc, phc = cosA, sinA, phA
        else:
            xk = np.ascontiguousarray(x[b])
            cosc, sinc, phc = cosfm, sinfm, phB
        maps.append(dict(
            xk=xk, wqT=wq, wkT=wk, wvT=wv, woT=wo,
            wgN=wgN, wgT=wgT, wuN=wuN, wuT=wuT, wdN=wdN, wdT=wdT,
            cosf=cosc, sinf=sinc,
            conA=ones, conI=ident, conR=ropeRT, conT=tri, conP=phc, conC=cvec,
        ))
    return maps


def gather_out(results):
    out = np.empty((4, 4096, 2048), np.float32)
    for core in range(8):
        b, half = core // 2, core % 2
        out[b, half * 2048:(half + 1) * 2048] = results[core]["y"]
    return out


_CACHED = {}

def _get_program():
    if "nc" not in _CACHED:
        _CACHED["nc"] = build_program(debug_out=(), phases="AWBCFD")[0]
    return _CACHED["nc"]


def kernel(**inputs):
    from concourse.bass_utils import run_bass_kernel_spmd
    maps = prep_inputs(inputs)
    nc = _get_program()
    res = run_bass_kernel_spmd(nc, maps, core_ids=list(range(8)))
    return gather_out(res.results)


# version 10 (final)
# speedup vs baseline: 1.3853x; 1.0533x over previous
import math
import numpy as np
import concourse.bass as bass
import concourse.mybir as mybir

F32 = mybir.dt.float32
BF16 = mybir.dt.bfloat16
I32 = mybir.dt.int32
OP = mybir.AluOpType
ACT = mybir.ActivationFunctionType
AX = mybir.AxisListType

T_KEYS = 4096
R_OWN = 2048
D = 2048
H = 16
HD = 128
I_FF = 5632
NT_KEY = T_KEYS // 128
NT_OWN = R_OWN // 128
NC_D = D // 128
NC_I = I_FF // 128
NJT = I_FF // 512
OWN0 = R_OWN


class Buf:
    __slots__ = ("name", "wr", "rd")
    def __init__(self, name):
        self.name = name
        self.wr = None
        self.rd = []


class KB:
    def __init__(self, nc):
        self.nc = nc
        self.eng = {
            "t": nc.tensor, "v": nc.vector, "s": nc.scalar,
            "g": nc.gpsimd, "y": nc.sync,
        }
        self.sem = {}
        self.cnt = {}
        self.wm = {}
        self._ctx = []
        for e in ("t", "v", "s", "g", "y"):
            cm = nc.semaphore(f"c_{e}")
            h = cm.__enter__()
            self._ctx.append(cm)
            self.sem[e] = h
            self.cnt[e] = 0
        self.NDMA = 8
        for d in ("dl", "ds"):
            for k in range(self.NDMA):
                cm = nc.semaphore(f"c_{d}{k}")
                h = cm.__enter__()
                self._ctx.append(cm)
                self.sem[f"{d}{k}"] = h
                self.cnt[f"{d}{k}"] = 0
            self.sem[d + "_rr"] = 0
        for e in ("t", "v", "s", "g", "y"):
            self.wm[e] = {}
        self.bufs = {}

    def close_sems(self):
        for s in reversed(self._ctx):
            s.__exit__(None, None, None)

    def buf(self, name):
        if name not in self.bufs:
            self.bufs[name] = Buf(name)
        return self.bufs[name]

    def _wait(self, consumer, token):
        if token is None:
            return
        sem_name, val = token
        if sem_name.startswith("dl") or sem_name.startswith("ds"):
            val = self.cnt[sem_name]
        w = self.wm[consumer]
        if w.get(sem_name, 0) >= val:
            return
        self.eng[consumer].wait_ge(self.sem[sem_name], val)
        w[sem_name] = val

    def op(self, e, fn, reads=(), writes=(), dma=None):
        reads = [self.buf(b) if isinstance(b, str) else b for b in reads]
        writes = [self.buf(b) if isinstance(b, str) else b for b in writes]
        for b in reads:
            self._wait(e, b.wr)
        for b in writes:
            self._wait(e, b.wr)
            for t in b.rd:
                self._wait(e, t)
        if dma is not None:
            k = self.sem[dma + "_rr"]
            self.sem[dma + "_rr"] = (k + 1) % self.NDMA
            dname = f"{dma}{k}"
            self._wait(e, (dname, self.cnt[dname]))
            r = fn(self.eng[e])
            self.cnt[dname] += 16
            r.then_inc(self.sem[dname], 16)
            tok = (dname, self.cnt[dname])
        else:
            r = fn(self.eng[e])
            self.cnt[e] += 1
            r.then_inc(self.sem[e], 1)
            tok = (e, self.cnt[e])
        for b in reads:
            b.rd.append(tok)
        for b in writes:
            b.wr = tok
            b.rd = []
        return tok

    def barrier(self):
        names = ["t", "v", "s", "g", "y"] + [f"{d}{k}" for d in ("dl", "ds") for k in range(self.NDMA)]
        targets = [(s, self.cnt[s]) for s in names if self.cnt[s] > 0]
        for e in ("t", "v", "s", "g", "y"):
            for sname, val in targets:
                if sname == e:
                    continue
                self._wait(e, (sname, val))


def snap_tiesup(kb, src, dst_bf16, tmp, n, sem_hint=None):
    raise NotImplementedError


def emit_snap(kb, srcap, srcbuf, dstap, dstbuf, t, n):
    B = kb.buf
    kb.op("v", lambda v: v.tensor_scalar(out=t["pi"][:, :n], in0=srcap.bitcast(I32),
          scalar1=0x7F800000, scalar2=None, op0=OP.bitwise_and),
          reads=[srcbuf], writes=["snap_pi"])
    kb.op("v", lambda v: v.tensor_scalar(out=t["u"][:, :n], in0=t["pi"][:, :n].bitcast(F32),
          scalar1=0.125, scalar2=2.0 ** -9, op0=OP.mult, op1=OP.max),
          reads=["snap_pi"], writes=["snap_u"])
    kb.op("v", lambda v: v.tensor_tensor(out=t["pi"][:, :n], in0=t["ci"][:, :n],
          in1=t["u"][:, :n].bitcast(I32), op=OP.subtract),
          reads=["snap_ci", "snap_u"], writes=["snap_pi"])
    kb.op("v", lambda v: v.tensor_tensor(out=t["y"][:, :n], in0=srcap,
          in1=t["pi"][:, :n].bitcast(F32), op=OP.mult),
          reads=[srcbuf, "snap_pi"], writes=["snap_y"])
    kb.op("v", lambda v: v.tensor_copy(out=t["f0"][:, :n], in_=t["y"][:, :n]),
          reads=["snap_y"], writes=["snap_f0"])
    kb.op("v", lambda v: v.tensor_copy(out=t["f1"][:, :n], in_=t["f0"][:, :n]),
          reads=["snap_f0"], writes=["snap_f1"])
    kb.op("v", lambda v: v.scalar_tensor_tensor(out=t["c1"][:, :n], in0=t["f1"][:, :n],
          scalar=-0.5, in1=t["y"][:, :n], op0=OP.add, op1=OP.is_le),
          reads=["snap_f1", "snap_y"], writes=["snap_c1"])
    kb.op("v", lambda v: v.scalar_tensor_tensor(out=t["c2"][:, :n], in0=t["f1"][:, :n],
          scalar=0.5, in1=t["y"][:, :n], op0=OP.add, op1=OP.is_le),
          reads=["snap_f1", "snap_y"], writes=["snap_c2"])
    kb.op("v", lambda v: v.tensor_tensor(out=t["c1"][:, :n], in0=t["c1"][:, :n],
          in1=t["c2"][:, :n], op=OP.add),
          reads=["snap_c1", "snap_c2"], writes=["snap_c1"])
    kb.op("v", lambda v: v.scalar_tensor_tensor(out=t["y"][:, :n], in0=t["c1"][:, :n],
          scalar=-1.0, in1=t["f1"][:, :n], op0=OP.add, op1=OP.add),
          reads=["snap_c1", "snap_f1"], writes=["snap_y"])
    kb.op("v", lambda v: v.tensor_tensor(out=dstap, in0=t["y"][:, :n],
          in1=t["u"][:, :n], op=OP.mult),
          reads=["snap_y", "snap_u"], writes=[dstbuf])


def emit_longdiv(kb, mx_ap, out_ap, t, n):
    B = "ld_"
    kb.op("v", lambda v: v.tensor_scalar(out=t["mi"][:, :n], in0=mx_ap.bitcast(I32),
          scalar1=0x7FFFFF, scalar2=0x800000, op0=OP.bitwise_and, op1=OP.bitwise_or),
          reads=["ld_mx"], writes=[B + "mi"])
    kb.op("v", lambda v: v.tensor_copy(out=t["mf"][:, :n], in_=t["mi"][:, :n]),
          reads=[B + "mi"], writes=[B + "mf"])
    kb.op("v", lambda v: v.tensor_scalar(out=t["mf"][:, :n], in0=t["mf"][:, :n],
          scalar1=2.0 ** -24, scalar2=None, op0=OP.mult),
          reads=[B + "mf"], writes=[B + "mf"])
    kb.op("v", lambda v: v.tensor_scalar(out=t["eb"][:, :n], in0=mx_ap.bitcast(I32),
          scalar1=23, scalar2=0xFF, op0=OP.logical_shift_right, op1=OP.bitwise_and),
          reads=["ld_mx"], writes=[B + "eb"])
    kb.op("v", lambda v: v.tensor_scalar(out=t["n2"][:, :n], in0=t["mf"][:, :n],
          scalar1=0.875, scalar2=None, op0=OP.is_gt),
          reads=[B + "mf"], writes=[B + "n2"])
    kb.op("v", lambda v: v.tensor_scalar(out=t["r"][:, :n], in0=t["n2"][:, :n],
          scalar1=0.4375, scalar2=0.4375, op0=OP.mult, op1=OP.add),
          reads=[B + "n2"], writes=[B + "r"])
    for qn in ("q0", "q1", "q2"):
        kb.op("v", lambda v, qn=qn: v.memset(t[qn][:, :n], 0.0),
              reads=[], writes=[B + qn])
    for i in range(1, 55):
        qn = "q0" if i <= 18 else ("q1" if i <= 36 else "q2")
        kb.op("v", lambda v: v.tensor_scalar(out=t["r"][:, :n], in0=t["r"][:, :n],
              scalar1=2.0, scalar2=None, op0=OP.mult),
              reads=[B + "r"], writes=[B + "r"])
        kb.op("v", lambda v: v.tensor_tensor(out=t["d"][:, :n], in0=t["r"][:, :n],
              in1=t["mf"][:, :n], op=OP.subtract),
              reads=[B + "r", B + "mf"], writes=[B + "d"])
        kb.op("v", lambda v: v.tensor_scalar(out=t["c"][:, :n], in0=t["d"][:, :n],
              scalar1=0.0, scalar2=None, op0=OP.is_ge),
              reads=[B + "d"], writes=[B + "c"])
        kb.op("v", lambda v: v.copy_predicated(t["r"][:, :n], t["c"][:, :n], t["d"][:, :n]),
              reads=[B + "c", B + "d", B + "r"], writes=[B + "r"])
        kb.op("v", lambda v, qn=qn: v.scalar_tensor_tensor(out=t[qn][:, :n],
              in0=t[qn][:, :n], scalar=2.0, in1=t["c"][:, :n], op0=OP.mult, op1=OP.add),
              reads=[B + qn, B + "c"], writes=[B + qn])
        if i == 53:
            kb.op("v", lambda v: v.tensor_copy(out=t["l"][:, :n], in_=t["c"][:, :n]),
                  reads=[B + "c"], writes=[B + "l"])
        if i == 54:
            kb.op("v", lambda v: v.tensor_copy(out=t["g"][:, :n], in_=t["c"][:, :n]),
                  reads=[B + "c"], writes=[B + "g"])
    kb.op("v", lambda v: v.tensor_scalar(out=t["st"][:, :n], in0=t["r"][:, :n],
          scalar1=0.0, scalar2=None, op0=OP.is_gt),
          reads=[B + "r"], writes=[B + "st"])
    kb.op("v", lambda v: v.tensor_tensor(out=t["st"][:, :n], in0=t["st"][:, :n],
          in1=t["l"][:, :n], op=OP.max),
          reads=[B + "st", B + "l"], writes=[B + "st"])
    kb.op("v", lambda v: v.tensor_tensor(out=t["st"][:, :n], in0=t["st"][:, :n],
          in1=t["g"][:, :n], op=OP.mult),
          reads=[B + "st", B + "g"], writes=[B + "st"])
    kb.op("v", lambda v: v.tensor_tensor(out=t["q2"][:, :n], in0=t["q2"][:, :n],
          in1=t["g"][:, :n], op=OP.subtract),
          reads=[B + "q2", B + "g"], writes=[B + "q2"])
    kb.op("v", lambda v: v.scalar_tensor_tensor(out=t["q2"][:, :n], in0=t["st"][:, :n],
          scalar=2.0, in1=t["q2"][:, :n], op0=OP.mult, op1=OP.add),
          reads=[B + "st", B + "q2"], writes=[B + "q2"])
    kb.op("v", lambda v: v.tensor_scalar(out=t["acc"][:, :n], in0=t["q0"][:, :n],
          scalar1=2.0 ** -18, scalar2=None, op0=OP.mult),
          reads=[B + "q0"], writes=[B + "acc"])
    kb.op("v", lambda v: v.scalar_tensor_tensor(out=t["acc"][:, :n], in0=t["q1"][:, :n],
          scalar=2.0 ** -36, in1=t["acc"][:, :n], op0=OP.mult, op1=OP.add),
          reads=[B + "q1", B + "acc"], writes=[B + "acc"])
    kb.op("v", lambda v: v.scalar_tensor_tensor(out=t["acc"][:, :n], in0=t["q2"][:, :n],
          scalar=2.0 ** -54, in1=t["acc"][:, :n], op0=OP.mult, op1=OP.add),
          reads=[B + "q2", B + "acc"], writes=[B + "acc"])
    kb.op("v", lambda v: v.tensor_copy(out=t["n2i"][:, :n], in_=t["n2"][:, :n]),
          reads=[B + "n2"], writes=[B + "n2i"])
    kb.op("v", lambda v: v.tensor_tensor(out=t["fb"][:, :n], in0=t["c260"][:, :n],
          in1=t["eb"][:, :n], op=OP.subtract),
          reads=[B + "c260", B + "eb"], writes=[B + "fb"])
    kb.op("v", lambda v: v.tensor_tensor(out=t["fb"][:, :n], in0=t["fb"][:, :n],
          in1=t["n2i"][:, :n], op=OP.subtract),
          reads=[B + "fb", B + "n2i"], writes=[B + "fb"])
    kb.op("v", lambda v: v.tensor_scalar(out=t["fb"][:, :n], in0=t["fb"][:, :n],
          scalar1=23, scalar2=None, op0=OP.logical_shift_left),
          reads=[B + "fb"], writes=[B + "fb"])
    kb.op("v", lambda v: v.tensor_tensor(out=out_ap, in0=t["acc"][:, :n],
          in1=t["fb"][:, :n].bitcast(F32), op=OP.mult),
          reads=[B + "acc", B + "fb"], writes=["ld_out"])



"""Phase emitters + full program builder for the prb megakernel."""


SNW = 320
NQT = R_OWN // 512
NKT = T_KEYS // 512


def alloc_all(nc, es):
    sb = {}
    def S(name, shape, dt=F32):
        sb[name] = es.enter_context(nc.sbuf_tensor(name, shape, dt))
    def P(name, shape, dt=F32):
        sb[name] = es.enter_context(nc.psum_tensor(name, shape, dt))
    S("ones", [128, 128]); S("ident", [128, 128]); S("ropeRT", [128, 128])
    S("trimask", [128, 2048], BF16); S("phb", [128, 32]); S("cvec", [128, 8])
    S("scv", [128, 1])
    S("big0", [128, 8192])
    S("xad", [128, 2048])
    S("wqsb", [128, 512], BF16)
    sb["xa"] = sb["big0"]
    sb["strip"] = sb["big0"]
    sb["wnat"] = sb["big0"]
    sb["tbuf"] = sb["big0"]
    sb["xnf"] = sb["big0"]
    sb["ksb"] = sb["big0"]
    sb["vsb"] = sb["big0"]
    S("stripb", [128, 2048], BF16)
    S("tb0", [128, 5632], BF16); S("tb1", [128, 5632], BF16)
    S("qtf", [128, 5632], BF16)
    S("sn_u", [128, SNW]); S("sn_y", [128, SNW]); S("sn_f1", [128, SNW])
    S("sn_c1", [128, SNW]); S("sn_c2", [128, SNW])
    S("sn_pi", [128, SNW], I32); S("sn_f0", [128, SNW], I32)
    S("sn_ci", [128, SNW], I32)
    for k in ("mf", "r", "d", "q0", "q1", "q2", "g", "l", "st", "n2", "acc"):
        S("ld_" + k, [128, 104])
    for k in ("eb", "mi", "n2i", "fb", "c260", "c"):
        S("ld_" + k, [128, 104], I32)
    S("sQmax", [128, 104]); S("sQ", [128, 104]); S("sRec", [128, 104])
    S("sT", [104, 128]); S("wT", [104, 128])
    S("mean", [128, 1]); S("var", [128, 1]); S("rstd", [128, 1])
    S("rmax", [128, 1]); S("sxq", [128, 1]); S("sxdT", [128, 16])
    S("stq", [128, 1]); S("stdq", [128, 1]); S("stdq2", [128, 1])
    S("wkt", [128, 2048]); S("wku", [128, 2048]); S("wv0", [128, 512]); S("wv1", [128, 512])
    S("kpre", [128, 512]); S("kpr2", [128, 512]); S("kc", [128, 512]); S("rs", [128, 512])
    S("kout", [128, 512])
    S("cost", [128, 512]); S("sint", [128, 512])
    S("qsb", [128, 512]); S("esb", [128, 512]); S("esc", [128, 512])
    sb["wq512"] = sb["qsb"]; sb["sqseg"] = sb["esb"]
    S("osb", [128, 512]); S("rsum", [1, 512])
    S("qxf", [128, 2048], BF16); S("qxf2", [128, 2048], BF16)
    S("qwgL", [128, 8192], BF16); S("qwuL", [128, 8192], BF16)
    S("gsb", [128, 512]); S("usb", [128, 512])
    S("ysb", [128, 512])
    S("wsgS", [128, 512]); S("wsuS", [128, 512]); S("wsdS", [128, 512])
    for i in range(8):
        P(f"p{i}", [128, 512])
    return sb


def declare_dram(nc, debug_out=()):
    d = {}
    def I(name, shape, dt=F32):
        d[name] = nc.dram_tensor(name, shape, dt, kind="ExternalInput").ap()
    def X(name, shape, dt=F32):
        if name in debug_out:
            d[name] = nc.dram_tensor(name, shape, dt, kind="ExternalOutput").ap()
        else:
            d[name] = nc.dram_tensor(name, shape, dt).ap()
    I("xk", [T_KEYS, D])
    I("wqT", [D, D]); I("wkT", [D, D]); I("wvT", [D, D]); I("woT", [D, D])
    I("wgN", [I_FF, D]); I("wgT", [D, I_FF])
    I("wuN", [I_FF, D]); I("wuT", [D, I_FF])
    I("wdN", [D, I_FF]); I("wdT", [I_FF, D])
    I("cosf", [128, T_KEYS]); I("sinf", [128, T_KEYS])
    I("conA", [128, 128]); I("conI", [128, 128]); I("conR", [128, 128])
    d["conT"] = nc.dram_tensor("conT", [128, 2048], BF16, kind="ExternalInput").ap()
    I("conP", [128, 32]); I("conC", [128, 8])
    d["y"] = nc.dram_tensor("y", [R_OWN, D], F32, kind="ExternalOutput").ap()
    X("XNF", [NT_KEY * NC_D * 128, 128])
    X("QXF", [NT_OWN * NC_D * 128, 128], BF16)
    X("KF", [H * NKT * 128, 512])
    X("VT", [NT_KEY * H * 128, 128])
    X("QF", [H * NQT * 128, 512])
    X("OF", [H * NQT * 128, 512])
    X("QWG", [NC_D * NJT * 128, 512], BF16)
    X("QWU", [NC_D * NJT * 128, 512], BF16)
    X("QWD", [NC_I * 4 * 128, 512], BF16)
    X("FFNY", [NT_OWN * 4 * 128, 512])
    X("WSQG", [128, I_FF])
    X("WSQU", [128, I_FF])
    X("WSQD", [128, D])
    X("WSG", [128, I_FF])
    X("WSU", [128, I_FF])
    X("WSD", [128, D])
    return d


def emit_consts(kb, sb, dram):
    for nm, dst in (("conA", "ones"), ("conI", "ident"), ("conR", "ropeRT"),
                    ("conT", "trimask"), ("conP", "phb"), ("conC", "cvec")):
        kb.op("y", lambda y, nm=nm, dst=dst: y.dma_start(out=sb[dst][:], in_=dram[nm][:]),
              writes=[dst], dma="dl")
    kb.op("v", lambda v: v.tensor_tensor(out=sb["scv"][:], in0=sb["cvec"][:, 1:2],
          in1=sb["cvec"][:, 2:3], op=OP.add), reads=["cvec"], writes=["scv"])
    kb.op("v", lambda v: v.memset(sb["ld_c260"][:], 263), writes=["ld_c260"])
    kb.op("v", lambda v: v.memset(sb["sn_ci"][:], 254 << 23), writes=["snap_ci"])


def snap_chunks(kb, sb, srcap, srcbuf, dstap, dstbuf, width):
    t = {k: sb["sn_" + k] for k in ("pi", "u", "y", "f0", "f1", "c1", "c2")}
    t["ci"] = sb["sn_ci"]
    off = 0
    while off < width:
        n = min(SNW, width - off)
        emit_snap(kb, srcap[:, off:off + n], srcbuf, dstap[:, off:off + n],
                  dstbuf, t, n)
        off += n


def emit_phaseA(kb, sb, dram):
    XA = sb["xad"][:]
    ST = sb["big0"][:, 6144:8192]
    for tt in range(NT_KEY):
        kb.op("y", lambda y, tt=tt: y.dma_start(out=XA,
              in_=dram["xk"][tt * 128:(tt + 1) * 128, :]), writes=["xa"], dma="dl")
        kb.op("v", lambda v: v.tensor_reduce(out=sb["mean"][:], in_=XA,
              axis=AX.X, op=OP.add), reads=["xa"], writes=["mean"])
        kb.op("v", lambda v: v.tensor_scalar(out=sb["mean"][:], in0=sb["mean"][:],
              scalar1=1.0 / 2048, scalar2=None, op0=OP.mult),
              reads=["mean"], writes=["mean"])
        kb.op("v", lambda v: v.tensor_scalar(out=XA, in0=XA,
              scalar1=sb["mean"][:], scalar2=None, op0=OP.subtract),
              reads=["xa", "mean"], writes=["xa"])
        kb.op("v", lambda v: v.tensor_tensor(out=ST, in0=XA, in1=XA, op=OP.mult),
              reads=["xa"], writes=["strip"])
        kb.op("v", lambda v: v.tensor_reduce(out=sb["var"][:], in_=ST,
              axis=AX.X, op=OP.add), reads=["strip"], writes=["var"])
        kb.op("v", lambda v: v.tensor_scalar(out=sb["var"][:], in0=sb["var"][:],
              scalar1=1.0 / 2048, scalar2=None, op0=OP.mult),
              reads=["var"], writes=["var"])
        kb.op("s", lambda s: s.activation(out=sb["rstd"][:], in_=sb["var"][:],
              func=ACT.Sqrt, bias=sb["cvec"][:, 4:5], scale=sb["cvec"][:, 5:6]),
              reads=["var", "cvec"], writes=["rstd"])
        kb.op("v", lambda v: v.reciprocal(out=sb["rstd"][:], in_=sb["rstd"][:]),
              reads=["rstd"], writes=["rstd"])
        kb.op("v", lambda v: v.tensor_scalar(out=XA, in0=XA,
              scalar1=sb["rstd"][:], scalar2=None, op0=OP.mult),
              reads=["xa", "rstd"], writes=["xa"])
        for c in range(NC_D):
            pn = "p0" if c % 2 == 0 else "p1"
            kb.op("t", lambda t, c=c, pn=pn: t.transpose(sb[pn][:, :128],
                  XA[:, c * 128:(c + 1) * 128], sb["ident"][:]),
                  reads=["xa", "ident"], writes=[pn])
            kb.op("s", lambda s, c=c, pn=pn: s.copy(out=ST[:, c * 128:(c + 1) * 128],
                  in_=sb[pn][:, :128]), reads=[pn], writes=["strip"])
        kb.op("g", lambda g, tt=tt: g.dma_start(
            out=bass.AP(dram["XNF"].tensor, tt * NC_D * 128 * 128,
                        [[128, 128], [16384, NC_D], [1, 128]]),
            in_=sb["big0"][:, 6144:8192].rearrange("p (c j) -> p c j", j=128)),
            reads=["strip"], dma="ds")
        if tt < NT_KEY - NT_OWN:
            yield
        if tt >= NT_KEY - NT_OWN:
            ot = tt - (NT_KEY - NT_OWN)
            kb.op("v", lambda v: v.tensor_reduce(out=sb["rmax"][:], in_=XA,
                  axis=AX.X, op=OP.max, apply_absolute_value=True),
                  reads=["xa"], writes=["rmax"])
            kb.op("v", lambda v: v.reciprocal(out=sb["sxq"][:], in_=sb["rmax"][:]),
                  reads=["rmax"], writes=["sxq"])
            kb.op("v", lambda v: v.tensor_scalar(out=sb["sxq"][:], in0=sb["sxq"][:],
                  scalar1=448.0, scalar2=None, op0=OP.mult),
                  reads=["sxq"], writes=["sxq"])
            kb.op("v", lambda v, ot=ot: v.reciprocal(
                  out=sb["sxdT"][:, ot:ot + 1], in_=sb["sxq"][:]),
                  reads=["sxq"], writes=["sxdT"])
            kb.op("v", lambda v: v.tensor_scalar(out=XA, in0=XA,
                  scalar1=sb["sxq"][:], scalar2=None, op0=OP.mult),
                  reads=["xa", "sxq"], writes=["xa"])
            snap_chunks(kb, sb, XA, "xa", ST, "strip", 2048)
            for c in range(NC_D):
                pn = "p0" if c % 2 == 0 else "p1"
                kb.op("t", lambda t, c=c, pn=pn: t.transpose(sb[pn][:, :128],
                      ST[:, c * 128:(c + 1) * 128], sb["ident"][:]),
                      reads=["strip", "ident"], writes=[pn])
                kb.op("s", lambda s, c=c, pn=pn: s.copy(
                      out=sb["stripb"][:, c * 128:(c + 1) * 128], in_=sb[pn][:, :128]),
                      reads=[pn], writes=["stripb"])
            kb.op("g", lambda g, ot=ot: g.dma_start(
                out=bass.AP(dram["QXF"].tensor, ot * NC_D * 128 * 128,
                            [[128, 128], [16384, NC_D], [1, 128]]),
                in_=sb["stripb"][:, 0:2048].rearrange("p (c j) -> p c j", j=128)),
                reads=["stripb"], dma="ds")


def emit_phaseW_nat(kb, sb, dram):
    WN = sb["big0"][:, 0:5632]
    for base, nat, ntiles, wid in ((0, "wgN", 44, 2048), (44, "wuN", 44, 2048),
                                   (88, "wdN", 16, 5632)):
        for c in range(ntiles):
            kb.op("y", lambda y, nat=nat, c=c, wid=wid: y.dma_start(
                out=WN[:, :wid], in_=dram[nat][c * 128:(c + 1) * 128, :]),
                writes=["wnat"], dma="dl")
            kb.op("v", lambda v, c=c, base=base, wid=wid: v.tensor_reduce(
                out=sb["sQmax"][:, base + c:base + c + 1], in_=WN[:, :wid],
                axis=AX.X, op=OP.max, apply_absolute_value=True),
                reads=["wnat"], writes=["sQmax"])
            yield


def emit_phaseW_mid(kb, sb, dram):
    ld = {k: sb["ld_" + k] for k in ("mf", "r", "c", "d", "q0", "q1", "q2", "g",
                                     "l", "st", "n2", "acc", "eb", "mi", "n2i",
                                     "fb", "c260")}
    kb.bufs["ld_mx"] = kb.buf("sQmax")
    emit_longdiv(kb, sb["sQmax"][:], sb["sQ"][:], ld, 104)
    kb.bufs["sQ"] = kb.buf("ld_out")
    kb.op("v", lambda v: v.reciprocal(out=sb["sRec"][:], in_=sb["sQ"][:]),
          reads=["sQ"], writes=["sRec"])
    kb.op("t", lambda t: t.transpose(sb["p2"][0:104, :128], sb["sQ"][:], sb["ident"][:]),
          reads=["sQ", "ident"], writes=["p2"])
    kb.op("s", lambda s: s.copy(out=sb["sT"][:], in_=sb["p2"][0:104, :128]),
          reads=["p2"], writes=["sT"])
    kb.op("t", lambda t: t.transpose(sb["p3"][0:104, :128], sb["sRec"][:], sb["ident"][:]),
          reads=["sRec", "ident"], writes=["p3"])
    kb.op("s", lambda s: s.copy(out=sb["wT"][:], in_=sb["p3"][0:104, :128]),
          reads=["p3"], writes=["wT"])
    SROW = sb["big0"][0:1, 0:5632]
    for srcT, scr3 in (("wT", ("WSG", "WSU", "WSD")), ("sT", ("WSQG", "WSQU", "WSQD"))):
        for scr, base, nseg in ((scr3[0], 0, 44), (scr3[1], 44, 44), (scr3[2], 88, 16)):
            for c in range(nseg):
                kb.op("y", lambda y, c=c, base=base, srcT=srcT: y.dma_start(
                    out=SROW[:, c * 128:(c + 1) * 128],
                    in_=sb[srcT][base + c:base + c + 1, :]),
                    reads=[srcT], writes=["srow"], dma="dl")
            for s4 in range(nseg // 4):
                pw = "p2" if s4 % 2 == 0 else "p3"
                kb.op("t", lambda t, s4=s4, pw=pw: t.matmul(sb[pw][:, :512],
                      sb["ones"][0:1, :], SROW[:, s4 * 512:(s4 + 1) * 512],
                      start=True, stop=True),
                      reads=["ones", "srow"], writes=[pw])
                kb.op("s", lambda s, pw=pw: s.copy(out=sb["ysb"][:], in_=sb[pw][:, :512]),
                      reads=[pw], writes=["ysb"])
                kb.op("g", lambda g, scr=scr, s4=s4: g.dma_start(
                    out=dram[scr][:, s4 * 512:(s4 + 1) * 512], in_=sb["ysb"][:]),
                    reads=["ysb"], dma="ds")


def emit_phaseW_quant(kb, sb, dram):
    for mat, src_, qscr, nfc, nblk, out in (
            ("g", "wgT", "WSQG", NC_D, NJT, "QWG"),
            ("u", "wuT", "WSQU", NC_D, NJT, "QWU"),
            ("d", "wdT", "WSQD", NC_I, 4, "QWD")):
        for jt in range(nblk):
            kb.op("y", lambda y, jt=jt, qscr=qscr: y.dma_start(
                out=sb["sqseg"][:, :512], in_=dram[qscr][:, jt * 512:(jt + 1) * 512]),
                writes=["sqseg"], dma="dl")
            for fc in range(nfc):
                kb.op("y", lambda y, src_=src_, fc=fc, jt=jt: y.dma_start(
                    out=sb["wq512"][:, :512], in_=dram[src_][fc * 128:(fc + 1) * 128,
                                                      jt * 512:(jt + 1) * 512]),
                    writes=["wq512"], dma="dl")
                kb.op("v", lambda v: v.tensor_tensor(out=sb["wq512"][:, :512],
                      in0=sb["wq512"][:, :512], in1=sb["sqseg"][:, :512], op=OP.mult),
                      reads=["wq512", "sqseg"], writes=["wq512"])
                snap_chunks(kb, sb, sb["wq512"][:, :512], "wq512",
                            sb["wqsb"][:], "wqsb", 512)
                kb.op("g", lambda g, out=out, fc=fc, jt=jt, nblk=nblk: g.dma_start(
                    out=dram[out][(fc * nblk + jt) * 128:(fc * nblk + jt + 1) * 128, :],
                    in_=sb["wqsb"][:]), reads=["wqsb"], dma="ds")
                yield


def emit_phaseB(kb, sb, dram):
    wkb = [f"wk{c}" for c in range(NC_D)]
    xnb = [f"xn{c}" for c in range(NC_D)]
    for tkt in range(NKT):
        tt0 = tkt * 4
        for c in range(NC_D):
            kb.op("y", lambda y, c=c, tt0=tt0: y.dma_start(
                out=sb["big0"][:, c * 512:(c + 1) * 512].rearrange("p (a j) -> p a j", j=128),
                in_=bass.AP(dram["XNF"].tensor, (tt0 * NC_D + c) * 16384,
                            [[128, 128], [NC_D * 16384, 4], [1, 128]])),
                writes=[xnb[c]], dma="dl")
        kb.op("y", lambda y, tkt=tkt: y.dma_start(out=sb["cost"][:],
              in_=dram["cosf"][:, tkt * 512:(tkt + 1) * 512]), writes=["cost"], dma="dl")
        kb.op("y", lambda y, tkt=tkt: y.dma_start(out=sb["sint"][:],
              in_=dram["sinf"][:, tkt * 512:(tkt + 1) * 512]), writes=["sint"], dma="dl")
        projs = [("wkT", "KF", NKT, tkt)]
        if tkt >= NKT - NQT:
            projs.append(("wqT", "QF", NQT, tkt - (NKT - NQT)))
        for wsrc, dst, nblk, bidx in projs:
            def kq_mm(h):
                wt = "wkt" if h % 2 == 0 else "wku"
                kb.op("y", lambda y, wsrc=wsrc, h=h, wt=wt: y.dma_start(
                    out=sb[wt][:].rearrange("p (c j) -> p c j", j=128),
                    in_=bass.AP(dram[wsrc].tensor, h * 128,
                                [[2048, 128], [128 * 2048, NC_D], [1, 128]])),
                    writes=[wt], dma="dl")
                pk = "p0" if h % 2 == 0 else "p1"
                for c in range(NC_D):
                    kb.op("t", lambda t, c=c, wt=wt, pk=pk: t.matmul(sb[pk][:, :512],
                          sb[wt][:, c * 128:(c + 1) * 128],
                          sb["big0"][:, c * 512:(c + 1) * 512],
                          start=(c == 0), stop=(c == NC_D - 1)),
                          reads=[wt, xnb[c]], writes=[pk])
            def rope_tail(h):
                pk = "p0" if h % 2 == 0 else "p1"
                kp = "kpre" if h % 2 == 0 else "kpr2"
                kb.op("s", lambda s, pk=pk, kp=kp: s.copy(out=sb[kp][:], in_=sb[pk][:, :512]),
                      reads=[pk], writes=[kp])
                kb.op("t", lambda t, kp=kp: t.matmul(sb["p2"][:, :512], sb["ropeRT"][:],
                      sb[kp][:], start=True, stop=True),
                      reads=["ropeRT", kp], writes=["p2"])
                kb.op("v", lambda v, kp=kp: v.tensor_tensor(out=sb["kc"][:], in0=sb[kp][:],
                      in1=sb["cost"][:], op=OP.mult),
                      reads=[kp, "cost"], writes=["kc"])
                kb.op("v", lambda v: v.tensor_tensor(out=sb["rs"][:],
                      in0=sb["p2"][:, :512], in1=sb["sint"][:], op=OP.mult),
                      reads=["p2", "sint"], writes=["rs"])
                kb.op("v", lambda v: v.tensor_tensor(out=sb["kout"][:], in0=sb["kc"][:],
                      in1=sb["rs"][:], op=OP.add),
                      reads=["kc", "rs"], writes=["kout"])
                kb.op("g", lambda g, dst=dst, h=h, nblk=nblk, bidx=bidx: g.dma_start(
                    out=dram[dst][(h * nblk + bidx) * 128:(h * nblk + bidx + 1) * 128, :],
                    in_=sb["kout"][:]), reads=["kout"], dma="ds")
            for h in range(H):
                kq_mm(h)
                if h >= 1:
                    rope_tail(h - 1)
                yield
            rope_tail(H - 1)
        for ot4 in range(4):
            for grp in range(4):
                wt = "wkt" if grp % 2 == 0 else "wku"
                kb.op("y", lambda y, grp=grp, ot4=ot4, wt=wt: y.dma_start(
                    out=sb[wt][:].rearrange("p (a j) -> p a j", j=512),
                    in_=bass.AP(dram["wvT"].tensor, grp * 4 * 128 * 2048 + ot4 * 512,
                                [[2048, 128], [128 * 2048, 4], [1, 512]])),
                    writes=[wt], dma="dl")
                for ci in range(4):
                    c = grp * 4 + ci
                    for ts4 in range(4):
                        pn = f"p{3 + ts4}"
                        kb.op("t", lambda t, c=c, ci=ci, ts4=ts4, pn=pn, wt=wt: t.matmul(
                              sb[pn][:, :512],
                              sb["big0"][:, c * 512 + ts4 * 128:c * 512 + ts4 * 128 + 128],
                              sb[wt][:, ci * 512:(ci + 1) * 512],
                              start=(c == 0), stop=(c == NC_D - 1)),
                              reads=[wt, xnb[c]], writes=[pn])
            for ts4 in range(4):
                pn = f"p{3 + ts4}"
                kb.op("s", lambda s, pn=pn: s.copy(out=sb["kout"][:], in_=sb[pn][:, :512]),
                      reads=[pn], writes=["kout"])
                kb.op("g", lambda g, tkt=tkt, ts4=ts4, ot4=ot4: g.dma_start(
                    out=bass.AP(dram["VT"].tensor,
                                ((tkt * 4 + ts4) * H + ot4 * 4) * 16384,
                                [[128, 128], [16384, 4], [1, 128]]),
                    in_=sb["kout"][:].rearrange("p (a j) -> p a j", j=128)),
                    reads=["kout"], dma="ds")
                yield


def emit_phaseC(kb, sb, dram):
    for h in range(H):
        for blk in range(NKT):
            kb.op("y", lambda y, h=h, blk=blk: y.dma_start(
                out=sb["big0"][:, blk * 512:(blk + 1) * 512],
                in_=dram["KF"][(h * NKT + blk) * 128:(h * NKT + blk + 1) * 128, :]),
                writes=[f"ksb{blk}"], dma="dl")
            kb.op("y", lambda y, h=h, blk=blk: y.dma_start(
                out=sb["big0"][:, 4096 + blk * 512:4096 + (blk + 1) * 512].rearrange(
                    "p (a j) -> p a j", j=128),
                in_=bass.AP(dram["VT"].tensor, (blk * 4 * H + h) * 16384,
                            [[128, 128], [H * 16384, 4], [1, 128]]),
                ), writes=[f"vsb{blk}"], dma="dl")
        for qt in range(NQT):
            kb.op("y", lambda y, h=h, qt=qt: y.dma_start(out=sb["qsb"][:],
                  in_=dram["QF"][(h * NQT + qt) * 128:(h * NQT + qt + 1) * 128, :]),
                  writes=["qsb"], dma="dl")
            nk = (T_KEYS - R_OWN) // 128 + 4 * (qt + 1)
            def s_stage(j):
                ps = "p0" if j % 2 == 0 else "p1"
                kb.op("t", lambda t, j=j, ps=ps: t.matmul(sb[ps][:, :512],
                      sb["big0"][:, j * 128:(j + 1) * 128], sb["qsb"][:],
                      start=True, stop=True),
                      reads=[f"ksb{j // 4}", "qsb"], writes=[ps])
            def e_stage(j):
                ps = "p0" if j % 2 == 0 else "p1"
                eb = "esb" if j % 2 == 0 else "esc"
                diag = j >= nk - 4
                if diag:
                    bias_ap, bias_buf = sb["cvec"][:, 0:1], "cvec"
                else:
                    bias_ap, bias_buf = sb["phb"][:, j:j + 1], "phb"
                kb.op("s", lambda s, ps=ps, eb=eb, bias_ap=bias_ap: s.activation(
                      out=sb[eb][:], in_=sb[ps][:, :512], func=ACT.Exp,
                      bias=bias_ap, scale=sb["scv"][:]),
                      reads=[ps, bias_buf, "scv"], writes=[eb])
                if diag:
                    off = j - (nk - 4)
                    kb.op("v", lambda v, off=off, eb=eb: v.tensor_tensor(out=sb[eb][:],
                          in0=sb[eb][:],
                          in1=sb["trimask"][:, off * 512:(off + 1) * 512], op=OP.mult),
                          reads=[eb, "trimask"], writes=[eb])
            def pv_stage(j):
                eb = "esb" if j % 2 == 0 else "esc"
                kb.op("t", lambda t, j=j, eb=eb: t.matmul(sb["p2"][:, :512],
                      sb["big0"][:, 4096 + j * 128:4096 + (j + 1) * 128], sb[eb][:],
                      start=(j == 0), stop=(j == nk - 1)),
                      reads=[f"vsb{j // 4}", eb], writes=["p2"])
                kb.op("t", lambda t, j=j, eb=eb: t.matmul(sb["p3"][0:1, :512],
                      sb["ones"][:, 0:1], sb[eb][:],
                      start=(j == 0), stop=(j == nk - 1)),
                      reads=["ones", eb], writes=["p3"])
            for j in range(nk):
                s_stage(j)
                e_stage(j)
                if j >= 1:
                    pv_stage(j - 1)
            pv_stage(nk - 1)
            for half in range(2):
                kb.op("v", lambda v, half=half: v.reciprocal(
                      out=sb["rsum"][0:1, half * 256:(half + 1) * 256],
                      in_=sb["p3"][0:1, half * 256:(half + 1) * 256]),
                      reads=["p3"], writes=["rsum"])
            kb.op("t", lambda t: t.matmul(sb["p0"][:, :512], sb["ones"][0:1, :],
                  sb["rsum"][0:1, :], start=True, stop=True),
                  reads=["ones", "rsum"], writes=["p0"])
            kb.op("s", lambda s: s.copy(out=sb["esb"][:], in_=sb["p0"][:, :512]),
                  reads=["p0"], writes=["esb"])
            kb.op("v", lambda v: v.tensor_tensor(out=sb["osb"][:], in0=sb["p2"][:, :512],
                  in1=sb["esb"][:], op=OP.mult),
                  reads=["p2", "esb"], writes=["osb"])
            kb.op("g", lambda g, h=h, qt=qt: g.dma_start(
                out=dram["OF"][(h * NQT + qt) * 128:(h * NQT + qt + 1) * 128, :],
                in_=sb["osb"][:]), reads=["osb"], dma="ds")


def _f_gu(kb, sb, dram, pr, qxb, qxb2):
    ot0, ot1 = 2 * pr, 2 * pr + 1
    kb.op("y", lambda y, ot0=ot0: y.dma_start(
        out=sb["qxf"][:].rearrange("p (c j) -> p c j", j=128),
        in_=bass.AP(dram["QXF"].tensor, ot0 * NC_D * 16384,
                    [[128, 128], [16384, NC_D], [1, 128]])),
        writes=qxb, dma="dl")
    kb.op("y", lambda y, ot1=ot1: y.dma_start(
        out=sb["qxf2"][:].rearrange("p (c j) -> p c j", j=128),
        in_=bass.AP(dram["QXF"].tensor, ot1 * NC_D * 16384,
                    [[128, 128], [16384, NC_D], [1, 128]])),
        writes=qxb2, dma="dl")
    for jt in range(NJT):
        kb.op("y", lambda y, jt=jt: y.dma_start(out=sb["wsgS"][:],
              in_=dram["WSG"][:, jt * 512:(jt + 1) * 512]), writes=["wsgS"], dma="dl")
        kb.op("y", lambda y, jt=jt: y.dma_start(out=sb["wsuS"][:],
              in_=dram["WSU"][:, jt * 512:(jt + 1) * 512]), writes=["wsuS"], dma="dl")
        kb.op("y", lambda y, jt=jt: y.dma_start(
            out=sb["qwgL"][:].rearrange("p (c j) -> p c j", j=512),
            in_=bass.AP(dram["QWG"].tensor, jt * 128 * 512,
                        [[512, 128], [NJT * 128 * 512, NC_D], [1, 512]])),
            writes=["qwgL"], dma="dl")
        kb.op("s", lambda s, jt=jt: s.dma_start(
            out=sb["qwuL"][:].rearrange("p (c j) -> p c j", j=512),
            in_=bass.AP(dram["QWU"].tensor, jt * 128 * 512,
                        [[512, 128], [NJT * 128 * 512, NC_D], [1, 512]])),
            writes=["qwuL"], dma="dl")
        for ti, (qx, qb, ot) in enumerate((("qxf", qxb, ot0), ("qxf2", qxb2, ot1))):
            pg = "p4" if ti == 0 else "p6"
            pu = "p5" if ti == 0 else "p7"
            for c in range(NC_D):
                kb.op("t", lambda t, c=c, pg=pg, qx=qx: t.matmul(sb[pg][:, :512],
                      sb[qx][:, c * 128:(c + 1) * 128],
                      sb["qwgL"][:, c * 512:(c + 1) * 512],
                      start=(c == 0), stop=(c == NC_D - 1)),
                      reads=[qb[c], "qwgL"], writes=[pg])
                kb.op("t", lambda t, c=c, pu=pu, qx=qx: t.matmul(sb[pu][:, :512],
                      sb[qx][:, c * 128:(c + 1) * 128],
                      sb["qwuL"][:, c * 512:(c + 1) * 512],
                      start=(c == 0), stop=(c == NC_D - 1)),
                      reads=[qb[c], "qwuL"], writes=[pu])
            kb.op("v", lambda v, ot=ot, pg=pg: v.scalar_tensor_tensor(out=sb["gsb"][:],
                  in0=sb[pg][:, :512], scalar=sb["sxdT"][:, ot:ot + 1],
                  in1=sb["wsgS"][:], op0=OP.mult, op1=OP.mult),
                  reads=[pg, "sxdT", "wsgS"], writes=["gsb"])
            kb.op("s", lambda s: s.activation(out=sb["kpre"][:], in_=sb["gsb"][:],
                  func=ACT.Silu), reads=["gsb"], writes=["kc2"])
            kb.op("v", lambda v, ot=ot, pu=pu: v.scalar_tensor_tensor(out=sb["usb"][:],
                  in0=sb[pu][:, :512], scalar=sb["sxdT"][:, ot:ot + 1],
                  in1=sb["wsuS"][:], op0=OP.mult, op1=OP.mult),
                  reads=[pu, "sxdT", "wsuS"], writes=["usb"])
            tb = "tb0" if ti == 0 else "tb1"
            kb.op("v", lambda v, jt=jt, tb=tb: v.tensor_tensor(
                  out=sb[tb][:, jt * 512:(jt + 1) * 512], in0=sb["kpre"][:],
                  in1=sb["usb"][:], op=OP.mult),
                  reads=["kc2", "usb"], writes=[tb])
        yield


def _f_snap(kb, sb, dram, ti, qtb_):
    tb = "tb0" if ti == 0 else "tb1"
    TB = sb[tb]
    S1 = sb["big0"][:, 0:2048]
    S2 = sb["big0"][:, 2048:4096]
    kb.op("v", lambda v, tb=tb: v.tensor_reduce(out=sb["stq"][:], in_=TB[:],
          axis=AX.X, op=OP.max, apply_absolute_value=True),
          reads=[tb], writes=["stq"])
    kb.op("v", lambda v: v.reciprocal(out=sb["stq"][:], in_=sb["stq"][:]),
          reads=["stq"], writes=["stq"])
    kb.op("v", lambda v: v.tensor_scalar(out=sb["stq"][:], in0=sb["stq"][:],
          scalar1=448.0, scalar2=None, op0=OP.mult),
          reads=["stq"], writes=["stq"])
    kb.op("v", lambda v: v.reciprocal(out=sb["stdq"][:], in_=sb["stq"][:]),
          reads=["stq"], writes=["stdq"])
    for sec in range(3):
        w0 = sec * 2048
        wd = min(2048, I_FF - w0)
        kb.op("v", lambda v, w0=w0, wd=wd, tb=tb: v.tensor_scalar(out=S1[:, :wd],
              in0=TB[:, w0:w0 + wd], scalar1=sb["stq"][:], scalar2=None, op0=OP.mult),
              reads=[tb, "stq"], writes=["fsn1"])
        snap_chunks(kb, sb, S1[:, :wd], "fsn1", S2[:, :wd], "fsn2", wd)
        for cl in range(wd // 128):
            c = sec * 16 + cl
            pn = "p6" if cl % 2 == 0 else "p7"
            kb.op("t", lambda t, cl=cl, pn=pn: t.transpose(sb[pn][:, :128],
                  S2[:, cl * 128:(cl + 1) * 128], sb["ident"][:]),
                  reads=["fsn2", "ident"], writes=[pn])
            kb.op("s", lambda s, c=c, pn=pn: s.copy(
                  out=sb["qtf"][:, c * 128:(c + 1) * 128], in_=sb[pn][:, :128]),
                  reads=[pn], writes=[qtb_[c]])
        yield


def _f_down(kb, sb, dram, ot, qtb_):
    for nt in range(4):
        kb.op("y", lambda y, nt=nt: y.dma_start(out=sb["wsdS"][:],
              in_=dram["WSD"][:, nt * 512:(nt + 1) * 512]), writes=["wsdS"], dma="dl")
        pd = "p2" if nt % 2 == 0 else "p3"
        for c0 in range(0, NC_I, 11):
            qe = "y" if (c0 // 11) % 2 == 0 else "s"
            dln = "qwgL" if (c0 // 11) % 2 == 0 else "qwuL"
            kb.op(qe, lambda e, c0=c0, nt=nt, dln=dln: e.dma_start(
                out=sb[dln][:, 0:5632].rearrange("p (c j) -> p c j", j=512),
                in_=bass.AP(dram["QWD"].tensor, (c0 * 4 + nt) * 128 * 512,
                            [[512, 128], [4 * 128 * 512, 11], [1, 512]])),
                writes=[dln], dma="dl")
            for c in range(c0, c0 + 11):
                kb.op("t", lambda t, c=c, c0=c0, pd=pd, dln=dln: t.matmul(
                      sb[pd][:, :512], sb["qtf"][:, c * 128:(c + 1) * 128],
                      sb[dln][:, (c - c0) * 512:(c - c0 + 1) * 512],
                      start=(c == 0), stop=(c == NC_I - 1)),
                      reads=[qtb_[c], dln], writes=[pd])
        kb.op("v", lambda v, nt=nt, pd=pd: v.scalar_tensor_tensor(out=sb["ysb"][:],
              in0=sb[pd][:, :512], scalar=sb["stdq2"][:],
              in1=sb["wsdS"][:], op0=OP.mult, op1=OP.mult),
              reads=[pd, "stdq2", "wsdS"], writes=["ysb"])
        kb.op("g", lambda g, ot=ot, nt=nt: g.dma_start(
            out=dram["FFNY"][(ot * 4 + nt) * 128:(ot * 4 + nt + 1) * 128, :],
            in_=sb["ysb"][:]), reads=["ysb"], dma="ds")
        yield


def emit_phaseF(kb, sb, dram):
    qxb = [f"qx{c}" for c in range(NC_D)]
    qxb2 = [f"qy{c}" for c in range(NC_D)]
    qtb_ = [f"qt{c}" for c in range(NC_I)]
    for pr in range(NT_OWN // 2):
        for _ in _f_gu(kb, sb, dram, pr, qxb, qxb2):
            pass
        for ti in range(2):
            for _ in _f_snap(kb, sb, dram, ti, qtb_):
                pass
            kb.op("v", lambda v: v.tensor_copy(out=sb["stdq2"][:], in_=sb["stdq"][:]),
                  reads=["stdq"], writes=["stdq2"])
            for _ in _f_down(kb, sb, dram, 2 * pr + ti, qtb_):
                pass


def emit_phaseD(kb, sb, dram):
    ofb = [f"of{h}" for h in range(H)]
    for qt in range(NQT):
        for h in range(H):
            kb.op("y", lambda y, h=h, qt=qt: y.dma_start(
                out=sb["big0"][:, h * 512:(h + 1) * 512],
                in_=dram["OF"][(h * NQT + qt) * 128:(h * NQT + qt + 1) * 128, :]),
                writes=[ofb[h]], dma="dl")
        for nt in range(4):
            for h in range(H):
                wb = "wv0" if h % 2 == 0 else "wv1"
                kb.op("y", lambda y, h=h, nt=nt, wb=wb: y.dma_start(
                    out=sb[wb][:], in_=dram["woT"][h * 128:(h + 1) * 128,
                                                   nt * 512:(nt + 1) * 512]),
                    writes=[wb], dma="dl")
                for ts4 in range(4):
                    pn = f"p{2 + ts4}"
                    kb.op("t", lambda t, h=h, ts4=ts4, pn=pn, wb=wb: t.matmul(
                          sb[pn][:, :512],
                          sb["big0"][:, h * 512 + ts4 * 128:h * 512 + ts4 * 128 + 128],
                          sb[wb][:], start=(h == 0), stop=(h == H - 1)),
                          reads=[ofb[h], wb], writes=[pn])
            for ts4 in range(4):
                pn = f"p{2 + ts4}"
                xb = "gsb" if ts4 % 2 == 0 else "usb"
                fb = "kc" if ts4 % 2 == 0 else "rs"
                trow = OWN0 + qt * 512 + ts4 * 128
                kb.op("y", lambda y, trow=trow, nt=nt, xb=xb: y.dma_start(
                    out=sb[xb][:], in_=dram["xk"][trow:trow + 128,
                                                  nt * 512:(nt + 1) * 512]),
                    writes=[xb], dma="dl")
                kb.op("y", lambda y, qt=qt, ts4=ts4, nt=nt, fb=fb: y.dma_start(
                    out=sb[fb][:],
                    in_=dram["FFNY"][((qt * 4 + ts4) * 4 + nt) * 128:
                                     ((qt * 4 + ts4) * 4 + nt + 1) * 128, :]),
                    writes=[fb], dma="dl")
                kb.op("v", lambda v, pn=pn, xb=xb: v.tensor_tensor(
                      out=sb["osb"][:], in0=sb[pn][:, :512], in1=sb[xb][:], op=OP.add),
                      reads=[pn, xb], writes=["osb"])
                kb.op("v", lambda v, fb=fb: v.tensor_tensor(out=sb["osb"][:],
                      in0=sb["osb"][:], in1=sb[fb][:], op=OP.add),
                      reads=["osb", fb], writes=["osb"])
                kb.op("g", lambda g, qt=qt, ts4=ts4, nt=nt: g.dma_start(
                    out=dram["y"][qt * 512 + ts4 * 128:qt * 512 + ts4 * 128 + 128,
                                  nt * 512:(nt + 1) * 512],
                    in_=sb["osb"][:]), reads=["osb"], dma="ds")


def _run(gen):
    for _ in gen:
        pass


def _interleave(gens, ratios):
    gens = [iter(g) for g in gens]
    done = [False] * len(gens)
    while not all(done):
        for i, g in enumerate(gens):
            if done[i]:
                continue
            for _ in range(ratios[i]):
                try:
                    next(g)
                except StopIteration:
                    done[i] = True
                    break


def build_program(debug_out=(), phases="AWBCFD"):
    import contextlib
    nc = bass.Bass("TRN2", target_bir_lowering=False, debug=False)
    dram = declare_dram(nc, debug_out)
    es = contextlib.ExitStack()
    sb = alloc_all(nc, es)
    with nc.Block():
        kb = KB(nc)
        emit_consts(kb, sb, dram)
        if "A" in phases and "W" in phases:
            _interleave([emit_phaseA(kb, sb, dram), emit_phaseW_nat(kb, sb, dram)],
                        [1, 3])
            kb.barrier()
            emit_phaseW_mid(kb, sb, dram)
            kb.barrier()
        elif "A" in phases:
            _run(emit_phaseA(kb, sb, dram))
            kb.barrier()
        elif "W" in phases:
            _run(emit_phaseW_nat(kb, sb, dram))
            kb.barrier()
            emit_phaseW_mid(kb, sb, dram)
            kb.barrier()
        if "B" in phases and "W" in phases:
            _interleave([emit_phaseB(kb, sb, dram), emit_phaseW_quant(kb, sb, dram)],
                        [1, 2])
            kb.barrier()
        elif "B" in phases:
            _run(emit_phaseB(kb, sb, dram))
            kb.barrier()
        elif "W" in phases:
            _run(emit_phaseW_quant(kb, sb, dram))
            kb.barrier()
        if "F" in phases:
            emit_phaseF(kb, sb, dram)
            kb.barrier()
        if "C" in phases:
            emit_phaseC(kb, sb, dram)
            kb.barrier()
        if "D" in phases:
            emit_phaseD(kb, sb, dram)
        for k in range(kb.NDMA):
            nm = f"ds{k}"
            if kb.cnt[nm]:
                kb.eng["g"].wait_ge(kb.sem[nm], kb.cnt[nm])
        n_instr = sum(kb.cnt[e] for e in ("t", "v", "s", "g", "y"))
        kb.close_sems()
    es.close()
    return nc, n_instr


"""Host-side input preparation (pure layout + shape-only constants)."""


def rope_tables():
    import jax
    import jax.numpy as jnp
    cpu = jax.devices('cpu')[0]
    with jax.default_device(cpu):
        half, T = 64, 4096
        inv_freq = 1.0 / 10000.0 ** (jnp.arange(half, dtype=jnp.float32) / half)
        t = jnp.arange(T, dtype=jnp.float32)
        freqs = jnp.einsum('t,f->tf', t, inv_freq)
        cos = np.asarray(jnp.cos(freqs))
        sin = np.asarray(jnp.sin(freqs))
    cosfm = np.repeat(cos, 2, axis=1).T.copy()
    sinfm = np.repeat(sin, 2, axis=1).T.copy()
    return cosfm.astype(np.float32), sinfm.astype(np.float32)


EBIAS = -20.0


def make_consts():
    ones = np.ones((128, 128), np.float32)
    ident = np.eye(128, dtype=np.float32)
    R = np.zeros((128, 128), np.float32)
    for i in range(64):
        R[2 * i, 2 * i + 1] = -1.0
        R[2 * i + 1, 2 * i] = 1.0
    ropeRT = R.T.copy()
    import ml_dtypes
    tri = np.zeros((128, 2048), np.float32)
    for o in range(4):
        tk = np.arange(128)[:, None]
        tq = np.arange(512)[None, :]
        tri[:, o * 512:(o + 1) * 512] = (tq >= tk + 128 * o).astype(np.float32)
    cvec = np.zeros((128, 8), np.float32)
    c = 1.0 / math.sqrt(128.0)
    a = np.float32(c)
    b = np.float32(c - np.float64(a))
    cvec[:, 0] = EBIAS
    cvec[:, 1] = a
    cvec[:, 2] = b
    cvec[:, 4] = 1e-5
    cvec[:, 5] = 1.0
    return ones, ident, ropeRT, tri.astype(ml_dtypes.bfloat16), cvec


def prep_inputs(ins):
    x = np.ascontiguousarray(ins["L0_attn_x"], dtype=np.float32)
    down_idx = np.asarray(ins["L0_ffn_down_reorder_index"])
    wq = np.ascontiguousarray(ins["L0_attn_Wq"].T)
    wk = np.ascontiguousarray(ins["L0_attn_Wk"].T)
    wv = np.ascontiguousarray(ins["L0_attn_Wv"].T)
    wo = np.ascontiguousarray(ins["L0_attn_Wo"].T)
    wgN = np.ascontiguousarray(ins["L0_ffn_w_gate"][down_idx])
    wgT = np.ascontiguousarray(wgN.T)
    wuN = np.ascontiguousarray(ins["L0_ffn_w_up"][down_idx])
    wuT = np.ascontiguousarray(wuN.T)
    wdN = np.ascontiguousarray(ins["L0_ffn_w_down"])
    wdT = np.ascontiguousarray(wdN.T)
    cosfm, sinfm = rope_tables()
    ones, ident, ropeRT, tri, cvec = make_consts()
    cosA = np.concatenate([cosfm[:, :2048], cosfm[:, :2048]], axis=1).copy()
    sinA = np.concatenate([sinfm[:, :2048], sinfm[:, :2048]], axis=1).copy()
    phA = np.full((128, 32), EBIAS, np.float32); phA[:, :16] = -1e30
    phB = np.full((128, 32), EBIAS, np.float32)
    maps = []
    for core in range(8):
        b, half = core // 2, core % 2
        if half == 0:
            xk = np.zeros((4096, 2048), np.float32)
            xk[2048:] = x[b, :2048]
            cosc, sinc, phc = cosA, sinA, phA
        else:
            xk = np.ascontiguousarray(x[b])
            cosc, sinc, phc = cosfm, sinfm, phB
        maps.append(dict(
            xk=xk, wqT=wq, wkT=wk, wvT=wv, woT=wo,
            wgN=wgN, wgT=wgT, wuN=wuN, wuT=wuT, wdN=wdN, wdT=wdT,
            cosf=cosc, sinf=sinc,
            conA=ones, conI=ident, conR=ropeRT, conT=tri, conP=phc, conC=cvec,
        ))
    return maps


def gather_out(results):
    out = np.empty((4, 4096, 2048), np.float32)
    for core in range(8):
        b, half = core // 2, core % 2
        out[b, half * 2048:(half + 1) * 2048] = results[core]["y"]
    return out


_CACHED = {}

def _get_program():
    if "nc" not in _CACHED:
        _CACHED["nc"] = build_program(debug_out=(), phases="AWBCFD")[0]
    return _CACHED["nc"]


def kernel(**inputs):
    from concourse.bass_utils import run_bass_kernel_spmd
    maps = prep_inputs(inputs)
    nc = _get_program()
    res = run_bass_kernel_spmd(nc, maps, core_ids=list(range(8)))
    return gather_out(res.results)
